# version 13
# baseline (speedup 1.0000x reference)
import numpy as np
from contextlib import ExitStack

import concourse.bass as bass
import concourse.mybir as mybir
from concourse.bass_utils import run_bass_kernel_spmd

F32 = mybir.dt.float32
BF16 = mybir.dt.bfloat16
AF = mybir.ActivationFunctionType
ALU = mybir.AluOpType
AX = mybir.AxisListType
MAGIC = 12582912.0

B, L, D = 8, 4096, 1024
H, HD = 16, 64
NT = L // 128
DFF0 = 4096
DFF1 = 2816
CS = 64
NC = L // CS
LN_EPS0 = 1e-5
TTT_LN_EPS = 1e-6
DENOM_EPS = 1e-6
ROPE_THETA = 10000.0
GELU_C = float(np.sqrt(2.0 / np.pi))
GELU_A = 0.044715
SQRT_HALF = float(np.sqrt(0.5))

DBG = {}


class PB:
    CENG = ("t", "v", "s", "p")
    QUEUES = ("y", "s", "p")

    NDSEM = 10

    def __init__(self, nc, ctx):
        self.nc = nc
        self.ctx = ctx
        self.sem = {e: ctx.enter_context(nc.semaphore(f"es_{e}")) for e in self.CENG}
        self.dpool = {}
        self.dsem_by_key = {}
        for q in self.QUEUES:
            pool = []
            for j in range(self.NDSEM):
                key = f"ds_{q}_{j}"
                h = ctx.enter_context(nc.semaphore(key))
                pool.append({"key": key, "h": h, "val": 0})
                self.dsem_by_key[key] = h
            self.dpool[q] = pool
        self.nops = {e: 0 for e in self.CENG}
        self.last_comp = {e: 0 for e in self.CENG}
        self.ops = {e: [] for e in list(self.CENG) + ["y"]}
        self.miles = {e: [] for e in self.CENG}
        self.semval = {e: 0 for e in self.CENG}
        self.wr = {}
        self.rd = {}
        self.watermark = {e: {} for e in list(self.CENG) + []}
        self.bank_last = {}
        self.relax_same_engine = False
        self.late_ops = {e: set() for e in self.CENG}
        self.barrier_tokens = []

    LATE_OPS = {"reduce_sum", "reduce_max", "tensor_reduce", "iota", "affine_select", "memset",
                "reciprocal", "bn_stats", "bn_aggr"}

    def _wait_for_token(self, eng, tok):
        kind = tok[0]
        if kind == "d":
            _, key, v = tok
            if self.watermark[eng].get(key, 0) >= v:
                return None
            self.watermark[eng][key] = v
            return (self.dsem_by_key[key], v)
        else:
            _, peng, opidx = tok
            if peng == eng and self.relax_same_engine and opidx not in self.late_ops[peng]:
                return None
            if peng == eng and opidx > self.nops[eng]:
                raise RuntimeError("self-wait on future op")
            v = self._milestone(peng, opidx)
            key = "e" + peng
            if self.watermark[eng].get(key, 0) >= v:
                return None
            self.watermark[eng][key] = v
            return (self.sem[peng], v)

    def _milestone(self, eng, opidx):
        for idx, val in self.miles[eng]:
            if idx >= opidx:
                return val
        self.semval[eng] += 1
        val = self.semval[eng]
        self.ops[eng][opidx - 1][2].append((self.sem[eng], 1))
        self.miles[eng].append((opidx, val))
        return val

    def _record(self, eng_kind, token, reads, writes):
        for r in reads:
            self.rd.setdefault(r, {})[eng_kind] = token
        for w in writes:
            self.wr[w] = token
            self.rd[w] = {}

    def _gather_waits(self, eng, reads, writes, banks):
        waits = []
        seen = set()

        def add(tok):
            if tok is None:
                return
            w = self._wait_for_token(eng, tok)
            if w is not None and (id(w[0]), w[1]) not in seen:
                seen.add((id(w[0]), w[1]))
                waits.append(w)

        for tok in self.barrier_tokens:
            add(tok)
        for r in reads:
            add(self.wr.get(r))
        for w in writes:
            add(self.wr.get(w))
            for tok in self.rd.get(w, {}).values():
                add(tok)
        for b in banks:
            last = self.bank_last.get(b)
            if last is not None:
                last_eng, tok = last
                if last_eng != eng:
                    add(tok)
        return waits

    @staticmethod
    def _thunk(fn):
        if callable(fn):
            return fn
        name, kw = fn
        return lambda e: getattr(e, name)(**kw)

    def op(self, eng, fn, reads=(), writes=(), banks=()):
        assert eng in self.CENG
        is_late = (not callable(fn)) and (fn[0] in self.LATE_OPS or fn[1].get("accum_out") is not None)
        fn = self._thunk(fn)
        waits = self._gather_waits(eng, reads, writes, banks)
        self.ops[eng].append([fn, waits, []])
        self.nops[eng] += 1
        if is_late:
            self.late_ops[eng].add(self.nops[eng])
        tok = ("e", eng, self.nops[eng])
        self._record(("e", eng), tok, reads, writes)
        self.last_comp[eng] = self.nops[eng]
        for b in banks:
            self.bank_last[b] = (eng, tok)
        return tok

    def dma(self, q, fn, reads=(), writes=()):
        iss = {"y": "y", "s": "s", "p": "p"}[q]
        fn = self._thunk(fn)
        if iss not in self.watermark:
            self.watermark[iss] = {}
        waits = []
        seen = set()

        def addw(w):
            if w is not None and (id(w[0]), w[1]) not in seen:
                seen.add((id(w[0]), w[1]))
                waits.append(w)

        def add(tok):
            if tok is None:
                return
            addw(self._wait_for_token(iss, tok))

        for tok in self.barrier_tokens:
            add(tok)
        for r in reads:
            add(self.wr.get(r))
        for w_ in writes:
            add(self.wr.get(w_))
            for tok in self.rd.get(w_, {}).values():
                add(tok)
        pool = self.dpool[q]
        pick = None
        for ent in pool:
            if self.watermark[iss].get(ent["key"], 0) >= ent["val"]:
                pick = ent
                break
        if pick is None:
            pick = pool[0]
            addw(self._wait_for_token(iss, ("d", pick["key"], pick["val"])))
        pool.remove(pick)
        pool.append(pick)
        pick["val"] += 16
        tok = ("d", pick["key"], pick["val"])
        self.ops[iss].append([fn, waits, [(pick["h"], 16)]])
        if iss != "y":
            self.nops[iss] += 1
        self._record(("d", pick["key"]), tok, reads, writes)
        return tok

    def mem_barrier(self):
        toks = []
        for e in self.CENG:
            if self.last_comp[e] > 0:
                toks.append(("e", e, self.last_comp[e]))
        for q in self.QUEUES:
            for ent in self.dpool[q]:
                if ent["val"] > 0:
                    toks.append(("d", ent["key"], ent["val"]))
        self.barrier_tokens = toks

    def barrier_token(self, eng):
        if self.nops[eng] == 0:
            return None
        return ("e", eng, self.nops[eng])

    def finalize(self):
        for q in self.QUEUES:
            for ent in self.dpool[q]:
                if ent["val"] > 0:
                    self.ops["y"].append([None, [(ent["h"], ent["val"])], []])

    def emit(self):
        self.finalize()
        nc = self.nc
        engmap = {"t": nc.tensor, "v": nc.vector, "s": nc.scalar, "p": nc.gpsimd, "y": nc.sync}
        with nc.Block() as block:
            def make_section(ename):
                def section(eng):
                    for fn, waits, incs in self.ops[ename]:
                        if fn is None:
                            for smh, v in waits:
                                eng.wait_ge(smh, v)
                            continue
                        for smh, v in waits:
                            eng.wait_ge(smh, v)
                        ins = fn(eng)
                        for smh, n in incs:
                            ins.then_inc(smh, n)
                return section

            block.tensor(make_section("t"))
            block.vector(make_section("v"))
            block.scalar(make_section("s"))
            block.gpsimd(make_section("p"))
            block.sync(make_section("y"))


class Common:

    def __init__(self, pb, ctx):
        nc = pb.nc
        self.pb = pb
        self.consts = ctx.enter_context(nc.sbuf_tensor("cc_consts", [128, 8], F32))
        self.ones_row_b = ctx.enter_context(nc.sbuf_tensor("cc_ones_row", [1, 128], BF16))
        self.ident_b = ctx.enter_context(nc.sbuf_tensor("cc_ident_b", [128, 128], BF16))
        self.ones128_b = ctx.enter_context(nc.sbuf_tensor("cc_ones128", [128, 1], BF16))
        self._cmap = {}
        self._cfree = list(range(8))
        p = pb
        p.op("v", lambda e: e.memset(self.ones_row_b[:], 1.0), writes=["cc_ones_row"])
        p.op("v", lambda e: e.memset(self.ones128_b[:], 1.0), writes=["cc_ones128"])
        p.op("v", lambda e: e.memset(self.ident_b[:], 1.0), writes=["cc_ident_tmp"])
        p.op("p", lambda e: e.affine_select(
            out=self.ident_b[:], in_=self.ident_b[:], pattern=[[1, 128]],
            compare_op=ALU.is_equal, fill=0.0, base=0, channel_multiplier=-1),
            reads=["cc_ident_tmp"], writes=["cc_ident"])

    def const(self, val):
        val = float(val)
        if val not in self._cmap:
            idx = self._cfree.pop(0)
            self._cmap[val] = idx
            self.pb.op("v", lambda e, i=idx, v=val: e.memset(self.consts[:, i:i + 1], v),
                       writes=[f"cc_const_{idx}"])
        idx = self._cmap[val]
        return self.consts[:, idx:idx + 1]

    def const_region(self, val):
        return f"cc_const_{self._cmap[float(val)]}"


def bank_regions(*idxs):
    return [f"bk{i}" for i in idxs]




def build_l0a():
    nc = bass.Bass("TRN2", target_bir_lowering=False)
    x = nc.dram_tensor("x", [L, D], F32, kind="ExternalInput").ap()
    wqkv = nc.dram_tensor("wqkv", [D, 3 * D], F32, kind="ExternalInput").ap()
    gamma = nc.dram_tensor("gamma", [1, D], F32, kind="ExternalInput").ap()
    beta = nc.dram_tensor("beta", [1, D], F32, kind="ExternalInput").ap()
    xn_h = nc.dram_tensor("xn_h", [L, D], F32, kind="ExternalOutput").ap()
    q_h = nc.dram_tensor("q_h", [L, D], mybir.dt.uint16, kind="ExternalOutput").ap().bitcast(BF16)
    kv_h = nc.dram_tensor("kv_h", [64, 1024], F32, kind="ExternalOutput").ap()
    ksum_h = nc.dram_tensor("ksum_h", [1, 1024], F32, kind="ExternalOutput").ap()
    stats_h = nc.dram_tensor("stats_h", [NT, 128, 2], F32, kind="ExternalOutput").ap()
    amax_h = nc.dram_tensor("amax_h", [1, 1], F32, kind="ExternalOutput").ap()

    ctx = ExitStack()
    with ctx:
        pb = PB(nc, ctx)
        cm = Common(pb, ctx)
        emit_l0a(pb, cm, ctx, dict(x=x, wqkv=wqkv, gamma=gamma, beta=beta, xn_h=xn_h,
                                   q_h=q_h, kv_h=kv_h, ksum_h=ksum_h, stats_h=stats_h,
                                   amax_h=amax_h))
        pb.emit()
    return nc


def emit_l0a(pb, cm, ctx, io):
    nc = pb.nc
    sb = lambda name, shape, dt: ctx.enter_context(nc.sbuf_tensor(name, shape, dt))
    ps = lambda name, shape, dt: ctx.enter_context(nc.psum_tensor("pa" + name, shape, dt))
    cm.const(LN_EPS0)

    wq_f = sb("l0a_wq_f", [128, 3 * D], F32)
    wq_b = [sb(f"l0a_wqb{k}", [128, 3 * D], BF16) for k in range(8)]
    for k in range(8):
        pb.dma("y", ("dma_start", dict(out=wq_f[:], in_=io["wqkv"][k * 128:(k + 1) * 128, :])),
               reads=["h_wqkv"], writes=["l0a_wq_f"])
        pb.op("v", ("tensor_copy", dict(out=wq_b[k][:], in_=wq_f[:])),
              reads=["l0a_wq_f"], writes=[f"l0a_wqb{k}"])

    gb_row = sb("l0a_gb_row", [1, 2 * D], F32)
    pb.dma("y", ("dma_start", dict(out=gb_row[:, 0:D], in_=io["gamma"])), reads=["h_gamma"], writes=["l0a_gb_row"])
    pb.dma("y", ("dma_start", dict(out=gb_row[:, D:2 * D], in_=io["beta"])), reads=["h_beta"], writes=["l0a_gb_row"])
    gamma_b = sb("l0a_gamma_b", [128, D], F32)
    beta_b = sb("l0a_beta_b", [128, D], F32)
    ones_row_f = sb("l0a_ones_row_f", [1, 128], F32)
    pb.op("v", ("memset", dict(ap=ones_row_f[:], constant=1.0)), writes=["l0a_ones_row_f"])
    bank = [ps(f"bk{i}", [128, 512], F32) for i in range(8)]
    bank_bf = [b[:].bitcast(BF16) for b in bank]
    for dst, rdst, off in ((gamma_b, "l0a_gamma_b", 0), (beta_b, "l0a_beta_b", D)):
        for j in range(2):
            pb.op("t", ("matmul", dict(out=bank[7][:, 0:512], lhsT=ones_row_f[:],
                                       rhs=gb_row[:, off + j * 512:off + (j + 1) * 512], start=True, stop=True)),
                  reads=["l0a_ones_row_f", "l0a_gb_row"], banks=bank_regions(7))
            pb.op("v", ("tensor_copy", dict(out=dst[:, j * 512:(j + 1) * 512], in_=bank[7][:, 0:512])),
                  writes=[rdst], banks=bank_regions(7))

    xt = [sb(f"l0a_xt{i}", [128, D], F32) for i in range(2)]
    xn = [sb(f"l0a_xn{i}", [128, D], F32) for i in range(2)]
    xn2b = [sb(f"l0a_xn2b{i}", [128, D], BF16) for i in range(2)]
    xT = [sb(f"l0a_xT{i}", [128, D], BF16) for i in range(2)]
    qb = [sb(f"l0a_qb{i}", [128, D], BF16) for i in range(2)]
    kb = [sb(f"l0a_kb{i}", [128, D], BF16) for i in range(2)]
    vb = [sb(f"l0a_vb{i}", [128, D], BF16) for i in range(2)]
    reluP = [sb(f"l0a_relu{j}", [128, 2 * D], F32) for j in range(2)]
    mintP = [sb(f"l0a_min{j}", [128, 2 * D], F32) for j in range(2)]
    exptP = [sb(f"l0a_exp{j}", [128, 2 * D], F32) for j in range(2)]
    statP = [sb(f"l0a_stat{j}", [128, 10], F32) for j in range(2)]
    sqP = [sb(f"l0a_sq{j}", [128, D], F32) for j in range(2)]
    amax_run = sb("l0a_amax_run", [128, 1], F32)
    kv_acc = sb("l0a_kv_acc", [64, 1024], F32)
    ksum_acc = sb("l0a_ksum_acc", [1, 1024], F32)
    stats_out = [sb(f"l0a_stats_out{i}", [128, 2], F32) for i in range(2)]
    pb.op("v", ("memset", dict(ap=amax_run[:], constant=0.0)), writes=["l0a_amax_run"])

    inv_d = 1.0 / D

    def ln_stats(src_ap, rsrc):
        pb.op("v", ("reduce_sum", dict(out=stat[:, 0:1], in_=src_ap, axis=AX.X)),
              reads=[rsrc], writes=[f"st_s1{sfx}"])
        pb.op("s", ("activation", dict(out=sq[:], in_=src_ap, func=AF.Square, scale=1.0)),
              reads=[rsrc], writes=[f"l0a_sq{sfx}"])
        pb.op("v", ("reduce_sum", dict(out=stat[:, 1:2], in_=sq[:], axis=AX.X)),
              reads=[f"l0a_sq{sfx}"], writes=[f"st_s2{sfx}"])
        pb.op("v", ("tensor_scalar", dict(out=stat[:, 2:3], in0=stat[:, 0:1], scalar1=inv_d, scalar2=None, op0=ALU.mult)),
              reads=[f"st_s1{sfx}"], writes=[f"st_mu{sfx}"])
        pb.op("v", ("tensor_scalar", dict(out=stat[:, 3:4], in0=stat[:, 2:3], scalar1=stat[:, 2:3], scalar2=-1.0, op0=ALU.mult, op1=ALU.mult)),
              reads=[f"st_mu{sfx}"], writes=[f"st_nmu2{sfx}"])
        pb.op("v", ("tensor_scalar", dict(out=stat[:, 4:5], in0=stat[:, 1:2], scalar1=inv_d, scalar2=None, op0=ALU.mult)),
              reads=[f"st_s2{sfx}"], writes=[f"st_ex2{sfx}"])
        pb.op("v", ("tensor_tensor", dict(out=stat[:, 4:5], in0=stat[:, 4:5], in1=stat[:, 3:4], op=ALU.add)),
              reads=[f"st_ex2{sfx}", f"st_nmu2{sfx}"], writes=[f"st_var{sfx}"])
        pb.op("s", ("activation", dict(out=stat[:, 5:6], in_=stat[:, 4:5], func=AF.Sqrt, bias=cm.const(LN_EPS0), scale=1.0)),
              reads=[f"st_var{sfx}", cm.const_region(LN_EPS0)], writes=[f"st_sd{sfx}"])
        pb.op("v", ("reciprocal", dict(out=stat[:, 6:7], in_=stat[:, 5:6])),
              reads=[f"st_sd{sfx}"], writes=[f"st_rstd{sfx}"])
        pb.op("v", ("tensor_scalar", dict(out=stat[:, 7:8], in0=stat[:, 2:3], scalar1=stat[:, 6:7], scalar2=-1.0, op0=ALU.mult, op1=ALU.mult)),
              reads=[f"st_mu{sfx}", f"st_rstd{sfx}"], writes=[f"st_nmr{sfx}"])

    for i in range(NT):
        db = i % 2
        xt_i, xn_i, xn2b_i, xT_i = xt[db], xn[db], xn2b[db], xT[db]
        qb_i, kb_i, vb_i, so_i = qb[db], kb[db], vb[db], stats_out[db]
        rxt, rxn, rxn2b, rxT = f"l0a_xt{db}", f"l0a_xn{db}", f"l0a_xn2b{db}", f"l0a_xT{db}"
        rqb, rkb, rvb, rso = f"l0a_qb{db}", f"l0a_kb{db}", f"l0a_vb{db}", f"l0a_stats_out{db}"
        relu_t, mint, expt, stat, sq = reluP[db], mintP[db], exptP[db], statP[db], sqP[db]
        sfx = str(db)

        pb.dma("y", ("dma_start", dict(out=xt_i[:], in_=io["x"][i * 128:(i + 1) * 128, :])),
               reads=["h_x"], writes=[rxt])

        ln_stats(xt_i[:], rxt)
        pb.op("s", ("activation", dict(out=xn_i[:], in_=xt_i[:], func=AF.Identity, bias=stat[:, 7:8], scale=stat[:, 6:7])),
              reads=[rxt, f"st_nmr{sfx}", f"st_rstd{sfx}"], writes=[rxn])
        pb.op("v", ("tensor_copy", dict(out=so_i[:, 0:1], in_=stat[:, 2:3])), reads=[f"st_mu{sfx}"], writes=[rso])
        pb.op("v", ("tensor_copy", dict(out=so_i[:, 1:2], in_=stat[:, 6:7])), reads=[f"st_rstd{sfx}"], writes=[rso])
        pb.dma("p", ("dma_start", dict(out=io["stats_h"][i], in_=so_i[:])), reads=[rso], writes=["h_stats"])
        pb.op("v", ("tensor_reduce", dict(out=stat[:, 8:9], in_=xn_i[:], axis=AX.X, op=ALU.max, apply_absolute_value=True)),
              reads=[rxn], writes=[f"st_am{sfx}"])
        pb.op("v", ("tensor_tensor", dict(out=amax_run[:], in0=amax_run[:], in1=stat[:, 8:9], op=ALU.max)),
              reads=[f"st_am{sfx}", "l0a_amax_run"], writes=["l0a_amax_run"])
        pb.dma("y", ("dma_start", dict(out=io["xn_h"][i * 128:(i + 1) * 128, :], in_=xn_i[:])),
               reads=[rxn], writes=["h_xn"])

        ln_stats(xn_i[:], rxn)
        pb.op("s", ("activation", dict(out=sq[:], in_=xn_i[:], func=AF.Identity, bias=stat[:, 7:8], scale=stat[:, 6:7])),
              reads=[rxn, f"st_nmr{sfx}", f"st_rstd{sfx}"], writes=[f"l0a_sq{sfx}"])
        pb.op("v", ("tensor_tensor", dict(out=sq[:], in0=sq[:], in1=gamma_b[:], op=ALU.mult)),
              reads=[f"l0a_sq{sfx}", "l0a_gamma_b"], writes=[f"l0a_sq{sfx}"])
        pb.op("v", ("tensor_tensor", dict(out=xn2b_i[:], in0=sq[:], in1=beta_b[:], op=ALU.add)),
              reads=[f"l0a_sq{sfx}", "l0a_beta_b"], writes=[rxn2b])

        for c in range(8):
            pb.op("t", ("transpose", dict(out=bank_bf[0][:, c * 128:(c + 1) * 128],
                                          in_=xn2b_i[:, c * 128:(c + 1) * 128], identity=cm.ident_b[:])),
                  reads=[rxn2b, "cc_ident"], banks=bank_regions(0))
        pb.op("s", ("copy", dict(out=xT_i[:], in_=bank_bf[0][:, 0:1024])),
              writes=[rxT], banks=bank_regions(0))

        for nb in range(6):
            for k in range(8):
                pb.op("t", ("matmul", dict(out=bank[nb][:, 0:512], lhsT=xT_i[:, k * 128:(k + 1) * 128],
                                           rhs=wq_b[k][:, nb * 512:(nb + 1) * 512],
                                           start=(k == 0), stop=(k == 7))),
                      reads=[rxT, f"l0a_wqb{k}"], banks=bank_regions(nb))

        for half, (b0_, b1_) in enumerate([(0, 1), (2, 3)]):
            dst = qb_i if half == 0 else kb_i
            rdst = rqb if half == 0 else rkb
            for j, b_ in enumerate((b0_, b1_)):
                pb.op("s", ("activation", dict(out=relu_t[:, half * D + j * 512: half * D + (j + 1) * 512],
                                               in_=bank[b_][:, 0:512], func=AF.Relu, scale=1.0)),
                      writes=[f"l0a_relu{sfx}"], banks=bank_regions(b_))
                pb.op("v", ("tensor_scalar", dict(out=mint[:, half * D + j * 512: half * D + (j + 1) * 512],
                                                  in0=bank[b_][:, 0:512], scalar1=0.0, scalar2=None, op0=ALU.min)),
                      writes=[f"l0a_min{sfx}"], banks=bank_regions(b_))
            pb.op("s", ("activation", dict(out=expt[:, half * D:(half + 1) * D], in_=mint[:, half * D:(half + 1) * D],
                                           func=AF.Exp, scale=1.0)),
                  reads=[f"l0a_min{sfx}"], writes=[f"l0a_exp{sfx}"])
            pb.op("v", ("tensor_tensor", dict(out=dst[:], in0=expt[:, half * D:(half + 1) * D],
                                              in1=relu_t[:, half * D:(half + 1) * D], op=ALU.add)),
                  reads=[f"l0a_exp{sfx}", f"l0a_relu{sfx}"], writes=[rdst])
        pb.op("s", ("copy", dict(out=vb_i[:, 0:512], in_=bank[4][:, 0:512])), writes=[rvb], banks=bank_regions(4))
        pb.op("s", ("copy", dict(out=vb_i[:, 512:1024], in_=bank[5][:, 0:512])), writes=[rvb], banks=bank_regions(5))

        pb.dma("p", ("dma_start", dict(out=io["q_h"][i * 128:(i + 1) * 128, :], in_=qb_i[:])),
               reads=[rqb], writes=["h_q"])

        for bnk in (6, 7):
            for j in range(8):
                h_ = 2 * j + (bnk - 6)
                pb.op("t", ("matmul", dict(out=bank[bnk][0:64, j * 64:(j + 1) * 64],
                                           lhsT=kb_i[:, h_ * 64:(h_ + 1) * 64], rhs=vb_i[:, h_ * 64:(h_ + 1) * 64],
                                           start=(j == 0), stop=(j == 7))),
                      reads=[rkb, rvb], banks=bank_regions(bnk))
            jj = bnk - 6
            pb.op("t", ("matmul", dict(out=bank[bnk][64:65, 0:512], lhsT=cm.ones128_b[:],
                                       rhs=kb_i[:, jj * 512:(jj + 1) * 512], start=True, stop=True)),
                  reads=[rkb, "cc_ones128"], banks=bank_regions(bnk))
            if i == 0:
                pb.op("v", ("tensor_copy", dict(out=kv_acc[0:64, (bnk - 6) * 512:(bnk - 5) * 512], in_=bank[bnk][0:64, 0:512])),
                      writes=["l0a_kv_acc"], banks=bank_regions(bnk))
                pb.op("v", ("tensor_copy", dict(out=ksum_acc[0:1, (bnk - 6) * 512:(bnk - 5) * 512], in_=bank[bnk][64:65, 0:512])),
                      writes=["l0a_ksum_acc"], banks=bank_regions(bnk))
            else:
                pb.op("v", ("tensor_tensor", dict(out=kv_acc[0:64, (bnk - 6) * 512:(bnk - 5) * 512],
                                                  in0=kv_acc[0:64, (bnk - 6) * 512:(bnk - 5) * 512],
                                                  in1=bank[bnk][0:64, 0:512], op=ALU.add)),
                      reads=["l0a_kv_acc"], writes=["l0a_kv_acc"], banks=bank_regions(bnk))
                pb.op("v", ("tensor_tensor", dict(out=ksum_acc[0:1, (bnk - 6) * 512:(bnk - 5) * 512],
                                                  in0=ksum_acc[0:1, (bnk - 6) * 512:(bnk - 5) * 512],
                                                  in1=bank[bnk][64:65, 0:512], op=ALU.add)),
                      reads=["l0a_ksum_acc"], writes=["l0a_ksum_acc"], banks=bank_regions(bnk))

    pb.dma("y", ("dma_start", dict(out=io["kv_h"], in_=kv_acc[:])), reads=["l0a_kv_acc"], writes=["h_kv"])
    pb.dma("y", ("dma_start", dict(out=io["ksum_h"], in_=ksum_acc[:])), reads=["l0a_ksum_acc"], writes=["h_ksum"])
    am_scratch = nc.dram_tensor("am_scratch", [128], F32, kind="Internal").ap()
    am_row = sb("l0a_am_row", [1, 128], F32)
    am_out = sb("l0a_am_out", [1, 1], F32)
    pb.dma("y", ("dma_start", dict(out=am_scratch.rearrange("(a b) -> a b", a=128), in_=amax_run[:])),
           reads=["l0a_amax_run"], writes=["h_am_scratch"])
    pb.dma("y", ("dma_start", dict(out=am_row[:], in_=am_scratch.rearrange("(a b) -> a b", a=1))),
           reads=["h_am_scratch"], writes=["l0a_am_row"])
    pb.op("v", ("reduce_max", dict(out=am_out[:], in_=am_row[:], axis=AX.X)), reads=["l0a_am_row"], writes=["l0a_am_out"])
    pb.dma("y", ("dma_start", dict(out=io["amax_h"], in_=am_out[:])), reads=["l0a_am_out"], writes=["h_amax"])


def build_l0b():
    nc = bass.Bass("TRN2", target_bir_lowering=False)
    io = dict(
        x=nc.dram_tensor("x", [L, D], F32, kind="ExternalInput").ap(),
        xn_h=nc.dram_tensor("xn_h", [L, D], F32, kind="ExternalInput").ap(),
        q_h=nc.dram_tensor("q_h", [L, D], BF16, kind="ExternalInput").ap(),
        kv_h=nc.dram_tensor("kv_h", [64, 1024], F32, kind="ExternalInput").ap(),
        ksum_h=nc.dram_tensor("ksum_h", [1, 1024], F32, kind="ExternalInput").ap(),
        stats_h=nc.dram_tensor("stats_h", [NT, 128, 2], F32, kind="ExternalInput").ap(),
        amax8=nc.dram_tensor("amax8", [1, 8], F32, kind="ExternalInput").ap(),
        wout=nc.dram_tensor("wout", [D, D], F32, kind="ExternalInput").ap(),
        w_in=nc.dram_tensor("w_in", [D, DFF0], F32, kind="ExternalInput").ap(),
        b_in=nc.dram_tensor("b_in", [1, DFF0], F32, kind="ExternalInput").ap(),
        part1_h=nc.dram_tensor("part1_h", [L, D], F32, kind="ExternalOutput").ap(),
        h2_h=nc.dram_tensor("h2_h", [L, DFF0], F32, kind="ExternalOutput").ap(),
        amaxh2=nc.dram_tensor("amaxh2", [1, 1], F32, kind="ExternalOutput").ap(),
    )
    ctx = ExitStack()
    with ctx:
        pb = PB(nc, ctx)
        cm = Common(pb, ctx)
        emit_l0b(pb, cm, ctx, io)
        pb.emit()
    return nc


def emit_l0b(pb, cm, ctx, io):
    nc = pb.nc
    sb = lambda name, shape, dt: ctx.enter_context(nc.sbuf_tensor(name, shape, dt))
    ps = lambda name, shape, dt: ctx.enter_context(nc.psum_tensor("pb_" + name, shape, dt))
    cm.const(MAGIC)
    cm.const(-MAGIC)

    bank = [ps(f"bk{i}", [128, 512], F32) for i in range(8)]
    bank_bf = [b[:].bitcast(BF16) for b in bank]
    ident_f = sb("l0b_ident_f", [128, 128], F32)
    pb.op("v", ("memset", dict(ap=ident_f[:], constant=1.0)), writes=["idf_tmp"])
    pb.op("p", ("affine_select", dict(out=ident_f[:], in_=ident_f[:], pattern=[[1, 128]],
                                      compare_op=ALU.is_equal, fill=0.0, base=0, channel_multiplier=-1)),
          reads=["idf_tmp"], writes=["idf"])
    ones_row_f = sb("l0b_ones_row_f", [1, 128], F32)
    pb.op("v", ("memset", dict(ap=ones_row_f[:], constant=1.0)), writes=["ones_row_f"])

    am8 = sb("l0b_am8", [1, 8], F32)
    sxr = sb("l0b_sxr", [1, 2], F32)
    sx_b = sb("l0b_sx_b", [128, 2], F32)
    kvb = sb("l0b_kvb", [64, 1024], BF16)
    ksum_b = sb("l0b_ksum_b", [128, 1024], F32)
    wout_b = [sb(f"l0b_woutb{k}", [128, D], BF16) for k in range(8)]
    bin_hi = sb("l0b_bin_hi", [1, DFF0], BF16)
    bin_lo = sb("l0b_bin_lo", [1, DFF0], BF16)
    win_b = [sb(f"l0b_winb{k}", [128, DFF0], BF16) for k in range(8)]

    amax_col_scr = nc.dram_tensor("amax_col_scr", [DFF0], F32, kind="Internal").ap()
    pre = ExitStack()
    psb = lambda name, shape, dt: pre.enter_context(nc.sbuf_tensor(name, shape, dt))
    wstg = [psb("l0b_wstg0", [128, DFF0], F32)]
    colmax = psb("l0b_colmax", [128, DFF0], F32)
    is_bcast = psb("l0b_is_bcast", [128, DFF0], F32)
    s_bcast = psb("l0b_s_bcast", [128, DFF0], F32)
    amax_row = psb("l0b_amax_row", [1, DFF0], F32)
    cred = psb("l0b_cred", [128, 2], F32)
    kvf = colmax[0:64, 0:1024]
    ksum_row = colmax[0:1, 1024:2048]
    wld = colmax[:, 2048:3072]

    pb.dma("y", ("dma_start", dict(out=am8[:], in_=io["amax8"])), reads=["h_amax8"], writes=["am8"])
    pb.op("v", ("reduce_max", dict(out=sxr[:, 0:1], in_=am8[:], axis=AX.X)), reads=["am8"], writes=["am_g"])
    pb.op("v", ("tensor_scalar", dict(out=sxr[:, 1:2], in0=sxr[:, 0:1], scalar1=1.0 / 32767.0, scalar2=None, op0=ALU.mult)),
          reads=["am_g"], writes=["sxr_inv"])
    pb.op("v", ("reciprocal", dict(out=sxr[:, 0:1], in_=sxr[:, 1:2])),
          reads=["sxr_inv"], writes=["sxr_s"])
    pb.op("t", ("matmul", dict(out=bank[0][:, 0:2], lhsT=ones_row_f[:], rhs=sxr[:, 0:2], start=True, stop=True)),
          reads=["sxr_s", "sxr_inv", "ones_row_f"], banks=bank_regions(0))
    pb.op("v", ("tensor_copy", dict(out=sx_b[:], in_=bank[0][:, 0:2])), writes=["sx_b"], banks=bank_regions(0))

    pb.dma("y", ("dma_start", dict(out=kvf, in_=io["kv_h"])), reads=["h_kv"], writes=["colmax"])
    pb.op("v", ("tensor_copy", dict(out=kvb[:], in_=kvf)), reads=["colmax"], writes=["kvb"])
    pb.dma("y", ("dma_start", dict(out=ksum_row, in_=io["ksum_h"])), reads=["h_ksum"], writes=["colmax"])
    for j in range(2):
        pb.op("t", ("matmul", dict(out=bank[1][:, 0:512], lhsT=ones_row_f[:], rhs=ksum_row[:, bass.ts(j, 512)],
                                   start=True, stop=True)),
              reads=["colmax", "ones_row_f"], banks=bank_regions(1))
        pb.op("v", ("tensor_copy", dict(out=ksum_b[:, j * 512:(j + 1) * 512], in_=bank[1][:, 0:512])),
              writes=["ksum_b"], banks=bank_regions(1))

    for k in range(8):
        pb.dma("y", ("dma_start", dict(out=wld, in_=io["wout"][k * 128:(k + 1) * 128, :])),
               reads=["h_wout"], writes=["colmax"])
        pb.op("v", ("tensor_copy", dict(out=wout_b[k][:], in_=wld)), reads=["colmax"], writes=[f"woutb{k}"])

    binrow = wstg[0][0:1, :]
    pb.dma("y", ("dma_start", dict(out=binrow, in_=io["b_in"])), reads=["h_bin"], writes=["wstg0"])
    pb.op("v", ("tensor_copy", dict(out=bin_hi[:], in_=binrow)), reads=["wstg0"], writes=["bin_hi"])
    pb.op("v", ("tensor_tensor", dict(out=binrow, in0=binrow, in1=bin_hi[:], op=ALU.subtract)),
          reads=["wstg0", "bin_hi"], writes=["wstg0"])
    pb.op("v", ("tensor_copy", dict(out=bin_lo[:], in_=binrow)), reads=["wstg0"], writes=["bin_lo"])

    for k in range(8):
        j = 0
        pb.dma("y", ("dma_start", dict(out=wstg[j][:], in_=io["w_in"][k * 128:(k + 1) * 128, :])),
               reads=["h_win"], writes=[f"wstg{j}"])
        pb.op("s", ("activation", dict(out=wstg[j][:], in_=wstg[j][:], func=AF.Abs, scale=1.0)),
              reads=[f"wstg{j}"], writes=[f"wstg{j}"])
        if k == 0:
            pb.op("v", ("tensor_copy", dict(out=colmax[:], in_=wstg[0][:])),
                  reads=["wstg0"], writes=["colmax"])
        else:
            pb.op("v", ("tensor_tensor", dict(out=colmax[:], in0=colmax[:], in1=wstg[j][:], op=ALU.max)),
                  reads=["colmax", f"wstg{j}"], writes=["colmax"])
    for c in range(32):
        pb.op("t", ("transpose", dict(out=bank[2][:, 0:128], in_=colmax[:, c * 128:(c + 1) * 128], identity=ident_f[:])),
              reads=["colmax", "idf"], banks=bank_regions(2))
        pb.op("v", ("reduce_max", dict(out=cred[:, 0:1], in_=bank[2][:, 0:128], axis=AX.X, apply_absolute_value=True)),
              writes=["cred"], banks=bank_regions(2))
        pb.dma("p", ("dma_start", dict(out=amax_col_scr[c * 128:(c + 1) * 128].rearrange("(a b) -> a b", a=128),
                                       in_=cred[:, 0:1])),
               reads=["cred"], writes=["h_amax_col"])
    pb.dma("y", ("dma_start", dict(out=amax_row[:], in_=amax_col_scr.rearrange("(a b) -> a b", a=1))),
           reads=["h_amax_col"], writes=["amax_row"])
    pb.op("v", ("tensor_scalar", dict(out=amax_row[:], in0=amax_row[:], scalar1=1e-8, scalar2=1.0 / 127.0,
                                      op0=ALU.max, op1=ALU.mult)),
          reads=["amax_row"], writes=["amax_row"])
    for j in range(8):
        pb.op("t", ("matmul", dict(out=bank[3][:, 0:512], lhsT=ones_row_f[:], rhs=amax_row[:, j * 512:(j + 1) * 512],
                                   start=True, stop=True)),
              reads=["amax_row", "ones_row_f"], banks=bank_regions(3))
        pb.op("v", ("tensor_copy", dict(out=is_bcast[:, j * 512:(j + 1) * 512], in_=bank[3][:, 0:512])),
              writes=["is_bcast"], banks=bank_regions(3))
    pb.op("v", ("reciprocal", dict(out=s_bcast[:], in_=is_bcast[:])), reads=["is_bcast"], writes=["s_bcast"])
    for k in range(8):
        j = 0
        rk = "wstg0"
        pb.dma("y", ("dma_start", dict(out=wstg[j][:], in_=io["w_in"][k * 128:(k + 1) * 128, :])),
               reads=["h_win"], writes=[rk])
        pb.op("v", ("tensor_tensor", dict(out=wstg[j][:], in0=wstg[j][:], in1=s_bcast[:], op=ALU.mult)),
              reads=[rk, "s_bcast"], writes=[rk])
        pb.op("s", ("activation", dict(out=wstg[j][:], in_=wstg[j][:], func=AF.Identity, bias=cm.const(MAGIC), scale=1.0)),
              reads=[rk, cm.const_region(MAGIC)], writes=[rk])
        pb.op("s", ("activation", dict(out=wstg[j][:], in_=wstg[j][:], func=AF.Identity, bias=cm.const(-MAGIC), scale=1.0)),
              reads=[rk, cm.const_region(-MAGIC)], writes=[rk])
        pb.op("v", ("tensor_scalar", dict(out=wstg[j][:], in0=wstg[j][:], scalar1=127.0, scalar2=-128.0, op0=ALU.min, op1=ALU.max)),
              reads=[rk], writes=[rk])
        pb.op("v", ("tensor_tensor", dict(out=win_b[k][:], in0=wstg[j][:], in1=is_bcast[:], op=ALU.mult)),
              reads=[rk, "is_bcast"], writes=[f"winb{k}"])
    pre.close()
    pb.mem_barrier()

    xt = [sb(f"l0b_xt{i}", [128, D], F32) for i in range(2)]
    xnt = [sb(f"l0b_xnt{i}", [128, D], F32) for i in range(2)]
    qt = [sb(f"l0b_qt{i}", [128, D], BF16) for i in range(2)]
    stt = [sb(f"l0b_stt{i}", [128, 2], F32) for i in range(2)]
    h2t = [sb(f"l0b_h2t{i}", [128, DFF0], F32) for i in range(2)]
    p1t = [sb(f"l0b_p1t{i}", [128, D], F32) for i in range(2)]
    qTP = [sb(f"l0b_qT{j}", [64, 2 * D], BF16) for j in range(2)]
    aobP = [sb(f"l0b_aob{j}", [128, D], BF16) for j in range(2)]
    aoTP = [sb(f"l0b_aoT{j}", [128, D], BF16) for j in range(2)]
    xsbP = [sb(f"l0b_xsb{j}", [128, D], BF16) for j in range(2)]
    xsTP = [sb(f"l0b_xsT{j}", [128, D], BF16) for j in range(2)]
    xxnP = [sb(f"l0b_xxn{j}", [128, D], F32) for j in range(2)]
    quantP = [sb(f"l0b_quant{j}", [128, D], F32) for j in range(2)]
    dumpP = [sb(f"l0b_dump{j}", [128, 64], F32) for j in range(2)]
    denP = [sb(f"l0b_den{j}", [128, 36], F32) for j in range(2)]
    amax_run = sb("l0b_amax_run", [128, 1], F32)
    amr = sb("l0b_amr", [128, 1], F32)
    pb.op("v", ("memset", dict(ap=amax_run[:], constant=0.0)), writes=["am_run"])

    for i in range(DBG.get("l0b_nt", NT)):
        db = i % 2
        xt_i, xnt_i, qt_i, stt_i, h2t_i, p1t_i = xt[db], xnt[db], qt[db], stt[db], h2t[db], p1t[db]
        rxt, rxn, rqt, rst, rh2, rp1 = (f"l0b_xt{db}", f"l0b_xnt{db}", f"l0b_qt{db}",
                                        f"l0b_stt{db}", f"l0b_h2t{db}", f"l0b_p1t{db}")
        qT, aob, aoT, xsb, xsT, xxn, quant, dump, den = (qTP[db], aobP[db], aoTP[db], xsbP[db],
                                                         xsTP[db], xxnP[db], quantP[db], dumpP[db], denP[db])
        sfx = str(db)
        sl = slice(i * 128, (i + 1) * 128)
        pb.dma("y", ("dma_start", dict(out=xnt_i[:], in_=io["xn_h"][sl, :])), reads=["h_xn"], writes=[rxn])
        pb.dma("y", ("dma_start", dict(out=qt_i[:], in_=io["q_h"][sl, :])), reads=["h_q"], writes=[rqt])
        pb.dma("p", ("dma_start", dict(out=xt_i[:], in_=io["x"][sl, :])), reads=["h_x"], writes=[rxt])
        pb.dma("p", ("dma_start", dict(out=stt_i[:], in_=io["stats_h"][i])), reads=["h_stats"], writes=[rst])

        skip_attn = DBG.get("l0b_skip_attn", False)
        skip_ffn = DBG.get("l0b_skip_ffn", False)
        if not skip_attn:
            upto = DBG.get("l0b_attn_upto", 99)
            if upto >= 1:
                for h in range(16):
                    tb = 0 if h < 8 else 5
                    pb.op("t", ("transpose", dict(out=bank_bf[tb][0:64, (h % 8) * 128:(h % 8 + 1) * 128],
                                                  in_=qt_i[:, h * 64:(h + 1) * 64], identity=cm.ident_b[:])),
                          reads=[rqt, "cc_ident"], banks=bank_regions(tb))
                pb.op("s", ("copy", dict(out=qT[:, 0:1024], in_=bank_bf[0][0:64, 0:1024])), writes=[f"qT{sfx}"], banks=bank_regions(0))
                pb.op("s", ("copy", dict(out=qT[:, 1024:2048], in_=bank_bf[5][0:64, 0:1024])), writes=[f"qT{sfx}"], banks=bank_regions(5))
            for h in range(16 if upto >= 2 else 0):
                bnk = 1 + h // 8
                pb.op("t", ("matmul", dict(out=bank[bnk][:, (h % 8) * 64:(h % 8 + 1) * 64],
                                           lhsT=qT[:, h * 128:(h + 1) * 128],
                                           rhs=kvb[:, (h % 2) * 512 + (h // 2) * 64:(h % 2) * 512 + (h // 2 + 1) * 64],
                                           start=True, stop=True)),
                      reads=[f"qT{sfx}", "kvb"], banks=bank_regions(bnk))
            for h in range(16 if upto >= 3 else 0):
                pb.op("v", ("scalar_tensor_tensor", dict(out=dump[:], in0=qt_i[:, h * 64:(h + 1) * 64], scalar=1.0,
                                                         in1=ksum_b[:, h * 64:(h + 1) * 64], op0=ALU.mult, op1=ALU.mult,
                                                         accum_out=den[:, h:h + 1])),
                      reads=[rqt, "ksum_b"], writes=[f"dump{sfx}", f"den{sfx}_{h}"])
            if upto >= 3:
                pb.op("v", ("tensor_scalar", dict(out=den[:, 0:16], in0=den[:, 0:16], scalar1=DENOM_EPS, scalar2=None, op0=ALU.add)),
                      reads=[f"den{sfx}_{h}" for h in range(16)], writes=[f"den_eps{sfx}"])
                pb.op("v", ("reciprocal", dict(out=den[:, 16:32], in_=den[:, 0:16])), reads=[f"den_eps{sfx}"], writes=[f"den_rec{sfx}"])
            for bnk in ((1, 2) if upto >= 4 else ()):
                off = (bnk - 1) * 8
                rec_b = den[:, 16 + off:16 + off + 8].unsqueeze(2).broadcast_to([128, 8, 64])
                pb.op("v", ("tensor_tensor", dict(out=aob[:, off * 64:(off + 8) * 64].rearrange("p (h e) -> p h e", h=8),
                                                  in0=bank[bnk][:, 0:512].rearrange("p (h e) -> p h e", h=8),
                                                  in1=rec_b, op=ALU.mult)),
                      reads=[f"den_rec{sfx}"], writes=[f"aob{sfx}"], banks=bank_regions(bnk))
            for c in range(8 if upto >= 5 else 0):
                pb.op("t", ("transpose", dict(out=bank_bf[db][:, c * 128:(c + 1) * 128],
                                              in_=aob[:, c * 128:(c + 1) * 128], identity=cm.ident_b[:])),
                      reads=[f"aob{sfx}", "cc_ident"], banks=bank_regions(0))
            pb.op("s", ("copy", dict(out=aoT[:], in_=bank_bf[db][:, 0:1024])), writes=[f"aoT{sfx}"], banks=bank_regions(0))
            for nb in range(2 if upto >= 6 else 0):
                for k in range(8):
                    pb.op("t", ("matmul", dict(out=bank[3 + nb][:, 0:512], lhsT=aoT[:, k * 128:(k + 1) * 128],
                                               rhs=wout_b[k][:, nb * 512:(nb + 1) * 512], start=(k == 0), stop=(k == 7))),
                          reads=[f"aoT{sfx}", f"woutb{k}"], banks=bank_regions(3 + nb))
            if upto >= 7:
                pb.op("v", ("tensor_scalar", dict(out=den[:, 32:33], in0=stt_i[:, 1:2], scalar1=1.0, scalar2=None, op0=ALU.add)),
                      reads=[rst], writes=[f"sc1{sfx}"])
                pb.op("v", ("tensor_scalar", dict(out=den[:, 33:34], in0=stt_i[:, 0:1], scalar1=stt_i[:, 1:2], scalar2=-1.0,
                                                  op0=ALU.mult, op1=ALU.mult)),
                      reads=[rst], writes=[f"bi1{sfx}"])
                pb.op("s", ("activation", dict(out=xxn[:], in_=xt_i[:], func=AF.Identity, bias=den[:, 33:34], scale=den[:, 32:33])),
                      reads=[rxt, f"sc1{sfx}", f"bi1{sfx}"], writes=[f"xxn{sfx}"])
                for nb in range(2):
                    pb.op("v", ("tensor_tensor", dict(out=p1t_i[:, nb * 512:(nb + 1) * 512], in0=xxn[:, nb * 512:(nb + 1) * 512],
                                                      in1=bank[3 + nb][:, 0:512], op=ALU.add)),
                          reads=[f"xxn{sfx}"], writes=[rp1], banks=bank_regions(3 + nb))
                pb.dma("y", ("dma_start", dict(out=io["part1_h"][sl, :], in_=p1t_i[:])), reads=[rp1], writes=["h_part1"])

        if not skip_ffn:
            pb.op("v", ("tensor_scalar", dict(out=quant[:], in0=xnt_i[:], scalar1=sx_b[:, 0:1], scalar2=MAGIC,
                                              op0=ALU.mult, op1=ALU.add)),
                  reads=[rxn, "sx_b"], writes=[f"quant{sfx}"])
            pb.op("s", ("activation", dict(out=quant[:], in_=quant[:], func=AF.Identity, bias=cm.const(-MAGIC), scale=1.0)),
                  reads=[f"quant{sfx}", cm.const_region(-MAGIC)], writes=[f"quant{sfx}"])
            pb.op("v", ("tensor_scalar", dict(out=quant[:], in0=quant[:], scalar1=32767.0, scalar2=-32768.0,
                                              op0=ALU.min, op1=ALU.max)),
                  reads=[f"quant{sfx}"], writes=[f"quant{sfx}"])
            pb.op("v", ("tensor_scalar", dict(out=xsb[:], in0=quant[:], scalar1=sx_b[:, 1:2], scalar2=None, op0=ALU.mult)),
                  reads=[f"quant{sfx}", "sx_b"], writes=[f"xsb{sfx}"])
            for c in range(8):
                pb.op("t", ("transpose", dict(out=bank_bf[db][:, c * 128:(c + 1) * 128],
                                              in_=xsb[:, c * 128:(c + 1) * 128], identity=cm.ident_b[:])),
                      reads=[f"xsb{sfx}", "cc_ident"], banks=bank_regions(0))
            pb.op("s", ("copy", dict(out=xsT[:], in_=bank_bf[db][:, 0:1024])), writes=[f"xsT{sfx}"], banks=bank_regions(0))
            for nb in range(8):
                bnk = 5 + nb % 3
                for k in range(8):
                    pb.op("t", ("matmul", dict(out=bank[bnk][:, 0:512], lhsT=xsT[:, k * 128:(k + 1) * 128],
                                               rhs=win_b[k][:, nb * 512:(nb + 1) * 512], start=(k == 0), stop=False)),
                          reads=[f"xsT{sfx}", f"winb{k}"], banks=bank_regions(bnk))
                pb.op("t", ("matmul", dict(out=bank[bnk][:, 0:512], lhsT=cm.ones_row_b[:],
                                           rhs=bin_hi[:, nb * 512:(nb + 1) * 512], start=False, stop=False)),
                      reads=["bin_hi", "cc_ones_row"], banks=bank_regions(bnk))
                pb.op("t", ("matmul", dict(out=bank[bnk][:, 0:512], lhsT=cm.ones_row_b[:],
                                           rhs=bin_lo[:, nb * 512:(nb + 1) * 512], start=False, stop=True)),
                      reads=["bin_lo", "cc_ones_row"], banks=bank_regions(bnk))
                pb.op("s", ("activation", dict(out=h2t_i[:, nb * 512:(nb + 1) * 512], in_=bank[bnk][:, 0:512],
                                               func=AF.Gelu, scale=1.0)),
                      writes=[rh2], banks=bank_regions(bnk))
                pb.op("v", ("tensor_reduce", dict(out=amr[:], in_=h2t_i[:, nb * 512:(nb + 1) * 512], axis=AX.X,
                                                  op=ALU.max, apply_absolute_value=True)),
                      reads=[rh2], writes=["amr"])
                pb.op("v", ("tensor_tensor", dict(out=amax_run[:], in0=amax_run[:], in1=amr[:], op=ALU.max)),
                      reads=["amr", "am_run"], writes=["am_run"])
            pb.dma("p", ("dma_start", dict(out=io["h2_h"][sl, :], in_=h2t_i[:])), reads=[rh2], writes=["h_h2"])

    am_scr = nc.dram_tensor("am_scr2", [128], F32, kind="Internal").ap()
    am_row = sb("l0b_am_rrow", [1, 128], F32)
    am_out = sb("l0b_am_out", [1, 1], F32)
    pb.dma("y", ("dma_start", dict(out=am_scr.rearrange("(a b) -> a b", a=128), in_=amax_run[:])),
           reads=["am_run"], writes=["h_amscr"])
    pb.dma("y", ("dma_start", dict(out=am_row[:], in_=am_scr.rearrange("(a b) -> a b", a=1))),
           reads=["h_amscr"], writes=["am_rrow"])
    pb.op("v", ("reduce_max", dict(out=am_out[:], in_=am_row[:], axis=AX.X)), reads=["am_rrow"], writes=["am_o"])
    pb.dma("y", ("dma_start", dict(out=io["amaxh2"], in_=am_out[:])), reads=["am_o"], writes=["h_amaxh2"])


def build_l0c():
    nc = bass.Bass("TRN2", target_bir_lowering=False)
    io = dict(
        part1_h=nc.dram_tensor("part1_h", [L, D], F32, kind="ExternalInput").ap(),
        h2_h=nc.dram_tensor("h2_h", [L, DFF0], F32, kind="ExternalInput").ap(),
        amaxh2_8=nc.dram_tensor("amaxh2_8", [1, 8], F32, kind="ExternalInput").ap(),
        w_out=nc.dram_tensor("w_out", [DFF0, D], F32, kind="ExternalInput").ap(),
        b_out=nc.dram_tensor("b_out", [1, D], F32, kind="ExternalInput").ap(),
        x1_h=nc.dram_tensor("x1_h", [L, D], F32, kind="ExternalOutput").ap(),
        xn1_h=nc.dram_tensor("xn1_h", [L, D], F32, kind="ExternalOutput").ap(),
        amax_xn1=nc.dram_tensor("amax_xn1", [1, 1], F32, kind="ExternalOutput").ap(),
    )
    ctx = ExitStack()
    with ctx:
        pb = PB(nc, ctx)
        cm = Common(pb, ctx)
        emit_l0c(pb, cm, ctx, io)
        pb.emit()
    return nc


def emit_l0c(pb, cm, ctx, io):
    nc = pb.nc
    sb = lambda name, shape, dt: ctx.enter_context(nc.sbuf_tensor(name, shape, dt))
    ps = lambda name, shape, dt: ctx.enter_context(nc.psum_tensor("pc_" + name, shape, dt))
    cm.const(MAGIC); cm.const(-MAGIC); cm.const(LN_EPS0)

    bank = [ps(f"bk{i}", [128, 512], F32) for i in range(8)]
    bank_bf = [b[:].bitcast(BF16) for b in bank]
    ones_row_f = sb("l0c_ones_row_f", [1, 128], F32)
    pb.op("v", ("memset", dict(ap=ones_row_f[:], constant=1.0)), writes=["ones_row_f"])

    wout_b = [sb(f"l0c_woutb{k}", [128, D], BF16) for k in range(32)]
    bout_hi = sb("l0c_bout_hi", [1, D], BF16)
    bout_lo = sb("l0c_bout_lo", [1, D], BF16)
    sh_b = sb("l0c_sh_b", [128, 2], F32)

    pre = ExitStack()
    psb = lambda name, shape, dt: pre.enter_context(nc.sbuf_tensor(name, shape, dt))
    wstg = psb("l0c_wstg", [128, 1024 + 8], F32)
    wsc = psb("l0c_wsc", [128, 4], F32)

    am8 = wstg[0:1, 1024:1032]
    pb.dma("y", ("dma_start", dict(out=am8, in_=io["amaxh2_8"])), reads=["h_am8"], writes=["am8"])
    shr = psb("l0c_shr", [1, 2], F32)
    pb.op("v", ("reduce_max", dict(out=shr[0:1, 0:1], in_=am8, axis=AX.X)), reads=["am8"], writes=["shr0"])
    pb.op("v", ("tensor_scalar", dict(out=shr[0:1, 1:2], in0=shr[0:1, 0:1], scalar1=1.0 / 32767.0, scalar2=None, op0=ALU.mult)),
          reads=["shr0"], writes=["shr1"])
    pb.op("v", ("reciprocal", dict(out=shr[0:1, 0:1], in_=shr[0:1, 0:1])), reads=["shr0"], writes=["shr0"])
    pb.op("v", ("tensor_scalar", dict(out=shr[0:1, 0:1], in0=shr[0:1, 0:1], scalar1=32767.0, scalar2=None, op0=ALU.mult)),
          reads=["shr0"], writes=["shr0"])
    pb.op("t", ("matmul", dict(out=bank[0][:, 0:2], lhsT=ones_row_f[:], rhs=shr[:, 0:2], start=True, stop=True)),
          reads=["shr0", "shr1", "ones_row_f"], banks=bank_regions(0))
    pb.op("v", ("tensor_copy", dict(out=sh_b[:], in_=bank[0][:, 0:2])), writes=["sh_b"], banks=bank_regions(0))

    bo = wstg[0:1, 0:1024]
    pb.dma("y", ("dma_start", dict(out=bo, in_=io["b_out"])), reads=["h_bout"], writes=["wstg"])
    pb.op("v", ("tensor_copy", dict(out=bout_hi[:], in_=bo)), reads=["wstg"], writes=["bout_hi"])
    pb.op("v", ("tensor_tensor", dict(out=bo, in0=bo, in1=bout_hi[:], op=ALU.subtract)),
          reads=["wstg", "bout_hi"], writes=["wstg"])
    pb.op("v", ("tensor_copy", dict(out=bout_lo[:], in_=bo)), reads=["wstg"], writes=["bout_lo"])

    for k in range(32):
        pb.dma("y", ("dma_start", dict(out=wstg[:, 0:1024], in_=io["w_out"][k * 128:(k + 1) * 128, :])),
               reads=["h_wout2"], writes=["wstg"])
        pb.op("v", ("tensor_reduce", dict(out=wsc[:, 0:1], in_=wstg[:, 0:1024], axis=AX.X, op=ALU.max,
                                          apply_absolute_value=True)),
              reads=["wstg"], writes=["wsc_am"])
        pb.op("v", ("tensor_scalar", dict(out=wsc[:, 1:2], in0=wsc[:, 0:1], scalar1=1e-8, scalar2=1.0 / 127.0,
                                          op0=ALU.max, op1=ALU.mult)),
              reads=["wsc_am"], writes=["wsc_inv"])
        pb.op("v", ("reciprocal", dict(out=wsc[:, 2:3], in_=wsc[:, 1:2])), reads=["wsc_inv"], writes=["wsc_s"])
        pb.op("v", ("tensor_scalar", dict(out=wstg[:, 0:1024], in0=wstg[:, 0:1024], scalar1=wsc[:, 2:3], scalar2=MAGIC,
                                          op0=ALU.mult, op1=ALU.add)),
              reads=["wstg", "wsc_s"], writes=["wstg"])
        pb.op("s", ("activation", dict(out=wstg[:, 0:1024], in_=wstg[:, 0:1024], func=AF.Identity,
                                       bias=cm.const(-MAGIC), scale=1.0)),
              reads=["wstg", cm.const_region(-MAGIC)], writes=["wstg"])
        pb.op("v", ("tensor_scalar", dict(out=wstg[:, 0:1024], in0=wstg[:, 0:1024], scalar1=127.0, scalar2=-128.0,
                                          op0=ALU.min, op1=ALU.max)),
              reads=["wstg"], writes=["wstg"])
        pb.op("v", ("tensor_scalar", dict(out=wout_b[k][:], in0=wstg[:, 0:1024], scalar1=wsc[:, 1:2], scalar2=None,
                                          op0=ALU.mult)),
              reads=["wstg", "wsc_inv"], writes=[f"woutb{k}"])
    pre.close()
    pb.mem_barrier()

    h2t = [sb(f"l0c_h2t{i}", [128, DFF0], F32) for i in range(2)]
    p1t = [sb(f"l0c_p1t{i}", [128, D], F32) for i in range(2)]
    x1t = [sb(f"l0c_x1t{i}", [128, D], F32) for i in range(2)]
    xn1t = [sb(f"l0c_xn1t{i}", [128, D], F32) for i in range(2)]
    hsbP = [sb(f"l0c_hsb{j}", [128, DFF0], BF16) for j in range(2)]
    hsTP = [sb(f"l0c_hsT{j}", [128, DFF0], BF16) for j in range(2)]
    statP = [sb(f"l0c_stat{j}", [128, 10], F32) for j in range(2)]
    sqP = [sb(f"l0c_sq{j}", [128, D], F32) for j in range(2)]
    amax_run = sb("l0c_amax_run", [128, 1], F32)
    pb.op("v", ("memset", dict(ap=amax_run[:], constant=0.0)), writes=["am_run"])
    inv_d = 1.0 / D

    for i in range(NT):
        db = i % 2
        h2_i, p1_i, x1_i, xn1_i = h2t[db], p1t[db], x1t[db], xn1t[db]
        rh2, rp1, rx1, rxn1 = f"l0c_h2t{db}", f"l0c_p1t{db}", f"l0c_x1t{db}", f"l0c_xn1t{db}"
        hsb, hsT, stat, sq = hsbP[db], hsTP[db], statP[db], sqP[db]
        sfx = str(db)
        sl = slice(i * 128, (i + 1) * 128)
        pb.dma("y", ("dma_start", dict(out=h2_i[:], in_=io["h2_h"][sl, :])), reads=["h_h2"], writes=[rh2])
        pb.dma("p", ("dma_start", dict(out=p1_i[:], in_=io["part1_h"][sl, :])), reads=["h_part1"], writes=[rp1])

        pb.op("v", ("tensor_scalar", dict(out=h2_i[:], in0=h2_i[:], scalar1=sh_b[:, 0:1], scalar2=MAGIC,
                                          op0=ALU.mult, op1=ALU.add)),
              reads=[rh2, "sh_b"], writes=[rh2])
        pb.op("s", ("activation", dict(out=h2_i[:], in_=h2_i[:], func=AF.Identity, bias=cm.const(-MAGIC), scale=1.0)),
              reads=[rh2, cm.const_region(-MAGIC)], writes=[rh2])
        pb.op("v", ("tensor_scalar", dict(out=h2_i[:], in0=h2_i[:], scalar1=32767.0, scalar2=-32768.0,
                                          op0=ALU.min, op1=ALU.max)),
              reads=[rh2], writes=[rh2])
        pb.op("v", ("tensor_scalar", dict(out=hsb[:], in0=h2_i[:], scalar1=sh_b[:, 1:2], scalar2=None, op0=ALU.mult)),
              reads=[rh2, "sh_b"], writes=[f"hsb{sfx}"])
        for rnd in range(2):
            for c in range(16):
                cc = rnd * 16 + c
                tb = c // 8
                pb.op("t", ("transpose", dict(out=bank_bf[tb][:, (c % 8) * 128:(c % 8 + 1) * 128],
                                              in_=hsb[:, cc * 128:(cc + 1) * 128], identity=cm.ident_b[:])),
                      reads=[f"hsb{sfx}", "cc_ident"], banks=bank_regions(tb))
            for tb in range(2):
                pb.op("s", ("copy", dict(out=hsT[:, rnd * 2048 + tb * 1024: rnd * 2048 + (tb + 1) * 1024],
                                         in_=bank_bf[tb][:, 0:1024])),
                      writes=[f"hsT{sfx}"], banks=bank_regions(tb))
        for nb in range(2):
            for k in range(32):
                pb.op("t", ("matmul", dict(out=bank[2 + nb][:, 0:512], lhsT=hsT[:, k * 128:(k + 1) * 128],
                                           rhs=wout_b[k][:, nb * 512:(nb + 1) * 512], start=(k == 0), stop=False)),
                      reads=[f"hsT{sfx}", f"woutb{k}"], banks=bank_regions(2 + nb))
            pb.op("t", ("matmul", dict(out=bank[2 + nb][:, 0:512], lhsT=cm.ones_row_b[:],
                                       rhs=bout_hi[:, nb * 512:(nb + 1) * 512], start=False, stop=False)),
                  reads=["bout_hi", "cc_ones_row"], banks=bank_regions(2 + nb))
            pb.op("t", ("matmul", dict(out=bank[2 + nb][:, 0:512], lhsT=cm.ones_row_b[:],
                                       rhs=bout_lo[:, nb * 512:(nb + 1) * 512], start=False, stop=True)),
                  reads=["bout_lo", "cc_ones_row"], banks=bank_regions(2 + nb))
            pb.op("v", ("tensor_tensor", dict(out=x1_i[:, nb * 512:(nb + 1) * 512], in0=p1_i[:, nb * 512:(nb + 1) * 512],
                                              in1=bank[2 + nb][:, 0:512], op=ALU.add)),
                  reads=[rp1], writes=[rx1], banks=bank_regions(2 + nb))
        pb.dma("y", ("dma_start", dict(out=io["x1_h"][sl, :], in_=x1_i[:])), reads=[rx1], writes=["h_x1"])

        pb.op("v", ("reduce_sum", dict(out=stat[:, 0:1], in_=x1_i[:], axis=AX.X)), reads=[rx1], writes=[f"st_s1{sfx}"])
        pb.op("s", ("activation", dict(out=sq[:], in_=x1_i[:], func=AF.Square, scale=1.0)), reads=[rx1], writes=[f"sq{sfx}"])
        pb.op("v", ("reduce_sum", dict(out=stat[:, 1:2], in_=sq[:], axis=AX.X)), reads=[f"sq{sfx}"], writes=[f"st_s2{sfx}"])
        pb.op("v", ("tensor_scalar", dict(out=stat[:, 2:3], in0=stat[:, 0:1], scalar1=inv_d, scalar2=None, op0=ALU.mult)),
              reads=[f"st_s1{sfx}"], writes=[f"st_mu{sfx}"])
        pb.op("v", ("tensor_scalar", dict(out=stat[:, 3:4], in0=stat[:, 2:3], scalar1=stat[:, 2:3], scalar2=-1.0,
                                          op0=ALU.mult, op1=ALU.mult)),
              reads=[f"st_mu{sfx}"], writes=[f"st_nmu2{sfx}"])
        pb.op("v", ("tensor_scalar", dict(out=stat[:, 4:5], in0=stat[:, 1:2], scalar1=inv_d, scalar2=None, op0=ALU.mult)),
              reads=[f"st_s2{sfx}"], writes=[f"st_ex2{sfx}"])
        pb.op("v", ("tensor_tensor", dict(out=stat[:, 4:5], in0=stat[:, 4:5], in1=stat[:, 3:4], op=ALU.add)),
              reads=[f"st_ex2{sfx}", f"st_nmu2{sfx}"], writes=[f"st_var{sfx}"])
        pb.op("s", ("activation", dict(out=stat[:, 5:6], in_=stat[:, 4:5], func=AF.Sqrt, bias=cm.const(LN_EPS0), scale=1.0)),
              reads=[f"st_var{sfx}", cm.const_region(LN_EPS0)], writes=[f"st_sd{sfx}"])
        pb.op("v", ("reciprocal", dict(out=stat[:, 6:7], in_=stat[:, 5:6])), reads=[f"st_sd{sfx}"], writes=[f"st_rstd{sfx}"])
        pb.op("v", ("tensor_scalar", dict(out=stat[:, 7:8], in0=stat[:, 2:3], scalar1=stat[:, 6:7], scalar2=-1.0,
                                          op0=ALU.mult, op1=ALU.mult)),
              reads=[f"st_mu{sfx}", f"st_rstd{sfx}"], writes=[f"st_nmr{sfx}"])
        pb.op("s", ("activation", dict(out=xn1_i[:], in_=x1_i[:], func=AF.Identity, bias=stat[:, 7:8], scale=stat[:, 6:7])),
              reads=[rx1, f"st_nmr{sfx}", f"st_rstd{sfx}"], writes=[rxn1])
        pb.op("v", ("tensor_reduce", dict(out=stat[:, 8:9], in_=xn1_i[:], axis=AX.X, op=ALU.max, apply_absolute_value=True)),
              reads=[rxn1], writes=[f"st_am{sfx}"])
        pb.op("v", ("tensor_tensor", dict(out=amax_run[:], in0=amax_run[:], in1=stat[:, 8:9], op=ALU.max)),
              reads=[f"st_am{sfx}", "am_run"], writes=["am_run"])
        pb.dma("p", ("dma_start", dict(out=io["xn1_h"][sl, :], in_=xn1_i[:])), reads=[rxn1], writes=["h_xn1"])

    am_scr = nc.dram_tensor("am_scr3", [128], F32, kind="Internal").ap()
    am_row = sb("l0c_am_row", [1, 128], F32)
    am_out = sb("l0c_am_out", [1, 1], F32)
    pb.dma("y", ("dma_start", dict(out=am_scr.rearrange("(a b) -> a b", a=128), in_=amax_run[:])),
           reads=["am_run"], writes=["h_amscr"])
    pb.dma("y", ("dma_start", dict(out=am_row[:], in_=am_scr.rearrange("(a b) -> a b", a=1))),
           reads=["h_amscr"], writes=["am_rrow"])
    pb.op("v", ("reduce_max", dict(out=am_out[:], in_=am_row[:], axis=AX.X)), reads=["am_rrow"], writes=["am_o"])
    pb.dma("y", ("dma_start", dict(out=io["amax_xn1"], in_=am_out[:])), reads=["am_o"], writes=["h_amax1"])


def build_l1a():
    nc = bass.Bass("TRN2", target_bir_lowering=False)
    U16 = mybir.dt.uint16
    io = dict(
        xn1_h=nc.dram_tensor("xn1_h", [L, D], F32, kind="ExternalInput").ap(),
        wq=nc.dram_tensor("wq", [D, D], F32, kind="ExternalInput").ap(),
        wk=nc.dram_tensor("wk", [D, D], F32, kind="ExternalInput").ap(),
        wv=nc.dram_tensor("wv", [D, D], F32, kind="ExternalInput").ap(),
        lr_w=nc.dram_tensor("lr_w", [H, D], F32, kind="ExternalInput").ap(),
        lr_b=nc.dram_tensor("lr_b", [1, H], F32, kind="ExternalInput").ap(),
        tnw=nc.dram_tensor("tnw", [1, D], F32, kind="ExternalInput").ap(),
        tnb=nc.dram_tensor("tnb", [1, D], F32, kind="ExternalInput").ap(),
        qtok_h=nc.dram_tensor("qtok_h", [L, D], F32, kind="ExternalOutput").ap(),
        ktok_h=nc.dram_tensor("ktok_h", [L, D], U16, kind="ExternalOutput").ap().bitcast(BF16),
        tgt_h=nc.dram_tensor("tgt_h", [L, D], F32, kind="ExternalOutput").ap(),
        eta_h=nc.dram_tensor("eta_h", [L, H], F32, kind="ExternalOutput").ap(),
        qT_h=nc.dram_tensor("qT_h", [NT, 64, 16 * 128], U16, kind="ExternalOutput").ap().bitcast(BF16),
        kT_h=nc.dram_tensor("kT_h", [NT, 64, 16 * 128], U16, kind="ExternalOutput").ap().bitcast(BF16),
    )
    ctx = ExitStack()
    with ctx:
        pb = PB(nc, ctx)
        cm = Common(pb, ctx)
        emit_l1a(pb, cm, ctx, io)
        pb.emit()
    return nc


def emit_l1a(pb, cm, ctx, io):
    nc = pb.nc
    sb = lambda name, shape, dt: ctx.enter_context(nc.sbuf_tensor(name, shape, dt))
    ps = lambda name, shape, dt: ctx.enter_context(nc.psum_tensor("pd" + name, shape, dt))
    cm.const(MAGIC); cm.const(-MAGIC); cm.const(1e-12); cm.const(1e-8)

    bank = [ps(f"bk{i}", [128, 512], F32) for i in range(8)]
    bank_bf = [b[:].bitcast(BF16) for b in bank]
    ones_row_f = sb("l1a_ones_row_f", [1, 128], F32)
    pb.op("v", ("memset", dict(ap=ones_row_f[:], constant=1.0)), writes=["ones_row_f"])

    wqT = [sb(f"l1a_wqT{k}", [128, D], BF16) for k in range(8)]
    wkT = [sb(f"l1a_wkT{k}", [128, D], BF16) for k in range(8)]
    wvT = [sb(f"l1a_wvT{k}", [128, D], BF16) for k in range(8)]
    lrT = [sb(f"l1a_lrT{k}", [128, H], BF16) for k in range(8)]
    lrb_hi = sb("l1a_lrb_hi", [1, H], BF16)
    lrb_lo = sb("l1a_lrb_lo", [1, H], BF16)
    g4b = sb("l1a_g4b", [128, D], F32)
    b4b = sb("l1a_b4b", [128, D], F32)
    cosb = sb("l1a_cosb", [128, D], F32)
    sinb = sb("l1a_sinb", [128, D], F32)

    pre = ExitStack()
    psb = lambda name, shape, dt: pre.enter_context(nc.sbuf_tensor(name, shape, dt))
    stg = psb("l1a_stg", [128, D], F32)
    stgb = psb("l1a_stgb", [128, D], BF16)
    trig = psb("l1a_trig", [128, 8], F32)
    trow = psb("l1a_trow", [1, 160], F32)

    for wname, wdst in (("wq", wqT), ("wk", wkT), ("wv", wvT)):
        for k in range(8):
            pb.dma("y", ("dma_start", dict(out=stg[:], in_=io[wname][k * 128:(k + 1) * 128, :])),
                   reads=[f"h_{wname}"], writes=["stg"])
            pb.op("v", ("tensor_copy", dict(out=stgb[:], in_=stg[:])), reads=["stg"], writes=["stgb"])
            for c in range(8):
                pb.op("t", ("transpose", dict(out=bank_bf[c % 2][:, (c // 2) * 128:(c // 2 + 1) * 128],
                                              in_=stgb[:, c * 128:(c + 1) * 128], identity=cm.ident_b[:])),
                      reads=["stgb", "cc_ident"], banks=bank_regions(c % 2))
            for c in range(8):
                pb.op("s", ("copy", dict(out=wdst[c][:, k * 128:(k + 1) * 128],
                                         in_=bank_bf[c % 2][:, (c // 2) * 128:(c // 2 + 1) * 128])),
                      writes=[f"{wname}T{c}"], banks=bank_regions(c % 2))
    pb.dma("y", ("dma_start", dict(out=stg[0:16, :], in_=io["lr_w"])), reads=["h_lrw"], writes=["stg"])
    pb.op("v", ("tensor_copy", dict(out=stgb[0:16, :], in_=stg[0:16, :])), reads=["stg"], writes=["stgb"])
    for k in range(8):
        pb.op("t", ("transpose", dict(out=bank_bf[2][0:128, k * 16:(k + 1) * 16],
                                      in_=stgb[0:16, k * 128:(k + 1) * 128], identity=cm.ident_b[0:16, 0:16])),
              reads=["stgb", "cc_ident"], banks=bank_regions(2))
    for k in range(8):
        pb.op("s", ("copy", dict(out=lrT[k][:], in_=bank_bf[2][:, k * 16:(k + 1) * 16])),
              writes=[f"lrT{k}"], banks=bank_regions(2))
    pb.dma("y", ("dma_start", dict(out=stg[0:1, 0:16], in_=io["lr_b"])), reads=["h_lrb"], writes=["stg"])
    pb.op("v", ("tensor_copy", dict(out=lrb_hi[:], in_=stg[0:1, 0:16])), reads=["stg"], writes=["lrb_hi"])
    pb.op("v", ("tensor_tensor", dict(out=stg[0:1, 0:16], in0=stg[0:1, 0:16], in1=lrb_hi[:], op=ALU.subtract)),
          reads=["stg", "lrb_hi"], writes=["stg"])
    pb.op("v", ("tensor_copy", dict(out=lrb_lo[:], in_=stg[0:1, 0:16])), reads=["stg"], writes=["lrb_lo"])
    for name, dst, rdst in (("tnw", g4b, "g4b"), ("tnb", b4b, "b4b")):
        pb.dma("y", ("dma_start", dict(out=trow[0:1, 0:128].tensor.ap()[0:1, 0:0] if False else stg[0:1, :], in_=io[name])),
               reads=[f"h_{name}"], writes=["stg"])
        for j in range(2):
            pb.op("t", ("matmul", dict(out=bank[3][:, 0:512], lhsT=ones_row_f[:], rhs=stg[0:1, j * 512:(j + 1) * 512],
                                       start=True, stop=True)),
                  reads=["stg", "ones_row_f"], banks=bank_regions(3))
            pb.op("v", ("tensor_copy", dict(out=dst[:, j * 512:(j + 1) * 512], in_=bank[3][:, 0:512])),
                  writes=[rdst], banks=bank_regions(3))

    pb.op("p", ("iota", dict(out=trow[0:1, 0:128], pattern=[[1, 128]], base=0, channel_multiplier=0, allow_small_or_imprecise_dtypes=True)),
          writes=["trow_pos"])
    posrow = trow[0:1, 0:128]
    tmaskr = psb("l1a_tmask", [1, 128], F32)
    pb.op("v", ("tensor_single_scalar", dict(out=tmaskr[:], in_=posrow, scalar=64.0, op=ALU.is_ge)),
          reads=["trow_pos"], writes=["tmask"])
    pb.op("v", ("scalar_tensor_tensor", dict(out=posrow, in0=tmaskr[:], scalar=-64.0, in1=posrow,
                                             op0=ALU.mult, op1=ALU.add)),
          reads=["tmask", "trow_pos"], writes=["trow_pos"])
    invf = trow[0:1, 128:160]
    pb.op("p", ("iota", dict(out=invf, pattern=[[1, 32]], base=0, channel_multiplier=0, allow_small_or_imprecise_dtypes=True)),
          writes=["invf"])
    pb.op("s", ("activation", dict(out=invf, in_=invf, func=AF.Exp, scale=-float(np.log(ROPE_THETA)) * 2.0 / 64.0)),
          reads=["invf"], writes=["invf"])
    pb.op("t", ("matmul", dict(out=bank[4][:, 0:32], lhsT=posrow, rhs=invf, start=True, stop=True)),
          reads=["trow_pos", "invf"], banks=bank_regions(4))
    fr = psb("l1a_fr", [128, 96], F32)
    pb.op("v", ("tensor_copy", dict(out=fr[:, 0:32], in_=bank[4][:, 0:32])), writes=["fr"], banks=bank_regions(4))
    cos_t = psb("l1a_cos", [128, 64], F32)
    sin_t = psb("l1a_sin", [128, 64], F32)
    TWO_PI = float(2.0 * np.pi)
    pb.op("v", ("tensor_scalar", dict(out=fr[:, 32:64], in0=fr[:, 0:32], scalar1=1.0 / TWO_PI, scalar2=MAGIC,
                                      op0=ALU.mult, op1=ALU.add)),
          reads=["fr"], writes=["fr32"])
    pb.op("s", ("activation", dict(out=fr[:, 32:64], in_=fr[:, 32:64], func=AF.Identity, bias=cm.const(-MAGIC), scale=1.0)),
          reads=["fr32", cm.const_region(-MAGIC)], writes=["fr32"])
    pb.op("v", ("scalar_tensor_tensor", dict(out=fr[:, 32:64], in0=fr[:, 32:64], scalar=-TWO_PI, in1=fr[:, 0:32],
                                             op0=ALU.mult, op1=ALU.add)),
          reads=["fr32", "fr"], writes=["fr32"])
    pb.op("s", ("activation", dict(out=sin_t[:, 0:32], in_=fr[:, 32:64], func=AF.Sin, scale=1.0)),
          reads=["fr32"], writes=["sin_t"])
    pb.op("v", ("tensor_scalar", dict(out=fr[:, 64:96], in0=fr[:, 0:32], scalar1=-1.0, scalar2=float(np.pi / 2),
                                      op0=ALU.mult, op1=ALU.add)),
          reads=["fr"], writes=["fr64"])
    pb.op("v", ("tensor_scalar", dict(out=fr[:, 32:64], in0=fr[:, 64:96], scalar1=1.0 / TWO_PI, scalar2=MAGIC,
                                      op0=ALU.mult, op1=ALU.add)),
          reads=["fr64"], writes=["fr32"])
    pb.op("s", ("activation", dict(out=fr[:, 32:64], in_=fr[:, 32:64], func=AF.Identity, bias=cm.const(-MAGIC), scale=1.0)),
          reads=["fr32", cm.const_region(-MAGIC)], writes=["fr32"])
    pb.op("v", ("scalar_tensor_tensor", dict(out=fr[:, 32:64], in0=fr[:, 32:64], scalar=-TWO_PI, in1=fr[:, 64:96],
                                             op0=ALU.mult, op1=ALU.add)),
          reads=["fr32", "fr64"], writes=["fr32"])
    pb.op("s", ("activation", dict(out=cos_t[:, 0:32], in_=fr[:, 32:64], func=AF.Sin, scale=1.0)),
          reads=["fr32"], writes=["cos_t"])
    pb.op("v", ("tensor_copy", dict(out=sin_t[:, 32:64], in_=sin_t[:, 0:32])), reads=["sin_t"], writes=["sin_t"])
    pb.op("v", ("tensor_copy", dict(out=cos_t[:, 32:64], in_=cos_t[:, 0:32])), reads=["cos_t"], writes=["cos_t"])
    pb.op("v", ("tensor_copy", dict(out=cosb[:].rearrange("p (h e) -> p h e", h=H),
                                    in_=cos_t[:].unsqueeze(1).broadcast_to([128, H, 64]))),
          reads=["cos_t"], writes=["cosb"])
    pb.op("v", ("tensor_copy", dict(out=sinb[:].rearrange("p (h e) -> p h e", h=H),
                                    in_=sin_t[:].unsqueeze(1).broadcast_to([128, H, 64]))),
          reads=["sin_t"], writes=["sinb"])
    pre.close()
    pb.mem_barrier()

    xnt = [sb(f"l1a_xnt{i}", [128, D], F32) for i in range(2)]
    xnbP = [sb(f"l1a_xnb{j}", [128, D], BF16) for j in range(2)]
    xnTP = [sb(f"l1a_xnT{j}", [128, D], BF16) for j in range(2)]
    qnP = [sb(f"l1a_qn{j}", [128, D], F32) for j in range(2)]
    knP = [sb(f"l1a_kn{j}", [128, D], F32) for j in range(2)]
    rotP = [sb(f"l1a_rot{j}", [128, D], F32) for j in range(2)]
    qr = [sb(f"l1a_qr{i}", [128, D], F32) for i in range(2)]
    kr = [sb(f"l1a_kr{i}", [128, D], F32) for i in range(2)]
    tgt = [sb(f"l1a_tgt{i}", [128, D], F32) for i in range(2)]
    eta_t = [sb(f"l1a_eta{i}", [128, H], F32) for i in range(2)]
    qrb = [sb(f"l1a_qrb{i}", [128, D], BF16) for i in range(2)]
    krb = [sb(f"l1a_krb{i}", [128, D], BF16) for i in range(2)]
    qTt = [sb(f"l1a_qTt{i}", [64, 2048], BF16) for i in range(2)]
    kTt = [sb(f"l1a_kTt{i}", [64, 2048], BF16) for i in range(2)]
    ssP = [sb(f"l1a_ss{j}", [128, 2 * H + 4], F32) for j in range(2)]
    dsP = [sb(f"l1a_ds{j}", [128, 2 * H], F32) for j in range(2)]
    sqsP = [sb(f"l1a_sqs{j}", [128, D], F32) for j in range(2)]
    dumpP = [sb(f"l1a_dump{j}", [128, 64], F32) for j in range(2)]

    for i in range(NT):
        db = i % 2
        xnt_i, qr_i, kr_i, tgt_i, eta_i = xnt[db], qr[db], kr[db], tgt[db], eta_t[db]
        qrb_i, krb_i, qTt_i, kTt_i = qrb[db], krb[db], qTt[db], kTt[db]
        rxn, rqr, rkr, rtgt, reta = f"l1a_xnt{db}", f"l1a_qr{db}", f"l1a_kr{db}", f"l1a_tgt{db}", f"l1a_eta{db}"
        rqrb, rkrb, rqT, rkT = f"l1a_qrb{db}", f"l1a_krb{db}", f"l1a_qTt{db}", f"l1a_kTt{db}"
        xnb, xnT, qn, kn, rot = xnbP[db], xnTP[db], qnP[db], knP[db], rotP[db]
        ss, ds, sqs, dump = ssP[db], dsP[db], sqsP[db], dumpP[db]
        sfx = str(db)
        sl = slice(i * 128, (i + 1) * 128)

        pb.dma("y", ("dma_start", dict(out=xnt_i[:], in_=io["xn1_h"][sl, :])), reads=["h_xn1"], writes=[rxn])
        pb.op("v", ("tensor_copy", dict(out=xnb[:], in_=xnt_i[:])), reads=[rxn], writes=[f"xnb{sfx}"])
        for c in range(8):
            pb.op("t", ("transpose", dict(out=bank_bf[0][:, c * 128:(c + 1) * 128],
                                          in_=xnb[:, c * 128:(c + 1) * 128], identity=cm.ident_b[:])),
                  reads=[f"xnb{sfx}", "cc_ident"], banks=bank_regions(0))
        pb.op("s", ("copy", dict(out=xnT[:], in_=bank_bf[0][:, 0:1024])), writes=[f"xnT{sfx}"], banks=bank_regions(0))

        for wn, wT, bks in (("wq", wqT, (1, 2)), ("wk", wkT, (3, 4)), ("wv", wvT, (5, 6))):
            for nb in range(2):
                for k in range(8):
                    pb.op("t", ("matmul", dict(out=bank[bks[nb]][:, 0:512], lhsT=xnT[:, k * 128:(k + 1) * 128],
                                               rhs=wT[k][:, nb * 512:(nb + 1) * 512], start=(k == 0), stop=(k == 7))),
                          reads=[f"xnT{sfx}", f"{wn}T{k}"], banks=bank_regions(bks[nb]))
        for k in range(8):
            pb.op("t", ("matmul", dict(out=bank[7][:, 0:16], lhsT=xnT[:, k * 128:(k + 1) * 128],
                                       rhs=lrT[k][:], start=(k == 0), stop=False)),
                  reads=[f"xnT{sfx}", f"lrT{k}"], banks=bank_regions(7))
        pb.op("t", ("matmul", dict(out=bank[7][:, 0:16], lhsT=cm.ones_row_b[:], rhs=lrb_hi[:], start=False, stop=False)),
              reads=["lrb_hi", "cc_ones_row"], banks=bank_regions(7))
        pb.op("t", ("matmul", dict(out=bank[7][:, 0:16], lhsT=cm.ones_row_b[:], rhs=lrb_lo[:], start=False, stop=True)),
              reads=["lrb_lo", "cc_ones_row"], banks=bank_regions(7))
        pb.op("s", ("activation", dict(out=eta_i[:], in_=bank[7][:, 0:16], func=AF.Sigmoid, scale=1.0)),
              writes=[reta], banks=bank_regions(7))
        pb.op("v", ("tensor_scalar", dict(out=eta_i[:], in0=eta_i[:], scalar1=1.0 / 64.0, scalar2=None, op0=ALU.mult)),
              reads=[reta], writes=[reta])
        pb.dma("p", ("dma_start", dict(out=io["eta_h"][sl, :], in_=eta_i[:])), reads=[reta], writes=["h_eta"])

        for (bks, dst, rdst, nrm_off) in (((1, 2), qn, f"qn{sfx}", 0), ((3, 4), kn, f"kn{sfx}", H)):
            for nb in range(2):
                pb.op("s", ("activation", dict(out=sqs[:, nb * 512:(nb + 1) * 512], in_=bank[bks[nb]][:, 0:512],
                                               func=AF.Square, scale=1.0)),
                      writes=[f"sqs{sfx}"], banks=bank_regions(bks[nb]))
            for h in range(H):
                pb.op("v", ("reduce_sum", dict(out=ss[:, nrm_off + h:nrm_off + h + 1], in_=sqs[:, h * 64:(h + 1) * 64], axis=AX.X)),
                      reads=[f"sqs{sfx}"], writes=[f"ss{sfx}_{nrm_off + h}"])
            pb.op("s", ("activation", dict(out=ss[:, nrm_off:nrm_off + H], in_=ss[:, nrm_off:nrm_off + H], func=AF.Sqrt, scale=1.0)),
                  reads=[f"ss{sfx}_{nrm_off + h}" for h in range(H)], writes=[f"ssn{sfx}"])
            pb.op("v", ("tensor_scalar", dict(out=ss[:, nrm_off:nrm_off + H], in0=ss[:, nrm_off:nrm_off + H],
                                              scalar1=1e-12, scalar2=None, op0=ALU.max)),
                  reads=[f"ssn{sfx}"], writes=[f"ssn{sfx}"])
            pb.op("v", ("reciprocal", dict(out=ss[:, nrm_off:nrm_off + H], in_=ss[:, nrm_off:nrm_off + H])),
                  reads=[f"ssn{sfx}"], writes=[f"ssn2{sfx}"])
            rn_b = ss[:, nrm_off:nrm_off + H].unsqueeze(2).broadcast_to([128, H, 64])
            for nb in range(2):
                pb.op("v", ("tensor_tensor", dict(out=dst[:, nb * 512:(nb + 1) * 512].rearrange("p (h e) -> p h e", h=8),
                                                  in0=bank[bks[nb]][:, 0:512].rearrange("p (h e) -> p h e", h=8),
                                                  in1=rn_b[:, nb * 8:(nb + 1) * 8, :], op=ALU.mult)),
                      reads=[f"ssn2{sfx}"], writes=[rdst], banks=bank_regions(bks[nb]))
        for (srcn, rsrc, dstq, rdq) in ((f"qn{sfx}", f"qn{sfx}", qr_i, rqr), (f"kn{sfx}", f"kn{sfx}", kr_i, rkr)):
            srct = qn if srcn.startswith("qn") else kn
            rv = srct[:].rearrange("p (h t e) -> p h t e", h=H, t=2)
            pb.op("v", ("tensor_scalar", dict(out=rot[:].rearrange("p (h t e) -> p h t e", h=H, t=2)[:, :, 0, :],
                                              in0=rv[:, :, 1, :], scalar1=-1.0, scalar2=None, op0=ALU.mult)),
                  reads=[rsrc], writes=[f"rot{sfx}"])
            pb.op("v", ("tensor_copy", dict(out=rot[:].rearrange("p (h t e) -> p h t e", h=H, t=2)[:, :, 1, :],
                                            in_=rv[:, :, 0, :])),
                  reads=[rsrc], writes=[f"rot{sfx}"])
            pb.op("p", ("tensor_tensor", dict(out=rot[:], in0=rot[:], in1=sinb[:], op=ALU.mult)),
                  reads=[f"rot{sfx}", "sinb"], writes=[f"rot{sfx}"])
            pb.op("v", ("tensor_tensor", dict(out=dstq[:], in0=srct[:], in1=cosb[:], op=ALU.mult)),
                  reads=[rsrc, "cosb"], writes=[rdq])
            pb.op("v", ("tensor_tensor", dict(out=dstq[:], in0=dstq[:], in1=rot[:], op=ALU.add)),
                  reads=[rdq, f"rot{sfx}"], writes=[rdq])
        for nb in range(2):
            pb.op("v", ("tensor_tensor", dict(out=tgt_i[:, nb * 512:(nb + 1) * 512],
                                              in0=bank[5 + nb][:, 0:512], in1=kr_i[:, nb * 512:(nb + 1) * 512],
                                              op=ALU.subtract)),
                  reads=[rkr], writes=[rtgt], banks=bank_regions(5 + nb))
        pb.dma("y", ("dma_start", dict(out=io["qtok_h"][sl, :], in_=qr_i[:])), reads=[rqr], writes=["h_qtok"])
        pb.op("s", ("copy", dict(out=qrb_i[:], in_=qr_i[:])), reads=[rqr], writes=[rqrb])
        pb.op("s", ("copy", dict(out=krb_i[:], in_=kr_i[:])), reads=[rkr], writes=[rkrb])
        pb.dma("p", ("dma_start", dict(out=io["ktok_h"][sl, :], in_=krb_i[:])), reads=[rkrb], writes=["h_ktok"])
        for (srcb, rsb, dstT, rdT, hb) in ((qrb_i, rqrb, qTt_i, rqT, "q"), (krb_i, rkrb, kTt_i, rkT, "k")):
            for h in range(16):
                tb = 0 if h < 8 else 5
                pb.op("t", ("transpose", dict(out=bank_bf[tb][0:64, (h % 8) * 128:(h % 8 + 1) * 128],
                                              in_=srcb[:, h * 64:(h + 1) * 64], identity=cm.ident_b[:])),
                      reads=[rsb, "cc_ident"], banks=bank_regions(tb))
            pb.op("s", ("copy", dict(out=dstT[:, 0:1024], in_=bank_bf[0][0:64, 0:1024])), writes=[rdT], banks=bank_regions(0))
            pb.op("s", ("copy", dict(out=dstT[:, 1024:2048], in_=bank_bf[5][0:64, 0:1024])), writes=[rdT], banks=bank_regions(5))
        pb.dma("y", ("dma_start", dict(out=io["qT_h"][i], in_=qTt_i[:])), reads=[rqT], writes=["h_qT"])
        pb.dma("y", ("dma_start", dict(out=io["kT_h"][i], in_=kTt_i[:])), reads=[rkT], writes=["h_kT"])

        for h in range(H):
            pb.op("v", ("reduce_sum", dict(out=ds[:, h:h + 1], in_=tgt_i[:, h * 64:(h + 1) * 64], axis=AX.X)),
                  reads=[rtgt], writes=[f"dsm{sfx}_{h}"])
        pb.op("v", ("tensor_scalar", dict(out=ds[:, 0:H], in0=ds[:, 0:H], scalar1=1.0 / 64.0, scalar2=None, op0=ALU.mult)),
              reads=[f"dsm{sfx}_{h}" for h in range(H)], writes=[f"ds_mu{sfx}"])
        mu_b = ds[:, 0:H].unsqueeze(2).broadcast_to([128, H, 64])
        pb.op("v", ("tensor_tensor", dict(out=tgt_i[:].rearrange("p (h e) -> p h e", h=H),
                                          in0=tgt_i[:].rearrange("p (h e) -> p h e", h=H), in1=mu_b, op=ALU.subtract)),
              reads=[rtgt, f"ds_mu{sfx}"], writes=[rtgt])
        for h in range(H):
            pb.op("v", ("scalar_tensor_tensor", dict(out=dump[:], in0=tgt_i[:, h * 64:(h + 1) * 64], scalar=1.0,
                                                     in1=tgt_i[:, h * 64:(h + 1) * 64], op0=ALU.mult, op1=ALU.mult,
                                                     accum_out=ds[:, H + h:H + h + 1])),
                  reads=[rtgt], writes=[f"dump{sfx}", f"dsv{sfx}_{h}"])
        pb.op("s", ("activation", dict(out=ds[:, H:2 * H], in_=ds[:, H:2 * H], func=AF.Sqrt, scale=1.0 / 63.0)),
              reads=[f"dsv{sfx}_{h}" for h in range(H)], writes=[f"ds_sd{sfx}"])
        pb.op("v", ("tensor_scalar", dict(out=ds[:, H:2 * H], in0=ds[:, H:2 * H], scalar1=1e-8, scalar2=None, op0=ALU.add)),
              reads=[f"ds_sd{sfx}"], writes=[f"ds_sd{sfx}"])
        pb.op("v", ("reciprocal", dict(out=ds[:, H:2 * H], in_=ds[:, H:2 * H])), reads=[f"ds_sd{sfx}"], writes=[f"ds_rd{sfx}"])
        rd_b = ds[:, H:2 * H].unsqueeze(2).broadcast_to([128, H, 64])
        pb.op("v", ("tensor_tensor", dict(out=tgt_i[:].rearrange("p (h e) -> p h e", h=H),
                                          in0=tgt_i[:].rearrange("p (h e) -> p h e", h=H), in1=rd_b, op=ALU.mult)),
              reads=[rtgt, f"ds_rd{sfx}"], writes=[rtgt])
        pb.op("p", ("tensor_tensor", dict(out=tgt_i[:], in0=tgt_i[:], in1=g4b[:], op=ALU.mult)),
              reads=[rtgt, "g4b"], writes=[rtgt])
        pb.op("v", ("tensor_tensor", dict(out=tgt_i[:], in0=tgt_i[:], in1=b4b[:], op=ALU.add)),
              reads=[rtgt, "b4b"], writes=[rtgt])
        pb.dma("p", ("dma_start", dict(out=io["tgt_h"][sl, :], in_=tgt_i[:])), reads=[rtgt], writes=["h_tgt"])


def build_l1b():
    nc = bass.Bass("TRN2", target_bir_lowering=False)
    io = dict(
        qT_h=nc.dram_tensor("qT_h", [NT, 64, 16 * 128], BF16, kind="ExternalInput").ap(),
        kT_h=nc.dram_tensor("kT_h", [NT, 64, 16 * 128], BF16, kind="ExternalInput").ap(),
        ktok_h=nc.dram_tensor("ktok_h", [L, D], BF16, kind="ExternalInput").ap(),
        qtok_h=nc.dram_tensor("qtok_h", [L, D], F32, kind="ExternalInput").ap(),
        tgt_h=nc.dram_tensor("tgt_h", [L, D], F32, kind="ExternalInput").ap(),
        eta_h=nc.dram_tensor("eta_h", [L, H], F32, kind="ExternalInput").ap(),
        w1=nc.dram_tensor("w1", [H, 64, 256], F32, kind="ExternalInput").ap(),
        b1=nc.dram_tensor("b1", [H, 1, 256], F32, kind="ExternalInput").ap(),
        w2=nc.dram_tensor("w2", [H, 256, 64], F32, kind="ExternalInput").ap(),
        b2=nc.dram_tensor("b2", [H, 1, 64], F32, kind="ExternalInput").ap(),
        tnw=nc.dram_tensor("tnw", [1, D], F32, kind="ExternalInput").ap(),
        tnb=nc.dram_tensor("tnb", [1, D], F32, kind="ExternalInput").ap(),
        y_h=nc.dram_tensor("y_h", [L, D], F32, kind="ExternalOutput").ap(),
    )
    ctx = ExitStack()
    with ctx:
        pb = PB(nc, ctx)
        cm = Common(pb, ctx)
        emit_l1b(pb, cm, ctx, io)
        pb.emit()
    return nc


def emit_l1b(pb, cm, ctx, io):
    nc = pb.nc
    sb = lambda name, shape, dt: ctx.enter_context(nc.sbuf_tensor(name, shape, dt))
    SQA = float(np.sqrt(GELU_A))
    SQC = float(np.sqrt(GELU_C))
    cm.const(TTT_LN_EPS)

    psA = ctx.enter_context(nc.psum_tensor("psA", [128, 2048], F32))
    psB = ctx.enter_context(nc.psum_tensor("psB", [128, 2048], F32))
    psA_bf = psA[:].bitcast(BF16)
    psB_bf = psB[:].bitcast(BF16)
    A = lambda *i: [f"bk{j}" for j in i]
    Bk = lambda *i: [f"bk{4 + j}" for j in i]
    ident_f = sb("ttt_ident_f", [128, 128], F32)
    pb.op("v", ("memset", dict(ap=ident_f[:], constant=1.0)), writes=["idf_t"])
    pb.op("p", ("affine_select", dict(out=ident_f[:], in_=ident_f[:], pattern=[[1, 128]],
                                      compare_op=ALU.is_equal, fill=0.0, base=0, channel_multiplier=-1)),
          reads=["idf_t"], writes=["idf"])

    slot = lambda h: 64 * (h // 8)
    pc = lambda h: h % 8
    HORD = [0, 8, 1, 9, 2, 10, 3, 11, 4, 12, 5, 13, 6, 14, 7, 15]
    tp_of = lambda h: None if h < 8 else (0, 64)

    W1f = sb("ttt_W1f", [128, 2048], F32)
    W1bP = [sb(f"ttt_W1b_{p}", [128, 2048], BF16) for p in range(2)]
    b1f = [sb(f"ttt_b1f{s}", [1, 2048], F32) for s in range(2)]
    b1bP = [sb(f"ttt_b1b_{p}", [2, 2048], BF16) for p in range(2)]
    W2bP = [sb(f"ttt_W2b_{p}", [128, 2048], BF16) for p in range(2)]
    W2Tf = sb("ttt_W2Tf", [128, 2048], F32)
    W2TbP = [sb(f"ttt_W2Tb_{p}", [128, 2048], BF16) for p in range(2)]
    b2f = [sb(f"ttt_b2f{s}", [1, 512], F32) for s in range(2)]
    b2bP = [sb(f"ttt_b2b_{p}", [2, 512], BF16) for p in range(2)]
    g4p = sb("ttt_g4p", [128, 512], F32)
    b4p = sb("ttt_b4p", [128, 512], F32)
    uptri_rep = sb("ttt_uptri", [128, 512], BF16)
    ind_s = [sb(f"ttt_ind{s}", [1, 128], BF16) for s in range(2)]
    ind2 = sb("ttt_ind2", [2, 128], BF16)
    b1s = sb("ttt_b1s", [1, 2048], BF16)
    b2s = sb("ttt_b2s", [1, 512], BF16)
    ind_f = [sb(f"ttt_indf{s}", [1, 128], F32) for s in range(2)]

    Z1f = sb("ttt_Z1f", [128, 2048], F32)
    X2b = sb("ttt_X2b", [128, 2048], BF16)
    X2T = sb("ttt_X2T", [128, 2048], BF16)
    tA = sb("ttt_tA", [128, 2048], F32)
    tB = sb("ttt_tB", [128, 2048], F32)
    gZ1n = sb("ttt_gZ1n", [128, 2048], BF16)
    Z2f = sb("ttt_Z2f", [128, 512], F32)
    gZ2n = sb("ttt_gZ2n", [128, 512], BF16)
    gZ2nT = sb("ttt_gZ2nT", [128, 512], BF16)
    Xbarb = sb("ttt_Xbarb", [128, 2048], BF16)
    XbarT = sb("ttt_XbarT", [128, 2048], BF16)
    Ct = sb("ttt_Ct", [128, 512], BF16)
    Dt = sb("ttt_Dt", [128, 512], BF16)
    Met = sb("ttt_Met", [128, 512], BF16)
    etkp = sb("ttt_etkp", [128, 512], BF16)
    etg2 = sb("ttt_etg2", [128, 512], BF16)
    etabp = sb("ttt_etabp", [128, 8], BF16)
    Zb2f = sb("ttt_Zb2f", [128, 512], F32)
    Z2s = sb("ttt_Z2s", [128, 512], F32)
    outp = sb("ttt_outp", [128, 512], F32)
    scr = sb("ttt_scr", [128, 512], F32)
    ds = sb("ttt_ds", [128, 64], F32)
    QTc = [sb(f"ttt_QTc{d}", [128, 512], BF16) for d in range(2)]
    KTc = [sb(f"ttt_KTc{d}", [128, 512], BF16) for d in range(2)]
    Ktokp = [sb(f"ttt_Ktokp{d}", [128, 512], BF16) for d in range(2)]
    Qtokp = [sb(f"ttt_Qtokp{d}", [128, 512], F32) for d in range(2)]
    TGTp = [sb(f"ttt_TGTp{d}", [128, 512], F32) for d in range(2)]
    etap = [sb(f"ttt_etap{d}", [128, 8], F32) for d in range(2)]
    ut1 = sb("ttt_ut1", [128, 64], BF16)

    for s in range(2):
        pb.op("v", ("memset", dict(ap=ind_s[s][:], constant=0.0)), writes=[f"ind{s}"])
        if s == 0:
            pass
        pb.op("v", ("memset", dict(ap=ind_s[s][0:1, s * 64:(s + 1) * 64], constant=1.0)), writes=[f"ind{s}"])
        pb.op("v", ("memset", dict(ap=ind_f[s][:], constant=0.0)), writes=[f"indf{s}"])
        pb.op("v", ("memset", dict(ap=ind_f[s][0:1, s * 64:(s + 1) * 64], constant=1.0)), writes=[f"indf{s}"])
    pb.op("v", ("tensor_copy", dict(out=ind2[0:1, :], in_=ind_s[0][:])), reads=["ind0"], writes=["ind2"])
    pb.dma("s", ("dma_start", dict(out=ind2[1:2, :], in_=ind_s[1][:])), reads=["ind1"], writes=["ind2"])
    pb.op("v", ("memset", dict(ap=ut1[:], constant=1.0)), writes=["ut1"])
    for s in range(2):
        pb.op("p", ("affine_select", dict(out=ut1[s * 64:(s + 1) * 64, :], in_=ut1[s * 64:(s + 1) * 64, :],
                                          pattern=[[1, 64]], compare_op=ALU.is_ge, fill=0.0, base=0,
                                          channel_multiplier=-1)),
              reads=["ut1"], writes=["ut1"])
    pb.op("v", ("tensor_copy", dict(out=uptri_rep[:].rearrange("p (g e) -> p g e", g=8),
                                    in_=ut1[:].unsqueeze(1).broadcast_to([128, 8, 64]))),
          reads=["ut1"], writes=["uptri"])
    for h in range(H):
        pb.dma("y", ("dma_start", dict(out=W1f[slot(h):slot(h) + 64, pc(h) * 256:(pc(h) + 1) * 256], in_=io["w1"][h])),
               reads=["h_w1"], writes=["W1f"])
        for cc in range(2):
            pb.dma("s", ("dma_start", dict(out=Z1f[:, 0:64] if False else tA[:, cc * 64:(cc + 1) * 64],
                                           in_=io["w2"][h][cc * 128:(cc + 1) * 128, :])),
                   reads=["h_w2"], writes=["tA"])
        for cc in range(2):
            pb.op("t", ("transpose", dict(out=psA[0:64, 0:128], in_=tA[:, cc * 64:(cc + 1) * 64],
                                          identity=ident_f[:])),
                  reads=["tA", "idf"], banks=A(0))
            pb.op("v", ("tensor_copy", dict(out=W2Tf[slot(h):slot(h) + 64, pc(h) * 256 + cc * 128:pc(h) * 256 + (cc + 1) * 128],
                                            in_=psA[0:64, 0:128])),
                  writes=["W2Tf"], banks=A(0))
    b1r = io["b1"].rearrange("h a f -> a (h f)")
    b2r = io["b2"].rearrange("h a f -> a (h f)")
    pb.dma("p", ("dma_start", dict(out=b1f[0][:], in_=b1r[0:1, 0:2048])), reads=["h_b1"], writes=["b1f0"])
    pb.dma("p", ("dma_start", dict(out=b1f[1][:], in_=b1r[0:1, 2048:4096])), reads=["h_b1"], writes=["b1f1"])
    pb.dma("p", ("dma_start", dict(out=b2f[0][:], in_=b2r[0:1, 0:512])), reads=["h_b2"], writes=["b2f0"])
    pb.dma("p", ("dma_start", dict(out=b2f[1][:], in_=b2r[0:1, 512:1024])), reads=["h_b2"], writes=["b2f1"])
    pb.op("s", ("copy", dict(out=W1bP[0][:], in_=W1f[:])), reads=["W1f"], writes=["W1b_0"])
    pb.op("s", ("copy", dict(out=W2TbP[0][:], in_=W2Tf[:])), reads=["W2Tf"], writes=["W2Tb_0"])
    for s in range(2):
        if s == 0:
            pb.op("v", ("tensor_copy", dict(out=b1bP[0][0:1, :], in_=b1f[0][:])), reads=["b1f0"], writes=["b1b_0"])
            pb.op("v", ("tensor_copy", dict(out=b2bP[0][0:1, :], in_=b2f[0][:])), reads=["b2f0"], writes=["b2b_0"])
        else:
            pb.op("v", ("tensor_copy", dict(out=b1s[:], in_=b1f[1][:])), reads=["b1f1"], writes=["b1s"])
            pb.op("v", ("tensor_copy", dict(out=b2s[:], in_=b2f[1][:])), reads=["b2f1"], writes=["b2s"])
            pb.dma("s", ("dma_start", dict(out=b1bP[0][1:2, :], in_=b1s[:])), reads=["b1s"], writes=["b1b_0"])
            pb.dma("s", ("dma_start", dict(out=b2bP[0][1:2, :], in_=b2s[:])), reads=["b2s"], writes=["b2b_0"])
    grow = sb("ttt_grow", [1, 2048], F32)
    pb.dma("y", ("dma_start", dict(out=grow[0:1, 0:1024], in_=io["tnw"])), reads=["h_tnw"], writes=["grow"])
    pb.dma("y", ("dma_start", dict(out=grow[0:1, 1024:2048], in_=io["tnb"])), reads=["h_tnb"], writes=["grow"])
    for s in range(2):
        for (coff, dst, rdst) in ((0, g4p, "g4p"), (1024, b4p, "b4p")):
            pb.op("t", ("matmul", dict(out=psA[:, 0:512], lhsT=ind_f[s][:],
                                       rhs=grow[0:1, coff + s * 512:coff + (s + 1) * 512], start=True, stop=True)),
                  reads=["grow", f"indf{s}"], banks=A(0))
            pb.op("v", ("tensor_copy", dict(out=dst[s * 64:(s + 1) * 64, :], in_=psA[s * 64:(s + 1) * 64, 0:512])),
                  writes=[rdst], banks=A(0))

    seg = lambda t_: t_[:].rearrange("p (g e) -> p g e", e=64)

    def w2b_from_w2t(p):
        for t_ in range(32):
            h, cc = t_ // 2, t_ % 2
            pb.op("t", ("transpose", dict(out=psB_bf[:, t_ * 64:(t_ + 1) * 64],
                                          in_=W2TbP[p][slot(h):slot(h) + 64, pc(h) * 256 + cc * 128:pc(h) * 256 + (cc + 1) * 128],
                                          identity=cm.ident_b[slot(h):slot(h) + 64, slot(h):slot(h) + 64])),
                  reads=[f"W2Tb_{p}", "cc_ident"], banks=Bk(t_ // 8))
        pb.op("s", ("copy", dict(out=W2bP[p][:], in_=psB_bf[:, 0:2048])), writes=[f"W2b_{p}"], banks=Bk(0, 1, 2, 3))

    w2b_from_w2t(0)

    NCH = DBG.get("ttt_nc", NC)
    X2TP = [sb(f"ttt_X2T_{p}", [128, 2048], BF16) for p in range(2)]
    gZ1nP = [sb(f"ttt_gZ1n_{p}", [128, 2048], BF16) for p in range(2)]
    gZ2nP = [sb(f"ttt_gZ2n_{p}", [128, 512], BF16) for p in range(2)]
    psA_Z2 = psA[:, 1024:1536]
    psA_g2T = psA_bf[:, 3072:3584]

    def stage_a1a(c):
        d = c % 2
        cur, nxt = c % 2, (c + 1) % 2
        pp = c % 2
        i, half = c // 2, c % 2
        W1b = W1bP[cur]; rW1b = f"W1b_{cur}"
        W2Tb = W2TbP[cur]; rW2Tb = f"W2Tb_{cur}"
        W2b = W2bP[cur]; rW2b = f"W2b_{cur}"
        b1b = b1bP[cur]; b2b = b2bP[cur]
        rb1b = f"b1b_{cur}"
        rb2b = f"b2b_{cur}"
        X2T = X2TP[pp]; rX2T = f"X2T_{pp}"
        gZ1n = gZ1nP[pp]; rgZ1n = f"gZ1n_{pp}"
        gZ2n = gZ2nP[pp]; rgZ2n = f"gZ2n_{pp}"
        QT, KT, Kt, Qt, TG, ET = QTc[d], KTc[d], Ktokp[d], Qtokp[d], TGTp[d], etap[d]
        rQT, rKT, rKt, rQt, rTG, rET = (f"QTc{d}", f"KTc{d}", f"Ktokp{d}", f"Qtokp{d}", f"TGTp{d}", f"etap{d}")
        rows = slice(c * 64, (c + 1) * 64)
        eta_b64 = ET[:].unsqueeze(2).broadcast_to([128, 8, 64])
        for s in range(2):
            pb.dma("y", ("dma_start", dict(
                out=QT[s * 64:(s + 1) * 64, :],
                in_=io["qT_h"][i].rearrange("d (h t) -> d h t", h=H)[:, s * 8:(s + 1) * 8, half * 64:(half + 1) * 64])),
                reads=["h_qT"], writes=[rQT])
            pb.dma("y", ("dma_start", dict(
                out=KT[s * 64:(s + 1) * 64, :],
                in_=io["kT_h"][i].rearrange("d (h t) -> d h t", h=H)[:, s * 8:(s + 1) * 8, half * 64:(half + 1) * 64])),
                reads=["h_kT"], writes=[rKT])
            pb.dma("s", ("dma_start", dict(out=Kt[s * 64:(s + 1) * 64, :], in_=io["ktok_h"][rows, s * 512:(s + 1) * 512])),
                   reads=["h_ktok"], writes=[rKt])
            pb.dma("s", ("dma_start", dict(out=Qt[s * 64:(s + 1) * 64, :], in_=io["qtok_h"][rows, s * 512:(s + 1) * 512])),
                   reads=["h_qtok"], writes=[rQt])
            pb.dma("p", ("dma_start", dict(out=TG[s * 64:(s + 1) * 64, :], in_=io["tgt_h"][rows, s * 512:(s + 1) * 512])),
                   reads=["h_tgt"], writes=[rTG])
            pb.dma("p", ("dma_start", dict(out=ET[s * 64:(s + 1) * 64, :], in_=io["eta_h"][rows, s * 8:(s + 1) * 8])),
                   reads=["h_eta"], writes=[rET])

        for h in HORD:
            pb.op("t", ("matmul", dict(skip_group_check=True,
                                       out=psA[slot(h):slot(h) + 64, pc(h) * 256:(pc(h) + 1) * 256],
                                       lhsT=KT[slot(h):slot(h) + 64, pc(h) * 64:(pc(h) + 1) * 64],
                                       rhs=W1b[slot(h):slot(h) + 64, pc(h) * 256:(pc(h) + 1) * 256],
                                       start=(pc(h) % 2 == 0), stop=False)),
                  reads=[rKT, rW1b], banks=A(pc(h) // 2))
        for bk in range(4):
            pb.op("t", ("matmul", dict(skip_group_check=True, out=psA[:, bk * 512:(bk + 1) * 512], lhsT=ind2[:],
                                       rhs=b1b[0:2, bk * 512:(bk + 1) * 512], start=False, stop=True)),
                  reads=[rb1b, "ind2"], banks=A(bk))
        pb.op("s", ("copy", dict(out=Z1f[:], in_=psA[:, 0:2048])), writes=["Z1f"], banks=A(0, 1, 2, 3))

        pb.op("s", ("activation", dict(out=X2b[:], in_=Z1f[:], func=AF.Gelu_apprx_tanh, scale=1.0)),
              reads=["Z1f"], writes=["X2b"])
        pb.op("s", ("activation", dict(out=tA[:], in_=Z1f[:], func=AF.Square, scale=SQA)),
              reads=["Z1f"], writes=["tA"])
        pb.op("v", ("scalar_tensor_tensor", dict(out=tB[:], in0=tA[:], scalar=1.0, in1=Z1f[:],
                                                 op0=ALU.add, op1=ALU.mult)),
              reads=["tA", "Z1f"], writes=["tB"])
        pb.op("s", ("activation", dict(out=tB[:], in_=tB[:], func=AF.Tanh, scale=GELU_C)),
              reads=["tB"], writes=["tB"])
        pb.op("p", ("tensor_scalar", dict(out=tA[:], in0=tA[:], scalar1=3.0, scalar2=1.0, op0=ALU.mult, op1=ALU.add)),
              reads=["tA"], writes=["tA"])
        pb.op("v", ("tensor_tensor", dict(out=tA[:], in0=tA[:], in1=Z1f[:], op=ALU.mult)),
              reads=["tA", "Z1f"], writes=["tA"])
        pb.op("s", ("activation", dict(out=Z1f[:], in_=tB[:], func=AF.Square, scale=SQC)),
              reads=["tB"], writes=["Z1f"])
        pb.op("p", ("tensor_scalar", dict(out=Z1f[:], in0=Z1f[:], scalar1=-1.0, scalar2=GELU_C, op0=ALU.mult, op1=ALU.add)),
              reads=["Z1f"], writes=["Z1f"])
        pb.op("v", ("tensor_tensor", dict(out=tA[:], in0=tA[:], in1=Z1f[:], op=ALU.mult)),
              reads=["tA", "Z1f"], writes=["tA"])
        pb.op("v", ("tensor_tensor", dict(out=tA[:], in0=tA[:], in1=tB[:], op=ALU.add)),
              reads=["tA", "tB"], writes=["tA"])

        for t_ in range(32):
            h, cc = t_ // 2, t_ % 2
            pb.op("t", ("transpose", dict(out=psA_bf[:, t_ * 64:(t_ + 1) * 64],
                                          in_=X2b[slot(h):slot(h) + 64, pc(h) * 256 + cc * 128:pc(h) * 256 + (cc + 1) * 128],
                                          identity=cm.ident_b[slot(h):slot(h) + 64, slot(h):slot(h) + 64])),
                  reads=["X2b", "cc_ident"], banks=A(t_ // 16))
        pb.op("s", ("copy", dict(out=X2T[:], in_=psA_bf[:, 0:2048])), writes=[rX2T], banks=A(0, 1))

        for h in HORD:
            for cc in range(2):
                pb.op("t", ("matmul", dict(skip_group_check=True,
                                           out=psA_Z2[slot(h):slot(h) + 64, (h % 8) * 64:(h % 8 + 1) * 64],
                                           lhsT=X2T[:, (2 * h + cc) * 64:(2 * h + cc + 1) * 64],
                                           rhs=W2b[:, (2 * h + cc) * 64:(2 * h + cc + 1) * 64],
                                           start=(h in (0, 8) and cc == 0), stop=False, tile_position=tp_of(h))),
                      reads=[rX2T, rW2b], banks=A(2))
        pb.op("t", ("matmul", dict(skip_group_check=True, out=psA_Z2[:, 0:512], lhsT=ind2[:],
                                   rhs=b2b[0:2, :], start=False, stop=True)),
              reads=[rb2b, "ind2"], banks=A(2))
        pb.op("s", ("copy", dict(out=Z2f[:], in_=psA_Z2[:, 0:512])), writes=["Z2f"], banks=A(2))

        pb.op("v", ("reduce_sum", dict(out=ds[:, 0:8].unsqueeze(2), in_=seg(Z2f), axis=AX.X)),
              reads=["Z2f"], writes=["ds_mu"])
        pb.op("v", ("tensor_scalar", dict(out=ds[:, 0:8], in0=ds[:, 0:8], scalar1=1.0 / 64.0, scalar2=None, op0=ALU.mult)),
              reads=["ds_mu"], writes=["ds_mu"])
        pb.op("v", ("tensor_tensor", dict(out=seg(Z2f), in0=seg(Z2f),
                                          in1=ds[:, 0:8].unsqueeze(2).broadcast_to([128, 8, 64]), op=ALU.subtract)),
              reads=["Z2f", "ds_mu"], writes=["Z2f"])
        pb.op("s", ("activation", dict(out=scr[:], in_=Z2f[:], func=AF.Square, scale=1.0)),
              reads=["Z2f"], writes=["scr"])
        pb.op("v", ("reduce_sum", dict(out=ds[:, 8:16].unsqueeze(2), in_=seg(scr), axis=AX.X)),
              reads=["scr"], writes=["ds_var"])
        pb.op("s", ("activation", dict(out=ds[:, 8:16], in_=ds[:, 8:16], func=AF.Sqrt, bias=cm.const(TTT_LN_EPS),
                                       scale=1.0 / 64.0)),
              reads=["ds_var", cm.const_region(TTT_LN_EPS)], writes=["ds_sd"])
        pb.op("v", ("reciprocal", dict(out=ds[:, 8:16], in_=ds[:, 8:16])), reads=["ds_sd"], writes=["ds_rstd"])
        pb.op("v", ("tensor_tensor", dict(out=seg(Z2f), in0=seg(Z2f),
                                          in1=ds[:, 8:16].unsqueeze(2).broadcast_to([128, 8, 64]), op=ALU.mult)),
              reads=["Z2f", "ds_rstd"], writes=["Z2f"])
        pb.op("v", ("tensor_tensor", dict(out=scr[:], in0=Z2f[:], in1=g4p[:], op=ALU.mult)),
              reads=["Z2f", "g4p"], writes=["scr"])
        pb.op("p", ("tensor_tensor", dict(out=scr[:], in0=scr[:], in1=b4p[:], op=ALU.add)),
              reads=["scr", "b4p"], writes=["scr"])
        pb.op("v", ("tensor_tensor", dict(out=scr[:], in0=scr[:], in1=TG[:], op=ALU.subtract)),
              reads=["scr", rTG], writes=["scr"])
        pb.op("v", ("tensor_tensor", dict(out=scr[:], in0=scr[:], in1=g4p[:], op=ALU.mult)),
              reads=["scr", "g4p"], writes=["scr"])
        pb.op("v", ("reduce_sum", dict(out=ds[:, 16:24].unsqueeze(2), in_=seg(scr), axis=AX.X)),
              reads=["scr"], writes=["ds_s1"])
        pb.op("v", ("tensor_tensor", dict(out=Z2s[:], in0=scr[:], in1=Z2f[:], op=ALU.mult)),
              reads=["scr", "Z2f"], writes=["Z2s"])
        pb.op("v", ("reduce_sum", dict(out=ds[:, 24:32].unsqueeze(2), in_=seg(Z2s), axis=AX.X)),
              reads=["Z2s"], writes=["ds_s2"])
        pb.op("v", ("scalar_tensor_tensor", dict(out=seg(scr), in0=seg(scr), scalar=-64.0,
                                                 in1=ds[:, 16:24].unsqueeze(2).broadcast_to([128, 8, 64]),
                                                 op0=ALU.mult, op1=ALU.add)),
              reads=["scr", "ds_s1"], writes=["scr"])
        pb.op("v", ("tensor_tensor", dict(out=seg(Z2s), in0=seg(Z2f),
                                          in1=ds[:, 24:32].unsqueeze(2).broadcast_to([128, 8, 64]), op=ALU.mult)),
              reads=["Z2f", "ds_s2"], writes=["Z2s"])
        pb.op("v", ("tensor_tensor", dict(out=scr[:], in0=scr[:], in1=Z2s[:], op=ALU.add)),
              reads=["scr", "Z2s"], writes=["scr"])
        pb.op("v", ("tensor_scalar", dict(out=ds[:, 32:40], in0=ds[:, 8:16], scalar1=1.0 / 64.0, scalar2=None, op0=ALU.mult)),
              reads=["ds_rstd"], writes=["ds_r64"])
        pb.op("v", ("tensor_tensor", dict(out=gZ2n[:].rearrange("p (g e) -> p g e", e=64), in0=seg(scr),
                                          in1=ds[:, 32:40].unsqueeze(2).broadcast_to([128, 8, 64]), op=ALU.mult)),
              reads=["scr", "ds_r64"], writes=[rgZ2n])

    def stage_a1b(c):
        d = c % 2
        cur, nxt = c % 2, (c + 1) % 2
        pp = c % 2
        i, half = c // 2, c % 2
        W1b = W1bP[cur]; rW1b = f"W1b_{cur}"
        W2Tb = W2TbP[cur]; rW2Tb = f"W2Tb_{cur}"
        W2b = W2bP[cur]; rW2b = f"W2b_{cur}"
        b1b = b1bP[cur]; b2b = b2bP[cur]
        rb1b = f"b1b_{cur}"
        rb2b = f"b2b_{cur}"
        X2T = X2TP[pp]; rX2T = f"X2T_{pp}"
        gZ1n = gZ1nP[pp]; rgZ1n = f"gZ1n_{pp}"
        gZ2n = gZ2nP[pp]; rgZ2n = f"gZ2n_{pp}"
        QT, KT, Kt, Qt, TG, ET = QTc[d], KTc[d], Ktokp[d], Qtokp[d], TGTp[d], etap[d]
        rQT, rKT, rKt, rQt, rTG, rET = (f"QTc{d}", f"KTc{d}", f"Ktokp{d}", f"Qtokp{d}", f"TGTp{d}", f"etap{d}")
        rows = slice(c * 64, (c + 1) * 64)
        eta_b64 = ET[:].unsqueeze(2).broadcast_to([128, 8, 64])
        for h in HORD:
            pb.op("t", ("transpose", dict(out=psA_g2T[slot(h):slot(h) + 64, pc(h) * 64:(pc(h) + 1) * 64],
                                          in_=gZ2n[slot(h):slot(h) + 64, pc(h) * 64:(pc(h) + 1) * 64],
                                          identity=cm.ident_b[slot(h):slot(h) + 64, slot(h):slot(h) + 64])),
                  reads=[rgZ2n, "cc_ident"], banks=A(3))
        pb.op("s", ("copy", dict(out=gZ2nT[:], in_=psA_g2T)), writes=["gZ2nT"], banks=A(3))

        for h in HORD:
            pb.op("t", ("matmul", dict(skip_group_check=True,
                                       out=psA[slot(h):slot(h) + 64, pc(h) * 256:(pc(h) + 1) * 256],
                                       lhsT=gZ2nT[slot(h):slot(h) + 64, pc(h) * 64:(pc(h) + 1) * 64],
                                       rhs=W2Tb[slot(h):slot(h) + 64, pc(h) * 256:(pc(h) + 1) * 256],
                                       start=(pc(h) % 2 == 0), stop=(pc(h) % 2 == 1))),
                  reads=["gZ2nT", rW2Tb], banks=A(pc(h) // 2))
        pb.op("s", ("activation", dict(out=tB[:], in_=psA[:, 0:2048], func=AF.Copy, scale=0.5)),
              writes=["tB"], banks=A(0, 1, 2, 3))
        pb.op("v", ("scalar_tensor_tensor", dict(out=gZ1n[:], in0=tA[:], scalar=1.0, in1=tB[:],
                                                 op0=ALU.add, op1=ALU.mult)),
              reads=["tA", "tB"], writes=[rgZ1n])

        pb.op("v", ("tensor_tensor", dict(out=etkp[:].rearrange("p (g e) -> p g e", e=64),
                                          in0=Kt[:].rearrange("p (g e) -> p g e", e=64),
                                          in1=eta_b64, op=ALU.mult)),
              reads=[rKt, rET], writes=["etkp"])
        pb.op("v", ("tensor_tensor", dict(out=etg2[:].rearrange("p (g e) -> p g e", e=64),
                                          in0=gZ2n[:].rearrange("p (g e) -> p g e", e=64),
                                          in1=eta_b64, op=ALU.mult)),
              reads=[rgZ2n, rET], writes=["etg2"])
        pb.op("v", ("tensor_copy", dict(out=etabp[:], in_=ET[:])), reads=[rET], writes=["etabp"])
        for h in HORD:
            pb.op("t", ("matmul", dict(skip_group_check=True,
                                       out=psA[slot(h):slot(h) + 64, pc(h) * 256:(pc(h) + 1) * 256],
                                       lhsT=etkp[slot(h):slot(h) + 64, pc(h) * 64:(pc(h) + 1) * 64],
                                       rhs=gZ1n[slot(h):slot(h) + 64, pc(h) * 256:(pc(h) + 1) * 256],
                                       start=(pc(h) % 2 == 0), stop=(pc(h) % 2 == 1))),
                  reads=["etkp", rgZ1n], banks=A(pc(h) // 2))
        pb.op("v", ("tensor_tensor", dict(out=W1f[:], in0=W1f[:], in1=psA[:, 0:2048], op=ALU.add)),
              reads=["W1f"], writes=["W1f"], banks=A(0, 1, 2, 3))
        for h in HORD:
            pb.op("t", ("matmul", dict(skip_group_check=True,
                                       out=psA[slot(h):slot(h) + 1, pc(h) * 256:(pc(h) + 1) * 256],
                                       lhsT=etabp[slot(h):slot(h) + 64, pc(h):pc(h) + 1],
                                       rhs=gZ1n[slot(h):slot(h) + 64, pc(h) * 256:(pc(h) + 1) * 256],
                                       start=True, stop=True)),
                  reads=["etabp", rgZ1n], banks=A(pc(h) // 2))
        for s in range(2):
            pb.op("v", ("tensor_tensor", dict(out=b1f[s][:], in0=b1f[s][:], in1=psA[s * 64:s * 64 + 1, 0:2048], op=ALU.add)),
                  reads=[f"b1f{s}"], writes=[f"b1f{s}"], banks=A(0, 1, 2, 3))
        for h in HORD:
            pb.op("t", ("matmul", dict(skip_group_check=True,
                                       out=psA[slot(h):slot(h) + 64, pc(h) * 256:(pc(h) + 1) * 256],
                                       lhsT=etg2[slot(h):slot(h) + 64, pc(h) * 64:(pc(h) + 1) * 64],
                                       rhs=X2b[slot(h):slot(h) + 64, pc(h) * 256:(pc(h) + 1) * 256],
                                       start=(pc(h) % 2 == 0), stop=(pc(h) % 2 == 1))),
                  reads=["etg2", "X2b"], banks=A(pc(h) // 2))
        pb.op("v", ("tensor_tensor", dict(out=W2Tf[:], in0=W2Tf[:], in1=psA[:, 0:2048], op=ALU.add)),
              reads=["W2Tf"], writes=["W2Tf"], banks=A(0, 1, 2, 3))
        for h in HORD:
            pb.op("t", ("matmul", dict(skip_group_check=True,
                                       out=psA[slot(h):slot(h) + 1, pc(h) * 64:(pc(h) + 1) * 64],
                                       lhsT=etabp[slot(h):slot(h) + 64, pc(h):pc(h) + 1],
                                       rhs=gZ2n[slot(h):slot(h) + 64, pc(h) * 64:(pc(h) + 1) * 64],
                                       start=True, stop=True)),
                  reads=["etabp", rgZ2n], banks=A(0))
        for s in range(2):
            pb.op("v", ("tensor_tensor", dict(out=b2f[s][:], in0=b2f[s][:], in1=psA[s * 64:s * 64 + 1, 0:512], op=ALU.add)),
                  reads=[f"b2f{s}"], writes=[f"b2f{s}"], banks=A(0))

    def stage_b(c):
        d = c % 2
        cur = c % 2
        pp = c % 2
        W1b = W1bP[cur]; rW1b = f"W1b_{cur}"
        W2b = W2bP[cur]; rW2b = f"W2b_{cur}"
        b1b = b1bP[cur]; b2b = b2bP[cur]
        rb1b = f"b1b_{cur}"
        rb2b = f"b2b_{cur}"
        X2T = X2TP[pp]; rX2T = f"X2T_{pp}"
        gZ1n = gZ1nP[pp]; rgZ1n = f"gZ1n_{pp}"
        gZ2n = gZ2nP[pp]; rgZ2n = f"gZ2n_{pp}"
        QT, KT, Qt, ET = QTc[d], KTc[d], Qtokp[d], etap[d]
        rQT, rKT, rQt, rET = f"QTc{d}", f"KTc{d}", f"Qtokp{d}", f"etap{d}"
        rows = slice(c * 64, (c + 1) * 64)
        eta_b64 = ET[:].unsqueeze(2).broadcast_to([128, 8, 64])

        pb.op("v", ("tensor_tensor", dict(out=Met[:].rearrange("p (g e) -> p g e", e=64),
                                          in0=uptri_rep[:].rearrange("p (g e) -> p g e", e=64),
                                          in1=eta_b64, op=ALU.mult)),
              reads=["uptri", rET], writes=["Met"])
        for h in HORD:
            pb.op("t", ("transpose" if False else "matmul", dict(skip_group_check=True,
                                       out=psB[slot(h):slot(h) + 64, pc(h) * 64:(pc(h) + 1) * 64],
                                       lhsT=KT[slot(h):slot(h) + 64, pc(h) * 64:(pc(h) + 1) * 64],
                                       rhs=QT[slot(h):slot(h) + 64, pc(h) * 64:(pc(h) + 1) * 64],
                                       start=True, stop=True)),
                  reads=[rKT, rQT], banks=Bk(0))
        pb.op("v", ("scalar_tensor_tensor", dict(out=Ct[:], in0=psB[:, 0:512], scalar=1.0, in1=Met[:],
                                                 op0=ALU.add, op1=ALU.mult)),
              reads=["Met"], writes=["Ct"], banks=Bk(0))

        for h in HORD:
            pb.op("t", ("matmul", dict(skip_group_check=True,
                                       out=psB[slot(h):slot(h) + 64, pc(h) * 256:(pc(h) + 1) * 256],
                                       lhsT=QT[slot(h):slot(h) + 64, pc(h) * 64:(pc(h) + 1) * 64],
                                       rhs=W1b[slot(h):slot(h) + 64, pc(h) * 256:(pc(h) + 1) * 256],
                                       start=(pc(h) % 2 == 0), stop=False)),
                  reads=[rQT, rW1b], banks=Bk(pc(h) // 2))
            pb.op("t", ("matmul", dict(skip_group_check=True,
                                       out=psB[slot(h):slot(h) + 64, pc(h) * 256:(pc(h) + 1) * 256],
                                       lhsT=Ct[slot(h):slot(h) + 64, pc(h) * 64:(pc(h) + 1) * 64],
                                       rhs=gZ1n[slot(h):slot(h) + 64, pc(h) * 256:(pc(h) + 1) * 256],
                                       start=False, stop=False)),
                  reads=["Ct", rgZ1n], banks=Bk(pc(h) // 2))
        for bk in range(4):
            pb.op("t", ("matmul", dict(skip_group_check=True, out=psB[:, bk * 512:(bk + 1) * 512], lhsT=ind2[:],
                                       rhs=b1b[0:2, bk * 512:(bk + 1) * 512], start=False, stop=True)),
                  reads=[rb1b, "ind2"], banks=Bk(bk))
        pb.op("s", ("activation", dict(out=Xbarb[:], in_=psB[:, 0:2048], func=AF.Gelu_apprx_tanh, scale=1.0)),
              writes=["Xbarb"], banks=Bk(0, 1, 2, 3))
        for t_ in range(32):
            h, cc = t_ // 2, t_ % 2
            pb.op("t", ("transpose", dict(out=psB_bf[:, t_ * 64:(t_ + 1) * 64],
                                          in_=Xbarb[slot(h):slot(h) + 64, pc(h) * 256 + cc * 128:pc(h) * 256 + (cc + 1) * 128],
                                          identity=cm.ident_b[slot(h):slot(h) + 64, slot(h):slot(h) + 64])),
                  reads=["Xbarb", "cc_ident"], banks=Bk(t_ // 16))
        pb.op("s", ("copy", dict(out=XbarT[:], in_=psB_bf[:, 0:2048])), writes=["XbarT"], banks=Bk(0, 1))

        for h in HORD:
            for cc in range(2):
                pb.op("t", ("matmul", dict(skip_group_check=True,
                                           out=psB[slot(h):slot(h) + 64, 1024 + (h % 8) * 64:1024 + (h % 8 + 1) * 64],
                                           lhsT=X2T[:, (2 * h + cc) * 64:(2 * h + cc + 1) * 64],
                                           rhs=XbarT[:, (2 * h + cc) * 64:(2 * h + cc + 1) * 64],
                                           start=(h in (0, 8) and cc == 0), stop=(h in (7, 15) and cc == 1),
                                           tile_position=tp_of(h))),
                      reads=[rX2T, "XbarT"], banks=Bk(2))
        pb.op("v", ("scalar_tensor_tensor", dict(out=Dt[:], in0=psB[:, 1024:1536], scalar=1.0, in1=Met[:],
                                                 op0=ALU.add, op1=ALU.mult)),
              reads=["Met"], writes=["Dt"], banks=Bk(2))

        for h in HORD:
            for cc in range(2):
                pb.op("t", ("matmul", dict(skip_group_check=True,
                                           out=psB[slot(h):slot(h) + 64, 1536 + (h % 8) * 64:1536 + (h % 8 + 1) * 64],
                                           lhsT=XbarT[:, (2 * h + cc) * 64:(2 * h + cc + 1) * 64],
                                           rhs=W2b[:, (2 * h + cc) * 64:(2 * h + cc + 1) * 64],
                                           start=(h in (0, 8) and cc == 0), stop=False, tile_position=tp_of(h))),
                      reads=["XbarT", rW2b], banks=Bk(3))
            pb.op("t", ("matmul", dict(skip_group_check=True,
                                       out=psB[slot(h):slot(h) + 64, 1536 + (h % 8) * 64:1536 + (h % 8 + 1) * 64],
                                       lhsT=Dt[slot(h):slot(h) + 64, pc(h) * 64:(pc(h) + 1) * 64],
                                       rhs=gZ2n[slot(h):slot(h) + 64, pc(h) * 64:(pc(h) + 1) * 64],
                                       start=False, stop=False)),
                  reads=["Dt", rgZ2n], banks=Bk(3))
        pb.op("t", ("matmul", dict(skip_group_check=True, out=psB[:, 1536:2048], lhsT=ind2[:],
                                   rhs=b2b[0:2, :], start=False, stop=True)),
              reads=[rb2b, "ind2"], banks=Bk(3))
        pb.op("s", ("copy", dict(out=Zb2f[:], in_=psB[:, 1536:2048])), writes=["Zb2f"], banks=Bk(3))

        pb.op("v", ("reduce_sum", dict(out=ds[:, 40:48].unsqueeze(2), in_=seg(Zb2f), axis=AX.X)),
              reads=["Zb2f"], writes=["ds_mu2"])
        pb.op("v", ("tensor_scalar", dict(out=ds[:, 40:48], in0=ds[:, 40:48], scalar1=1.0 / 64.0, scalar2=None, op0=ALU.mult)),
              reads=["ds_mu2"], writes=["ds_mu2"])
        pb.op("v", ("tensor_tensor", dict(out=seg(Zb2f), in0=seg(Zb2f),
                                          in1=ds[:, 40:48].unsqueeze(2).broadcast_to([128, 8, 64]), op=ALU.subtract)),
              reads=["Zb2f", "ds_mu2"], writes=["Zb2f"])
        pb.op("s", ("activation", dict(out=outp[:], in_=Zb2f[:], func=AF.Square, scale=1.0)),
              reads=["Zb2f"], writes=["outp"])
        pb.op("v", ("reduce_sum", dict(out=ds[:, 48:56].unsqueeze(2), in_=seg(outp), axis=AX.X)),
              reads=["outp"], writes=["ds_var2"])
        pb.op("s", ("activation", dict(out=ds[:, 48:56], in_=ds[:, 48:56], func=AF.Sqrt, bias=cm.const(TTT_LN_EPS),
                                       scale=1.0 / 64.0)),
              reads=["ds_var2", cm.const_region(TTT_LN_EPS)], writes=["ds_sd2"])
        pb.op("v", ("reciprocal", dict(out=ds[:, 48:56], in_=ds[:, 48:56])), reads=["ds_sd2"], writes=["ds_rstd2"])
        pb.op("v", ("tensor_tensor", dict(out=seg(Zb2f), in0=seg(Zb2f),
                                          in1=ds[:, 48:56].unsqueeze(2).broadcast_to([128, 8, 64]), op=ALU.mult)),
              reads=["Zb2f", "ds_rstd2"], writes=["Zb2f"])
        pb.op("v", ("tensor_tensor", dict(out=outp[:], in0=Zb2f[:], in1=g4p[:], op=ALU.mult)),
              reads=["Zb2f", "g4p"], writes=["outp"])
        pb.op("p", ("tensor_tensor", dict(out=outp[:], in0=outp[:], in1=b4p[:], op=ALU.add)),
              reads=["outp", "b4p"], writes=["outp"])
        pb.op("v", ("tensor_tensor", dict(out=outp[:], in0=outp[:], in1=Qt[:], op=ALU.add)),
              reads=["outp", rQt], writes=["outp"])
        for s in range(2):
            pb.dma("y", ("dma_start", dict(out=io["y_h"][rows, s * 512:(s + 1) * 512], in_=outp[s * 64:(s + 1) * 64, :])),
                   reads=["outp"], writes=["h_y"])

    def stage_a2(c):
        nxt = (c + 1) % 2
        pb.op("s", ("copy", dict(out=W1bP[nxt][:], in_=W1f[:])), reads=["W1f"], writes=[f"W1b_{nxt}"])
        pb.op("v", ("tensor_copy", dict(out=b1bP[nxt][0:1, :], in_=b1f[0][:])), reads=["b1f0"], writes=[f"b1b_{nxt}"])
        pb.op("v", ("tensor_copy", dict(out=b1s[:], in_=b1f[1][:])), reads=["b1f1"], writes=["b1s"])
        pb.dma("s", ("dma_start", dict(out=b1bP[nxt][1:2, :], in_=b1s[:])), reads=["b1s"], writes=[f"b1b_{nxt}"])
        pb.op("s", ("copy", dict(out=W2TbP[nxt][:], in_=W2Tf[:])), reads=["W2Tf"], writes=[f"W2Tb_{nxt}"])
        pb.op("v", ("tensor_copy", dict(out=b2bP[nxt][0:1, :], in_=b2f[0][:])), reads=["b2f0"], writes=[f"b2b_{nxt}"])
        pb.op("v", ("tensor_copy", dict(out=b2s[:], in_=b2f[1][:])), reads=["b2f1"], writes=["b2s"])
        pb.dma("s", ("dma_start", dict(out=b2bP[nxt][1:2, :], in_=b2s[:])), reads=["b2s"], writes=[f"b2b_{nxt}"])
        for t_ in range(32):
            h, cc = t_ // 2, t_ % 2
            pb.op("t", ("transpose", dict(out=psA_bf[:, t_ * 64:(t_ + 1) * 64],
                                          in_=W2TbP[nxt][slot(h):slot(h) + 64, pc(h) * 256 + cc * 128:pc(h) * 256 + (cc + 1) * 128],
                                          identity=cm.ident_b[slot(h):slot(h) + 64, slot(h):slot(h) + 64])),
                  reads=[f"W2Tb_{nxt}", "cc_ident"], banks=A(t_ // 16))
        pb.op("s", ("copy", dict(out=W2bP[nxt][:], in_=psA_bf[:, 0:2048])), writes=[f"W2b_{nxt}"], banks=A(0, 1))

    stage_a1a(0)
    stage_a1b(0)
    if NCH > 1:
        stage_a2(0)
        stage_a1a(1)
    for c in range(NCH):
        stage_b(c)
        if c + 1 < NCH:
            stage_a1b(c + 1)
        if c + 2 < NCH:
            stage_a2(c + 1)
            stage_a1a(c + 2)


def build_l1c():
    nc = bass.Bass("TRN2", target_bir_lowering=False)
    io = dict(
        y_h=nc.dram_tensor("y_h", [L, D], F32, kind="ExternalInput").ap(),
        xn1_h=nc.dram_tensor("xn1_h", [L, D], F32, kind="ExternalInput").ap(),
        amax8=nc.dram_tensor("amax8", [1, 8], F32, kind="ExternalInput").ap(),
        wo=nc.dram_tensor("wo", [D, D], F32, kind="ExternalInput").ap(),
        pnw=nc.dram_tensor("pnw", [1, D], F32, kind="ExternalInput").ap(),
        pnb=nc.dram_tensor("pnb", [1, D], F32, kind="ExternalInput").ap(),
        clip_x=nc.dram_tensor("clip_x", [1, 1], F32, kind="ExternalInput").ap(),
        w_gate=nc.dram_tensor("w_gate", [DFF1, D], F32, kind="ExternalInput").ap(),
        w_up=nc.dram_tensor("w_up", [DFF1, D], F32, kind="ExternalInput").ap(),
        attn1_h=nc.dram_tensor("attn1_h", [L, D], F32, kind="ExternalOutput").ap(),
        hgl_h=nc.dram_tensor("hgl_h", [L, DFF1], F32, kind="ExternalOutput").ap(),
        amax_h1=nc.dram_tensor("amax_h1", [1, 1], F32, kind="ExternalOutput").ap(),
    )
    ctx = ExitStack()
    with ctx:
        pb = PB(nc, ctx)
        cm = Common(pb, ctx)
        emit_l1c(pb, cm, ctx, io)
        pb.emit()
    return nc


def emit_l1c(pb, cm, ctx, io):
    nc = pb.nc
    sb = lambda name, shape, dt: ctx.enter_context(nc.sbuf_tensor(name, shape, dt))
    ps = lambda name, shape, dt: ctx.enter_context(nc.psum_tensor("pe" + name, shape, dt))
    cm.const(MAGIC); cm.const(-MAGIC); cm.const(LN_EPS0)
    NCI = 22

    bank = [ps(f"bk{i}", [128, 512], F32) for i in range(8)]
    bank_bf = [b[:].bitcast(BF16) for b in bank]
    ones_row_f = sb("c_ones_row_f", [1, 128], F32)
    pb.op("v", ("memset", dict(ap=ones_row_f[:], constant=1.0)), writes=["ones_row_f"])

    woT = [sb(f"c_woT{k}", [128, D], BF16) for k in range(8)]
    wgT = [sb(f"c_wgT{k}", [128, DFF1], BF16) for k in range(8)]
    wuT = [sb(f"c_wuT{k}", [128, DFF1], BF16) for k in range(8)]
    pnw_b = sb("c_pnw_b", [128, D], F32)
    pnb_b = sb("c_pnb_b", [128, D], F32)
    sq_b = sb("c_sq_b", [128, 2], F32)

    pre = ExitStack()
    psb = lambda name, shape, dt: pre.enter_context(nc.sbuf_tensor(name, shape, dt))
    stg = psb("c_stg", [128, D + 16], F32)
    stgb = psb("c_stgb", [128, D], BF16)
    wsc = psb("c_wsc", [128, 4], F32)

    pb.dma("y", ("dma_start", dict(out=stg[0:1, D:D + 8], in_=io["amax8"])), reads=["h_amax8"], writes=["stgr"])
    pb.dma("y", ("dma_start", dict(out=stg[0:1, D + 8:D + 9], in_=io["clip_x"])), reads=["h_clipx"], writes=["stgr"])
    pb.op("v", ("reduce_max", dict(out=wsc[0:1, 0:1], in_=stg[0:1, D:D + 8], axis=AX.X)), reads=["stgr"], writes=["wsc0"])
    pb.op("s", ("activation", dict(out=wsc[0:1, 1:2], in_=stg[0:1, D + 8:D + 9], func=AF.Sigmoid, scale=1.0)),
          reads=["stgr"], writes=["wsc1"])
    pb.op("v", ("tensor_tensor", dict(out=wsc[0:1, 2:3], in0=wsc[0:1, 0:1], in1=wsc[0:1, 1:2], op=ALU.mult)),
          reads=["wsc0", "wsc1"], writes=["wsc2"])
    pb.op("v", ("tensor_scalar", dict(out=wsc[0:1, 3:4], in0=wsc[0:1, 2:3], scalar1=1.0 / 127.0, scalar2=None, op0=ALU.mult)),
          reads=["wsc2"], writes=["wsc3"])
    pb.op("v", ("reciprocal", dict(out=wsc[0:1, 2:3], in_=wsc[0:1, 3:4])), reads=["wsc3"], writes=["wsc2"])
    pb.op("t", ("matmul", dict(out=bank[0][:, 0:2], lhsT=ones_row_f[:], rhs=wsc[0:1, 2:4], start=True, stop=True)),
          reads=["wsc2", "wsc3", "ones_row_f"], banks=bank_regions(0))
    pb.op("v", ("tensor_copy", dict(out=sq_b[:], in_=bank[0][:, 0:2])), writes=["sq_b"], banks=bank_regions(0))

    for name, dst, rdst in (("pnw", pnw_b, "pnw_b"), ("pnb", pnb_b, "pnb_b")):
        pb.dma("y", ("dma_start", dict(out=stg[0:1, 0:D], in_=io[name])), reads=[f"h_{name}"], writes=["stg"])
        for j in range(2):
            pb.op("t", ("matmul", dict(out=bank[1][:, 0:512], lhsT=ones_row_f[:], rhs=stg[0:1, j * 512:(j + 1) * 512],
                                       start=True, stop=True)),
                  reads=["stg", "ones_row_f"], banks=bank_regions(1))
            pb.op("v", ("tensor_copy", dict(out=dst[:, j * 512:(j + 1) * 512], in_=bank[1][:, 0:512])),
                  writes=[rdst], banks=bank_regions(1))

    for k in range(8):
        pb.dma("y", ("dma_start", dict(out=stg[:, 0:D], in_=io["wo"][k * 128:(k + 1) * 128, :])),
               reads=["h_wo"], writes=["stg"])
        pb.op("v", ("tensor_copy", dict(out=stgb[:], in_=stg[:, 0:D])), reads=["stg"], writes=["stgb"])
        for c in range(8):
            pb.op("t", ("transpose", dict(out=bank_bf[2][:, c * 128:(c + 1) * 128],
                                          in_=stgb[:, c * 128:(c + 1) * 128], identity=cm.ident_b[:])),
                  reads=["stgb", "cc_ident"], banks=bank_regions(2))
        for c in range(8):
            pb.op("s", ("copy", dict(out=woT[c][:, k * 128:(k + 1) * 128], in_=bank_bf[2][:, c * 128:(c + 1) * 128])),
                  writes=[f"woT{c}"], banks=bank_regions(2))

    for wname, wdst in (("w_gate", wgT), ("w_up", wuT)):
        for k in range(NCI):
            pb.dma("y", ("dma_start", dict(out=stg[:, 0:D], in_=io[wname][k * 128:(k + 1) * 128, :])),
                   reads=[f"h_{wname}"], writes=["stg"])
            pb.op("v", ("tensor_reduce", dict(out=wsc[:, 0:1], in_=stg[:, 0:D], axis=AX.X, op=ALU.max,
                                              apply_absolute_value=True)),
                  reads=["stg"], writes=["wsc_am"])
            pb.op("v", ("tensor_scalar", dict(out=wsc[:, 1:2], in0=wsc[:, 0:1], scalar1=1.0 / 127.0, scalar2=None,
                                              op0=ALU.mult)),
                  reads=["wsc_am"], writes=["wsc_inv"])
            pb.op("v", ("reciprocal", dict(out=wsc[:, 2:3], in_=wsc[:, 1:2])), reads=["wsc_inv"], writes=["wsc_s"])
            pb.op("v", ("tensor_scalar", dict(out=stg[:, 0:D], in0=stg[:, 0:D], scalar1=wsc[:, 2:3], scalar2=MAGIC,
                                              op0=ALU.mult, op1=ALU.add)),
                  reads=["stg", "wsc_s"], writes=["stg"])
            pb.op("s", ("activation", dict(out=stg[:, 0:D], in_=stg[:, 0:D], func=AF.Identity, bias=cm.const(-MAGIC),
                                           scale=1.0)),
                  reads=["stg", cm.const_region(-MAGIC)], writes=["stg"])
            pb.op("v", ("tensor_scalar", dict(out=stg[:, 0:D], in0=stg[:, 0:D], scalar1=127.0, scalar2=-128.0,
                                              op0=ALU.min, op1=ALU.max)),
                  reads=["stg"], writes=["stg"])
            pb.op("v", ("tensor_scalar", dict(out=stgb[:], in0=stg[:, 0:D], scalar1=wsc[:, 1:2], scalar2=None, op0=ALU.mult)),
                  reads=["stg", "wsc_inv"], writes=["stgb"])
            for c in range(8):
                pb.op("t", ("transpose", dict(out=bank_bf[3][:, c * 128:(c + 1) * 128],
                                              in_=stgb[:, c * 128:(c + 1) * 128], identity=cm.ident_b[:])),
                      reads=["stgb", "cc_ident"], banks=bank_regions(3))
            for c in range(8):
                pb.op("s", ("copy", dict(out=wdst[c][:, k * 128:(k + 1) * 128], in_=bank_bf[3][:, c * 128:(c + 1) * 128])),
                      writes=[f"{wname}T{c}"], banks=bank_regions(3))
    pre.close()
    pb.mem_barrier()

    yt = [sb(f"c_yt{i}", [128, D], F32) for i in range(2)]
    xnt = [sb(f"c_xnt{i}", [128, D], F32) for i in range(2)]
    a1t = [sb(f"c_a1t{i}", [128, D], F32) for i in range(2)]
    ht = [sb(f"c_ht{i}", [128, DFF1], F32) for i in range(2)]
    ynbP = [sb(f"c_ynb{j}", [128, D], BF16) for j in range(2)]
    ynTP = [sb(f"c_ynT{j}", [128, D], BF16) for j in range(2)]
    xqbP = [sb(f"c_xqb{j}", [128, D], BF16) for j in range(2)]
    xqTP = [sb(f"c_xqT{j}", [128, D], BF16) for j in range(2)]
    quantP = [sb(f"c_quant{j}", [128, D], F32) for j in range(2)]
    siluP = [sb(f"c_silu{j}", [128, 512], F32) for j in range(2)]
    statP = [sb(f"c_stat{j}", [128, 10], F32) for j in range(2)]
    sq2P = [sb(f"c_sq2{j}", [128, D], F32) for j in range(2)]
    amax_run = sb("c_amax_run", [128, 1], F32)
    amr = sb("c_amr", [128, 1], F32)
    pb.op("v", ("memset", dict(ap=amax_run[:], constant=0.0)), writes=["am_run"])
    inv_d = 1.0 / D
    NBI = [512, 512, 512, 512, 512, 256]

    for i in range(NT):
        db = i % 2
        y_i, xn_i, a1_i, h_i = yt[db], xnt[db], a1t[db], ht[db]
        ry, rxn, ra1, rh = f"c_yt{db}", f"c_xnt{db}", f"c_a1t{db}", f"c_ht{db}"
        ynb, ynT, xqb, xqT, quant, silu_t, stat, sq2 = (ynbP[db], ynTP[db], xqbP[db], xqTP[db],
                                                        quantP[db], siluP[db], statP[db], sq2P[db])
        sfx = str(db)
        sl = slice(i * 128, (i + 1) * 128)
        pb.dma("y", ("dma_start", dict(out=y_i[:], in_=io["y_h"][sl, :])), reads=["h_y"], writes=[ry])
        pb.dma("p", ("dma_start", dict(out=xn_i[:], in_=io["xn1_h"][sl, :])), reads=["h_xn1"], writes=[rxn])

        pb.op("v", ("reduce_sum", dict(out=stat[:, 0:1], in_=y_i[:], axis=AX.X)), reads=[ry], writes=[f"st_s1{sfx}"])
        pb.op("s", ("activation", dict(out=sq2[:], in_=y_i[:], func=AF.Square, scale=1.0)), reads=[ry], writes=[f"sq2{sfx}"])
        pb.op("v", ("reduce_sum", dict(out=stat[:, 1:2], in_=sq2[:], axis=AX.X)), reads=[f"sq2{sfx}"], writes=[f"st_s2{sfx}"])
        pb.op("v", ("tensor_scalar", dict(out=stat[:, 2:3], in0=stat[:, 0:1], scalar1=inv_d, scalar2=None, op0=ALU.mult)),
              reads=[f"st_s1{sfx}"], writes=[f"st_mu{sfx}"])
        pb.op("v", ("tensor_scalar", dict(out=stat[:, 3:4], in0=stat[:, 2:3], scalar1=stat[:, 2:3], scalar2=-1.0,
                                          op0=ALU.mult, op1=ALU.mult)),
              reads=[f"st_mu{sfx}"], writes=[f"st_nmu2{sfx}"])
        pb.op("v", ("tensor_scalar", dict(out=stat[:, 4:5], in0=stat[:, 1:2], scalar1=inv_d, scalar2=None, op0=ALU.mult)),
              reads=[f"st_s2{sfx}"], writes=[f"st_ex2{sfx}"])
        pb.op("v", ("tensor_tensor", dict(out=stat[:, 4:5], in0=stat[:, 4:5], in1=stat[:, 3:4], op=ALU.add)),
              reads=[f"st_ex2{sfx}", f"st_nmu2{sfx}"], writes=[f"st_var{sfx}"])
        pb.op("s", ("activation", dict(out=stat[:, 5:6], in_=stat[:, 4:5], func=AF.Sqrt, bias=cm.const(LN_EPS0), scale=1.0)),
              reads=[f"st_var{sfx}", cm.const_region(LN_EPS0)], writes=[f"st_sd{sfx}"])
        pb.op("v", ("reciprocal", dict(out=stat[:, 6:7], in_=stat[:, 5:6])), reads=[f"st_sd{sfx}"], writes=[f"st_rstd{sfx}"])
        pb.op("v", ("tensor_scalar", dict(out=stat[:, 7:8], in0=stat[:, 2:3], scalar1=stat[:, 6:7], scalar2=-1.0,
                                          op0=ALU.mult, op1=ALU.mult)),
              reads=[f"st_mu{sfx}", f"st_rstd{sfx}"], writes=[f"st_nmr{sfx}"])
        pb.op("s", ("activation", dict(out=sq2[:], in_=y_i[:], func=AF.Identity, bias=stat[:, 7:8], scale=stat[:, 6:7])),
              reads=[ry, f"st_nmr{sfx}", f"st_rstd{sfx}"], writes=[f"sq2{sfx}"])
        pb.op("v", ("tensor_tensor", dict(out=sq2[:], in0=sq2[:], in1=pnw_b[:], op=ALU.mult)),
              reads=[f"sq2{sfx}", "pnw_b"], writes=[f"sq2{sfx}"])
        pb.op("v", ("tensor_tensor", dict(out=ynb[:], in0=sq2[:], in1=pnb_b[:], op=ALU.add)),
              reads=[f"sq2{sfx}", "pnb_b"], writes=[f"ynb{sfx}"])
        for c in range(8):
            pb.op("t", ("transpose", dict(out=bank_bf[0][:, c * 128:(c + 1) * 128],
                                          in_=ynb[:, c * 128:(c + 1) * 128], identity=cm.ident_b[:])),
                  reads=[f"ynb{sfx}", "cc_ident"], banks=bank_regions(0))
        pb.op("s", ("copy", dict(out=ynT[:], in_=bank_bf[0][:, 0:1024])), writes=[f"ynT{sfx}"], banks=bank_regions(0))
        for nb in range(2):
            for k in range(8):
                pb.op("t", ("matmul", dict(out=bank[1 + nb][:, 0:512], lhsT=ynT[:, k * 128:(k + 1) * 128],
                                           rhs=woT[k][:, nb * 512:(nb + 1) * 512], start=(k == 0), stop=(k == 7))),
                      reads=[f"ynT{sfx}", f"woT{k}"], banks=bank_regions(1 + nb))
            pb.op("v", ("tensor_copy", dict(out=a1_i[:, nb * 512:(nb + 1) * 512], in_=bank[1 + nb][:, 0:512])),
                  writes=[ra1], banks=bank_regions(1 + nb))
        pb.dma("y", ("dma_start", dict(out=io["attn1_h"][sl, :], in_=a1_i[:])), reads=[ra1], writes=["h_attn1"])

        pb.op("v", ("tensor_scalar", dict(out=quant[:], in0=xn_i[:], scalar1=sq_b[:, 0:1], scalar2=MAGIC,
                                          op0=ALU.mult, op1=ALU.add)),
              reads=[rxn, "sq_b"], writes=[f"quant{sfx}"])
        pb.op("s", ("activation", dict(out=quant[:], in_=quant[:], func=AF.Identity, bias=cm.const(-MAGIC), scale=1.0)),
              reads=[f"quant{sfx}", cm.const_region(-MAGIC)], writes=[f"quant{sfx}"])
        pb.op("v", ("tensor_scalar", dict(out=quant[:], in0=quant[:], scalar1=127.0, scalar2=-128.0,
                                          op0=ALU.min, op1=ALU.max)),
              reads=[f"quant{sfx}"], writes=[f"quant{sfx}"])
        pb.op("v", ("tensor_scalar", dict(out=xqb[:], in0=quant[:], scalar1=sq_b[:, 1:2], scalar2=None, op0=ALU.mult)),
              reads=[f"quant{sfx}", "sq_b"], writes=[f"xqb{sfx}"])
        for c in range(8):
            pb.op("t", ("transpose", dict(out=bank_bf[0][:, c * 128:(c + 1) * 128],
                                          in_=xqb[:, c * 128:(c + 1) * 128], identity=cm.ident_b[:])),
                  reads=[f"xqb{sfx}", "cc_ident"], banks=bank_regions(0))
        pb.op("s", ("copy", dict(out=xqT[:], in_=bank_bf[0][:, 0:1024])), writes=[f"xqT{sfx}"], banks=bank_regions(0))
        off = 0
        for j, nsz in enumerate(NBI):
            gb = 3 + (j % 2)
            ub = 5 + (j % 2)
            for k in range(8):
                pb.op("t", ("matmul", dict(out=bank[gb][:, 0:nsz], lhsT=xqT[:, k * 128:(k + 1) * 128],
                                           rhs=wgT[k][:, off:off + nsz], start=(k == 0), stop=(k == 7))),
                      reads=[f"xqT{sfx}", f"w_gateT{k}"], banks=bank_regions(gb))
            for k in range(8):
                pb.op("t", ("matmul", dict(out=bank[ub][:, 0:nsz], lhsT=xqT[:, k * 128:(k + 1) * 128],
                                           rhs=wuT[k][:, off:off + nsz], start=(k == 0), stop=(k == 7))),
                      reads=[f"xqT{sfx}", f"w_upT{k}"], banks=bank_regions(ub))
            pb.op("s", ("activation", dict(out=silu_t[:, 0:nsz], in_=bank[gb][:, 0:nsz], func=AF.Silu, scale=1.0)),
                  writes=[f"silu_t{sfx}"], banks=bank_regions(gb))
            pb.op("v", ("tensor_tensor", dict(out=h_i[:, off:off + nsz], in0=silu_t[:, 0:nsz], in1=bank[ub][:, 0:nsz],
                                              op=ALU.mult)),
                  reads=[f"silu_t{sfx}"], writes=[rh], banks=bank_regions(ub))
            pb.op("v", ("tensor_reduce", dict(out=amr[:], in_=h_i[:, off:off + nsz], axis=AX.X, op=ALU.max,
                                              apply_absolute_value=True)),
                  reads=[rh], writes=[f"amr{sfx}"])
            pb.op("v", ("tensor_tensor", dict(out=amax_run[:], in0=amax_run[:], in1=amr[:], op=ALU.max)),
                  reads=[f"amr{sfx}", "am_run"], writes=["am_run"])
            off += nsz
        pb.dma("p", ("dma_start", dict(out=io["hgl_h"][sl, :], in_=h_i[:])), reads=[rh], writes=["h_hgl"])

    am_scr = nc.dram_tensor("am_scr4", [128], F32, kind="Internal").ap()
    am_row = sb("c_am_row", [1, 128], F32)
    am_out = sb("c_am_out", [1, 1], F32)
    pb.dma("y", ("dma_start", dict(out=am_scr.rearrange("(a b) -> a b", a=128), in_=amax_run[:])),
           reads=["am_run"], writes=["h_amscr"])
    pb.dma("y", ("dma_start", dict(out=am_row[:], in_=am_scr.rearrange("(a b) -> a b", a=1))),
           reads=["h_amscr"], writes=["am_rrow"])
    pb.op("v", ("reduce_max", dict(out=am_out[:], in_=am_row[:], axis=AX.X)), reads=["am_rrow"], writes=["am_o"])
    pb.dma("y", ("dma_start", dict(out=io["amax_h1"], in_=am_out[:])), reads=["am_o"], writes=["h_amaxh1"])


def build_l1d():
    nc = bass.Bass("TRN2", target_bir_lowering=False)
    io = dict(
        hgl_h=nc.dram_tensor("hgl_h", [L, DFF1], F32, kind="ExternalInput").ap(),
        amax8=nc.dram_tensor("amax8", [1, 8], F32, kind="ExternalInput").ap(),
        clip_down=nc.dram_tensor("clip_down", [1, 1], F32, kind="ExternalInput").ap(),
        w_down=nc.dram_tensor("w_down", [D, DFF1], F32, kind="ExternalInput").ap(),
        x1_h=nc.dram_tensor("x1_h", [L, D], F32, kind="ExternalInput").ap(),
        attn1_h=nc.dram_tensor("attn1_h", [L, D], F32, kind="ExternalInput").ap(),
        out_h=nc.dram_tensor("out_h", [L, D], F32, kind="ExternalOutput").ap(),
    )
    ctx = ExitStack()
    with ctx:
        pb = PB(nc, ctx)
        cm = Common(pb, ctx)
        emit_l1d(pb, cm, ctx, io)
        pb.emit()
    return nc


def emit_l1d(pb, cm, ctx, io):
    nc = pb.nc
    sb = lambda name, shape, dt: ctx.enter_context(nc.sbuf_tensor(name, shape, dt))
    ps = lambda name, shape, dt: ctx.enter_context(nc.psum_tensor("pf" + name, shape, dt))
    cm.const(MAGIC); cm.const(-MAGIC)
    NCI = 22

    bank = [ps(f"bk{i}", [128, 512], F32) for i in range(8)]
    bank_bf = [b[:].bitcast(BF16) for b in bank]
    ones_row_f = sb("d_ones_row_f", [1, 128], F32)
    pb.op("v", ("memset", dict(ap=ones_row_f[:], constant=1.0)), writes=["ones_row_f"])

    wdT = [sb(f"d_wdT{k}", [128, D], BF16) for k in range(NCI)]
    sq_b = sb("d_sq_b", [128, 2], F32)

    pre = ExitStack()
    psb = lambda name, shape, dt: pre.enter_context(nc.sbuf_tensor(name, shape, dt))
    stg = psb("d_stg", [128, DFF1 + 16], F32)
    stgb = psb("d_stgb", [128, DFF1], BF16)
    wsc = psb("d_wsc", [128, 4], F32)

    pb.dma("y", ("dma_start", dict(out=stg[0:1, DFF1:DFF1 + 8], in_=io["amax8"])), reads=["h_amax8"], writes=["stgr"])
    pb.dma("y", ("dma_start", dict(out=stg[0:1, DFF1 + 8:DFF1 + 9], in_=io["clip_down"])), reads=["h_clipd"], writes=["stgr"])
    pb.op("v", ("reduce_max", dict(out=wsc[0:1, 0:1], in_=stg[0:1, DFF1:DFF1 + 8], axis=AX.X)), reads=["stgr"], writes=["wsc0"])
    pb.op("s", ("activation", dict(out=wsc[0:1, 1:2], in_=stg[0:1, DFF1 + 8:DFF1 + 9], func=AF.Sigmoid, scale=1.0)),
          reads=["stgr"], writes=["wsc1"])
    pb.op("v", ("tensor_tensor", dict(out=wsc[0:1, 2:3], in0=wsc[0:1, 0:1], in1=wsc[0:1, 1:2], op=ALU.mult)),
          reads=["wsc0", "wsc1"], writes=["wsc2"])
    pb.op("v", ("tensor_scalar", dict(out=wsc[0:1, 3:4], in0=wsc[0:1, 2:3], scalar1=1.0 / 127.0, scalar2=None, op0=ALU.mult)),
          reads=["wsc2"], writes=["wsc3"])
    pb.op("v", ("reciprocal", dict(out=wsc[0:1, 2:3], in_=wsc[0:1, 3:4])), reads=["wsc3"], writes=["wsc2"])
    pb.op("t", ("matmul", dict(out=bank[0][:, 0:2], lhsT=ones_row_f[:], rhs=wsc[0:1, 2:4], start=True, stop=True)),
          reads=["wsc2", "wsc3", "ones_row_f"], banks=bank_regions(0))
    pb.op("v", ("tensor_copy", dict(out=sq_b[:], in_=bank[0][:, 0:2])), writes=["sq_b"], banks=bank_regions(0))

    for k in range(8):
        pb.dma("y", ("dma_start", dict(out=stg[:, 0:DFF1], in_=io["w_down"][k * 128:(k + 1) * 128, :])),
               reads=["h_wd"], writes=["stg"])
        pb.op("v", ("tensor_reduce", dict(out=wsc[:, 0:1], in_=stg[:, 0:DFF1], axis=AX.X, op=ALU.max,
                                          apply_absolute_value=True)),
              reads=["stg"], writes=["wsc_am"])
        pb.op("v", ("tensor_scalar", dict(out=wsc[:, 1:2], in0=wsc[:, 0:1], scalar1=1.0 / 127.0, scalar2=None, op0=ALU.mult)),
              reads=["wsc_am"], writes=["wsc_inv"])
        pb.op("v", ("reciprocal", dict(out=wsc[:, 2:3], in_=wsc[:, 1:2])), reads=["wsc_inv"], writes=["wsc_s"])
        pb.op("v", ("tensor_scalar", dict(out=stg[:, 0:DFF1], in0=stg[:, 0:DFF1], scalar1=wsc[:, 2:3], scalar2=MAGIC,
                                          op0=ALU.mult, op1=ALU.add)),
              reads=["stg", "wsc_s"], writes=["stg"])
        pb.op("s", ("activation", dict(out=stg[:, 0:DFF1], in_=stg[:, 0:DFF1], func=AF.Identity, bias=cm.const(-MAGIC),
                                       scale=1.0)),
              reads=["stg", cm.const_region(-MAGIC)], writes=["stg"])
        pb.op("v", ("tensor_scalar", dict(out=stg[:, 0:DFF1], in0=stg[:, 0:DFF1], scalar1=127.0, scalar2=-128.0,
                                          op0=ALU.min, op1=ALU.max)),
              reads=["stg"], writes=["stg"])
        pb.op("v", ("tensor_scalar", dict(out=stgb[:], in0=stg[:, 0:DFF1], scalar1=wsc[:, 1:2], scalar2=None, op0=ALU.mult)),
              reads=["stg", "wsc_inv"], writes=["stgb"])
        for c in range(NCI):
            pb.op("t", ("transpose", dict(out=bank_bf[1 + c % 2][:, (c // 2 % 4) * 128:(c // 2 % 4 + 1) * 128],
                                          in_=stgb[:, c * 128:(c + 1) * 128], identity=cm.ident_b[:])),
                  reads=["stgb", "cc_ident"], banks=bank_regions(1 + c % 2))
            pb.op("s", ("copy", dict(out=wdT[c][:, k * 128:(k + 1) * 128],
                                     in_=bank_bf[1 + c % 2][:, (c // 2 % 4) * 128:(c // 2 % 4 + 1) * 128])),
                  writes=[f"wdT{c}"], banks=bank_regions(1 + c % 2))
    pre.close()
    pb.mem_barrier()

    ht = [sb(f"d_ht{i}", [128, DFF1], F32) for i in range(2)]
    x1t = [sb(f"d_x1t{i}", [128, D], F32) for i in range(2)]
    a1t = [sb(f"d_a1t{i}", [128, D], F32) for i in range(2)]
    ot = [sb(f"d_ot{i}", [128, D], F32) for i in range(2)]
    hqbP = [sb(f"d_hqb{j}", [128, DFF1], BF16) for j in range(2)]
    hqTP = [sb(f"d_hqT{j}", [128, DFF1], BF16) for j in range(2)]

    for i in range(NT):
        db = i % 2
        h_i, x1_i, a1_i, o_i = ht[db], x1t[db], a1t[db], ot[db]
        rh, rx1, ra1, ro = f"d_ht{db}", f"d_x1t{db}", f"d_a1t{db}", f"d_ot{db}"
        hqb, hqT = hqbP[db], hqTP[db]
        sfx = str(db)
        sl = slice(i * 128, (i + 1) * 128)
        pb.dma("y", ("dma_start", dict(out=h_i[:], in_=io["hgl_h"][sl, :])), reads=["h_hgl"], writes=[rh])
        pb.dma("p", ("dma_start", dict(out=x1_i[:], in_=io["x1_h"][sl, :])), reads=["h_x1"], writes=[rx1])
        pb.dma("p", ("dma_start", dict(out=a1_i[:], in_=io["attn1_h"][sl, :])), reads=["h_attn1"], writes=[ra1])
        pb.op("v", ("tensor_scalar", dict(out=h_i[:], in0=h_i[:], scalar1=sq_b[:, 0:1], scalar2=MAGIC,
                                          op0=ALU.mult, op1=ALU.add)),
              reads=[rh, "sq_b"], writes=[rh])
        pb.op("s", ("activation", dict(out=h_i[:], in_=h_i[:], func=AF.Identity, bias=cm.const(-MAGIC), scale=1.0)),
              reads=[rh, cm.const_region(-MAGIC)], writes=[rh])
        pb.op("v", ("tensor_scalar", dict(out=h_i[:], in0=h_i[:], scalar1=127.0, scalar2=-128.0,
                                          op0=ALU.min, op1=ALU.max)),
              reads=[rh], writes=[rh])
        pb.op("v", ("tensor_scalar", dict(out=hqb[:], in0=h_i[:], scalar1=sq_b[:, 1:2], scalar2=None, op0=ALU.mult)),
              reads=[rh, "sq_b"], writes=[f"hqb{sfx}"])
        for c in range(NCI):
            tb = c % 2
            cslot = (c // 2) % 4
            pb.op("t", ("transpose", dict(out=bank_bf[tb][:, cslot * 128 + 512 * 0:cslot * 128 + 128 + 512 * 0],
                                          in_=hqb[:, c * 128:(c + 1) * 128], identity=cm.ident_b[:])),
                  reads=[f"hqb{sfx}", "cc_ident"], banks=bank_regions(tb))
            pb.op("s", ("copy", dict(out=hqT[:, c * 128:(c + 1) * 128], in_=bank_bf[tb][:, cslot * 128:cslot * 128 + 128])),
                  writes=[f"hqT{sfx}"], banks=bank_regions(tb))
        for nb in range(2):
            for k in range(NCI):
                pb.op("t", ("matmul", dict(out=bank[2 + nb][:, 0:512], lhsT=hqT[:, k * 128:(k + 1) * 128],
                                           rhs=wdT[k][:, nb * 512:(nb + 1) * 512], start=(k == 0), stop=(k == NCI - 1))),
                      reads=[f"hqT{sfx}", f"wdT{k}"], banks=bank_regions(2 + nb))
            pb.op("v", ("tensor_tensor", dict(out=o_i[:, nb * 512:(nb + 1) * 512], in0=x1_i[:, nb * 512:(nb + 1) * 512],
                                              in1=bank[2 + nb][:, 0:512], op=ALU.add)),
                  reads=[rx1], writes=[ro], banks=bank_regions(2 + nb))
        pb.op("v", ("tensor_tensor", dict(out=o_i[:], in0=o_i[:], in1=a1_i[:], op=ALU.add)),
              reads=[ro, ra1], writes=[ro])
        pb.dma("y", ("dma_start", dict(out=io["out_h"][sl, :], in_=o_i[:])), reads=[ro], writes=["h_out"])


def build_full():
    nc = bass.Bass("TRN2", target_bir_lowering=False)
    mkI = lambda name, shape, dt: nc.dram_tensor(name, shape, dt, kind="Internal").ap()
    ein = lambda name, shape: nc.dram_tensor(name, shape, F32, kind="ExternalInput").ap()
    x = ein("x", [L, D])
    wqkv = ein("wqkv", [D, 3 * D]); gamma = ein("gamma", [1, D]); beta = ein("beta", [1, D])
    wout = ein("wout", [D, D]); w_in = ein("w_in", [D, DFF0]); b_in = ein("b_in", [1, DFF0])
    w_out = ein("w_out", [DFF0, D]); b_out = ein("b_out", [1, D])
    wq = ein("wq", [D, D]); wk = ein("wk", [D, D]); wv = ein("wv", [D, D])
    lr_w = ein("lr_w", [H, D]); lr_b = ein("lr_b", [1, H])
    tnw = ein("tnw", [1, D]); tnb = ein("tnb", [1, D])
    w1 = ein("w1", [H, 64, 256]); b1 = ein("b1", [H, 1, 256])
    w2 = ein("w2", [H, 256, 64]); b2 = ein("b2", [H, 1, 64])
    wo = ein("wo", [D, D]); pnw = ein("pnw", [1, D]); pnb = ein("pnb", [1, D])
    clip_x = ein("clip_x", [1, 1]); clip_down = ein("clip_down", [1, 1])
    w_gate = ein("w_gate", [DFF1, D]); w_up = ein("w_up", [DFF1, D]); w_down = ein("w_down", [D, DFF1])
    out_h = nc.dram_tensor("out_h", [L, D], F32, kind="ExternalOutput").ap()
    xn_h = mkI("i_xn", [L, D], F32)
    q_h = mkI("i_q", [L, D], BF16)
    kv_h = mkI("i_kv", [64, 1024], F32)
    ksum_h = mkI("i_ksum", [1, 1024], F32)
    stats_h = mkI("i_stats", [NT, 128, 2], F32)
    part1_h = mkI("i_part1", [L, D], F32)
    h2_h = mkI("i_h2", [L, DFF0], F32)
    x1_h = mkI("i_x1", [L, D], F32)
    xn1_h = mkI("i_xn1", [L, D], F32)
    qtok_h = mkI("i_qtok", [L, D], F32)
    ktok_h = mkI("i_ktok", [L, D], BF16)
    tgt_h = mkI("i_tgt", [L, D], F32)
    eta_h = mkI("i_eta", [L, H], F32)
    qT_h = mkI("i_qT", [NT, 64, 16 * 128], BF16)
    kT_h = mkI("i_kT", [NT, 64, 16 * 128], BF16)
    y_h = mkI("i_y", [L, D], F32)
    attn1_h = mkI("i_attn1", [L, D], F32)
    hgl_h = mkI("i_hgl", [L, DFF1], F32)
    amp = [mkI(f"i_amp{j}", [1, 1], F32) for j in range(4)]
    ama = [mkI(f"i_ama{j}", [8, 1], F32) for j in range(4)]
    groups = [list(range(8))]

    ctx = ExitStack()
    with ctx:
        pb = PB(nc, ctx)
        cm = Common(pb, ctx)

        def gather(j, rpart, rall):
            pb.dma("p", lambda e, j=j: e.collective_compute(
                "AllGather", ALU.bypass, replica_groups=groups,
                ins=[amp[j]], outs=[ama[j]]),
                reads=[rpart], writes=[rall])

        def phase(emitfn, io):
            pctx = ExitStack()
            emitfn(pb, cm, pctx, io)
            pctx.close()
            pb.mem_barrier()

        phase(emit_l0a, dict(x=x, wqkv=wqkv, gamma=gamma, beta=beta, xn_h=xn_h, q_h=q_h,
                             kv_h=kv_h, ksum_h=ksum_h, stats_h=stats_h, amax_h=amp[0]))
        gather(0, "h_amax", "h_amax8")
        phase(emit_l0b, dict(x=x, xn_h=xn_h, q_h=q_h, kv_h=kv_h, ksum_h=ksum_h, stats_h=stats_h,
                             amax8=ama[0].rearrange("a b -> b a"), wout=wout, w_in=w_in, b_in=b_in,
                             part1_h=part1_h, h2_h=h2_h, amaxh2=amp[1]))
        gather(1, "h_amaxh2", "h_am8")
        phase(emit_l0c, dict(part1_h=part1_h, h2_h=h2_h, amaxh2_8=ama[1].rearrange("a b -> b a"),
                             w_out=w_out, b_out=b_out, x1_h=x1_h, xn1_h=xn1_h, amax_xn1=amp[2]))
        gather(2, "h_amax1", "h_amax8")
        phase(emit_l1a, dict(xn1_h=xn1_h, wq=wq, wk=wk, wv=wv, lr_w=lr_w, lr_b=lr_b, tnw=tnw, tnb=tnb,
                             qtok_h=qtok_h, ktok_h=ktok_h, tgt_h=tgt_h, eta_h=eta_h, qT_h=qT_h, kT_h=kT_h))
        phase(emit_l1b, dict(qT_h=qT_h, kT_h=kT_h, ktok_h=ktok_h, qtok_h=qtok_h, tgt_h=tgt_h,
                             eta_h=eta_h, w1=w1, b1=b1, w2=w2, b2=b2, tnw=tnw, tnb=tnb, y_h=y_h))
        phase(emit_l1c, dict(y_h=y_h, xn1_h=xn1_h, amax8=ama[2].rearrange("a b -> b a"), wo=wo,
                             pnw=pnw, pnb=pnb, clip_x=clip_x, w_gate=w_gate, w_up=w_up,
                             attn1_h=attn1_h, hgl_h=hgl_h, amax_h1=amp[3]))
        gather(3, "h_amaxh1", "h_amax8")
        phase(emit_l1d, dict(hgl_h=hgl_h, amax8=ama[3].rearrange("a b -> b a"), clip_down=clip_down,
                             w_down=w_down, x1_h=x1_h, attn1_h=attn1_h, out_h=out_h))
        pb.emit()
    return nc


def build_g3():
    nc = bass.Bass("TRN2", target_bir_lowering=False)
    U16 = mybir.dt.uint16
    ein = lambda name, shape, dt=F32: nc.dram_tensor(name, shape, dt, kind="ExternalInput").ap()
    eout = lambda name, shape: nc.dram_tensor(name, shape, F32, kind="ExternalOutput").ap()
    eoutb = lambda name, shape: nc.dram_tensor(name, shape, U16, kind="ExternalOutput").ap().bitcast(BF16)
    io_c = dict(
        part1_h=ein("part1_h", [L, D]), h2_h=ein("h2_h", [L, DFF0]), amaxh2_8=ein("amaxh2_8", [1, 8]),
        w_out=ein("w_out", [DFF0, D]), b_out=ein("b_out", [1, D]),
        x1_h=eout("x1_h", [L, D]), xn1_h=eout("xn1_h", [L, D]), amax_xn1=eout("amax_xn1", [1, 1]),
    )
    io_a = dict(
        xn1_h=io_c["xn1_h"],
        wq=ein("wq", [D, D]), wk=ein("wk", [D, D]), wv=ein("wv", [D, D]),
        lr_w=ein("lr_w", [H, D]), lr_b=ein("lr_b", [1, H]), tnw=ein("tnw", [1, D]), tnb=ein("tnb", [1, D]),
        qtok_h=eout("qtok_h", [L, D]), ktok_h=eoutb("ktok_h", [L, D]), tgt_h=eout("tgt_h", [L, D]),
        eta_h=eout("eta_h", [L, H]), qT_h=eoutb("qT_h", [NT, 64, 16 * 128]), kT_h=eoutb("kT_h", [NT, 64, 16 * 128]),
    )
    ctx = ExitStack()
    with ctx:
        pb = PB(nc, ctx)
        cm = Common(pb, ctx)
        pctx = ExitStack()
        emit_l0c(pb, cm, pctx, io_c)
        pctx.close()
        pb.mem_barrier()
        pctx = ExitStack()
        emit_l1a(pb, cm, pctx, io_a)
        pctx.close()
        pb.emit()
    return nc


def build_g4():
    nc = bass.Bass("TRN2", target_bir_lowering=False)
    ein = lambda name, shape, dt=F32: nc.dram_tensor(name, shape, dt, kind="ExternalInput").ap()
    eout = lambda name, shape: nc.dram_tensor(name, shape, F32, kind="ExternalOutput").ap()
    y_h = nc.dram_tensor("y_h", [L, D], F32, kind="Internal").ap()
    io_b = dict(
        qT_h=ein("qT_h", [NT, 64, 16 * 128], BF16), kT_h=ein("kT_h", [NT, 64, 16 * 128], BF16),
        ktok_h=ein("ktok_h", [L, D], BF16), qtok_h=ein("qtok_h", [L, D]), tgt_h=ein("tgt_h", [L, D]),
        eta_h=ein("eta_h", [L, H]), w1=ein("w1", [H, 64, 256]), b1=ein("b1", [H, 1, 256]),
        w2=ein("w2", [H, 256, 64]), b2=ein("b2", [H, 1, 64]), tnw=ein("tnw", [1, D]), tnb=ein("tnb", [1, D]),
        y_h=y_h,
    )
    io_c = dict(
        y_h=y_h, xn1_h=ein("xn1_h", [L, D]), amax8=ein("amax8", [1, 8]), wo=ein("wo", [D, D]),
        pnw=ein("pnw", [1, D]), pnb=ein("pnb", [1, D]), clip_x=ein("clip_x", [1, 1]),
        w_gate=ein("w_gate", [DFF1, D]), w_up=ein("w_up", [DFF1, D]),
        attn1_h=eout("attn1_h", [L, D]), hgl_h=eout("hgl_h", [L, DFF1]), amax_h1=eout("amax_h1", [1, 1]),
    )
    ctx = ExitStack()
    with ctx:
        pb = PB(nc, ctx)
        cm = Common(pb, ctx)
        pctx = ExitStack()
        emit_l1b(pb, cm, pctx, io_b)
        pctx.close()
        pb.mem_barrier()
        pctx = ExitStack()
        emit_l1c(pb, cm, pctx, io_c)
        pctx.close()
        pb.emit()
    return nc


_NC_CACHE = {}


def _get_nc(name):
    if name not in _NC_CACHE:
        _NC_CACHE[name] = globals()[f"build_{name}"]()
    return _NC_CACHE[name]


def _amax8(results, key):
    return np.concatenate([r[key].reshape(1, 1) for r in results], axis=1).astype(np.float32)


def kernel_unfused(**inputs):
    import ml_dtypes
    f = lambda k: np.ascontiguousarray(np.asarray(inputs[k], dtype=np.float32))
    bfv = lambda a: a.view(ml_dtypes.bfloat16) if a.dtype == np.uint16 else a
    x = f("L0_attn_x")
    gamma = f("L0_attn_gamma").reshape(1, D)
    beta = f("L0_attn_beta").reshape(1, D)
    cores = list(range(8))

    r1 = run_bass_kernel_spmd(_get_nc("l0a"), [dict(x=x[c], wqkv=f("L0_attn_Wqkv"), gamma=gamma, beta=beta)
                                               for c in cores], core_ids=cores).results
    am1 = _amax8(r1, "amax_h")
    r2 = run_bass_kernel_spmd(_get_nc("l0b"), [dict(x=x[c], xn_h=r1[c]["xn_h"], q_h=bfv(r1[c]["q_h"]),
                                                    kv_h=r1[c]["kv_h"], ksum_h=r1[c]["ksum_h"],
                                                    stats_h=r1[c]["stats_h"], amax8=am1,
                                                    wout=f("L0_attn_Wout"), w_in=f("L0_ffn_W_in"),
                                                    b_in=f("L0_ffn_b_in").reshape(1, DFF0))
                                               for c in cores], core_ids=cores).results
    am2 = _amax8(r2, "amaxh2")
    r3 = run_bass_kernel_spmd(_get_nc("l0c"), [dict(part1_h=r2[c]["part1_h"], h2_h=r2[c]["h2_h"], amaxh2_8=am2,
                                                    w_out=f("L0_ffn_W_out"), b_out=f("L0_ffn_b_out").reshape(1, D))
                                               for c in cores], core_ids=cores).results
    r4 = run_bass_kernel_spmd(_get_nc("l1a"), [dict(xn1_h=r3[c]["xn1_h"], wq=f("L1_attn_Wq"), wk=f("L1_attn_Wk"),
                                                    wv=f("L1_attn_Wv"), lr_w=f("L1_attn_lr_weight"),
                                                    lr_b=f("L1_attn_lr_bias").reshape(1, H),
                                                    tnw=f("L1_attn_ttt_norm_weight").reshape(1, D),
                                                    tnb=f("L1_attn_ttt_norm_bias").reshape(1, D))
                                               for c in cores], core_ids=cores).results
    r5 = run_bass_kernel_spmd(_get_nc("l1b"), [dict(qT_h=bfv(r4[c]["qT_h"]), kT_h=bfv(r4[c]["kT_h"]),
                                                    ktok_h=bfv(r4[c]["ktok_h"]), qtok_h=r4[c]["qtok_h"],
                                                    tgt_h=r4[c]["tgt_h"], eta_h=r4[c]["eta_h"],
                                                    w1=f("L1_attn_W1"), b1=f("L1_attn_b1"),
                                                    w2=f("L1_attn_W2"), b2=f("L1_attn_b2"),
                                                    tnw=f("L1_attn_ttt_norm_weight").reshape(1, D),
                                                    tnb=f("L1_attn_ttt_norm_bias").reshape(1, D))
                                               for c in cores], core_ids=cores).results
    r6 = run_bass_kernel_spmd(_get_nc("l1c"), [dict(y_h=r5[c]["y_h"], xn1_h=r3[c]["xn1_h"],
                                                    amax8=_amax8(r3, "amax_xn1"),
                                                    wo=f("L1_attn_Wo"), pnw=f("L1_attn_post_norm_weight").reshape(1, D),
                                                    pnb=f("L1_attn_post_norm_bias").reshape(1, D),
                                                    clip_x=f("L1_ffn_clip_x").reshape(1, 1),
                                                    w_gate=f("L1_ffn_w_gate"), w_up=f("L1_ffn_w_up"))
                                               for c in cores], core_ids=cores).results
    am4 = _amax8(r6, "amax_h1")
    r7 = run_bass_kernel_spmd(_get_nc("l1d"), [dict(hgl_h=r6[c]["hgl_h"], amax8=am4,
                                                    clip_down=f("L1_ffn_clip_down").reshape(1, 1),
                                                    w_down=f("L1_ffn_w_down"), x1_h=r3[c]["x1_h"],
                                                    attn1_h=r6[c]["attn1_h"])
                                               for c in cores], core_ids=cores).results
    return np.stack([r7[c]["out_h"] for c in cores], axis=0)


def kernel_fused(**inputs):
    f = lambda k: np.ascontiguousarray(np.asarray(inputs[k], dtype=np.float32))
    x = f("L0_attn_x")
    nc = _get_nc("full")
    common = dict(
        wqkv=f("L0_attn_Wqkv"), gamma=f("L0_attn_gamma").reshape(1, D), beta=f("L0_attn_beta").reshape(1, D),
        wout=f("L0_attn_Wout"), w_in=f("L0_ffn_W_in"), b_in=f("L0_ffn_b_in").reshape(1, DFF0),
        w_out=f("L0_ffn_W_out"), b_out=f("L0_ffn_b_out").reshape(1, D),
        wq=f("L1_attn_Wq"), wk=f("L1_attn_Wk"), wv=f("L1_attn_Wv"),
        lr_w=f("L1_attn_lr_weight"), lr_b=f("L1_attn_lr_bias").reshape(1, H),
        tnw=f("L1_attn_ttt_norm_weight").reshape(1, D), tnb=f("L1_attn_ttt_norm_bias").reshape(1, D),
        w1=f("L1_attn_W1"), b1=f("L1_attn_b1"), w2=f("L1_attn_W2"), b2=f("L1_attn_b2"),
        wo=f("L1_attn_Wo"), pnw=f("L1_attn_post_norm_weight").reshape(1, D),
        pnb=f("L1_attn_post_norm_bias").reshape(1, D),
        clip_x=f("L1_ffn_clip_x").reshape(1, 1), clip_down=f("L1_ffn_clip_down").reshape(1, 1),
        w_gate=f("L1_ffn_w_gate"), w_up=f("L1_ffn_w_up"), w_down=f("L1_ffn_w_down"),
    )
    res = run_bass_kernel_spmd(nc, [dict(common, x=x[c]) for c in range(8)], core_ids=list(range(8)))
    return np.stack([res.results[c]["out_h"] for c in range(8)], axis=0)


def kernel5(**inputs):
    import ml_dtypes
    f = lambda k: np.ascontiguousarray(np.asarray(inputs[k], dtype=np.float32))
    bfv = lambda a: a.view(ml_dtypes.bfloat16) if a.dtype == np.uint16 else a
    x = f("L0_attn_x")
    gamma = f("L0_attn_gamma").reshape(1, D)
    beta = f("L0_attn_beta").reshape(1, D)
    cores = list(range(8))
    r1 = run_bass_kernel_spmd(_get_nc("l0a"), [dict(x=x[c], wqkv=f("L0_attn_Wqkv"), gamma=gamma, beta=beta)
                                               for c in cores], core_ids=cores).results
    am1 = _amax8(r1, "amax_h")
    r2 = run_bass_kernel_spmd(_get_nc("l0b"), [dict(x=x[c], xn_h=r1[c]["xn_h"], q_h=bfv(r1[c]["q_h"]),
                                                    kv_h=r1[c]["kv_h"], ksum_h=r1[c]["ksum_h"],
                                                    stats_h=r1[c]["stats_h"], amax8=am1,
                                                    wout=f("L0_attn_Wout"), w_in=f("L0_ffn_W_in"),
                                                    b_in=f("L0_ffn_b_in").reshape(1, DFF0))
                                               for c in cores], core_ids=cores).results
    am2 = _amax8(r2, "amaxh2")
    r3 = run_bass_kernel_spmd(_get_nc("g3"), [dict(part1_h=r2[c]["part1_h"], h2_h=r2[c]["h2_h"], amaxh2_8=am2,
                                                   w_out=f("L0_ffn_W_out"), b_out=f("L0_ffn_b_out").reshape(1, D),
                                                   wq=f("L1_attn_Wq"), wk=f("L1_attn_Wk"), wv=f("L1_attn_Wv"),
                                                   lr_w=f("L1_attn_lr_weight"),
                                                   lr_b=f("L1_attn_lr_bias").reshape(1, H),
                                                   tnw=f("L1_attn_ttt_norm_weight").reshape(1, D),
                                                   tnb=f("L1_attn_ttt_norm_bias").reshape(1, D))
                                              for c in cores], core_ids=cores).results
    am3 = _amax8(r3, "amax_xn1")
    r4 = run_bass_kernel_spmd(_get_nc("g4"), [dict(qT_h=bfv(r3[c]["qT_h"]), kT_h=bfv(r3[c]["kT_h"]),
                                                   ktok_h=bfv(r3[c]["ktok_h"]), qtok_h=r3[c]["qtok_h"],
                                                   tgt_h=r3[c]["tgt_h"], eta_h=r3[c]["eta_h"],
                                                   w1=f("L1_attn_W1"), b1=f("L1_attn_b1"),
                                                   w2=f("L1_attn_W2"), b2=f("L1_attn_b2"),
                                                   tnw=f("L1_attn_ttt_norm_weight").reshape(1, D),
                                                   tnb=f("L1_attn_ttt_norm_bias").reshape(1, D),
                                                   xn1_h=r3[c]["xn1_h"], amax8=am3, wo=f("L1_attn_Wo"),
                                                   pnw=f("L1_attn_post_norm_weight").reshape(1, D),
                                                   pnb=f("L1_attn_post_norm_bias").reshape(1, D),
                                                   clip_x=f("L1_ffn_clip_x").reshape(1, 1),
                                                   w_gate=f("L1_ffn_w_gate"), w_up=f("L1_ffn_w_up"))
                                              for c in cores], core_ids=cores).results
    am4 = _amax8(r4, "amax_h1")
    r5 = run_bass_kernel_spmd(_get_nc("l1d"), [dict(hgl_h=r4[c]["hgl_h"], amax8=am4,
                                                    clip_down=f("L1_ffn_clip_down").reshape(1, 1),
                                                    w_down=f("L1_ffn_w_down"), x1_h=r3[c]["x1_h"],
                                                    attn1_h=r4[c]["attn1_h"])
                                               for c in cores], core_ids=cores).results
    return np.stack([r5[c]["out_h"] for c in cores], axis=0)


def kernel(**inputs):
    return kernel5(**inputs)


# version 14 (final)
# speedup vs baseline: 1.0201x; 1.0201x over previous
import numpy as np
from contextlib import ExitStack

import concourse.bass as bass
import concourse.mybir as mybir
from concourse.bass_utils import run_bass_kernel_spmd

F32 = mybir.dt.float32
BF16 = mybir.dt.bfloat16
AF = mybir.ActivationFunctionType
ALU = mybir.AluOpType
AX = mybir.AxisListType
MAGIC = 12582912.0

B, L, D = 8, 4096, 1024
H, HD = 16, 64
NT = L // 128
DFF0 = 4096
DFF1 = 2816
CS = 64
NC = L // CS
LN_EPS0 = 1e-5
TTT_LN_EPS = 1e-6
DENOM_EPS = 1e-6
ROPE_THETA = 10000.0
GELU_C = float(np.sqrt(2.0 / np.pi))
GELU_A = 0.044715
SQRT_HALF = float(np.sqrt(0.5))

DBG = {}


class PB:
    CENG = ("t", "v", "s", "p")
    QUEUES = ("y", "s", "p")

    NDSEM = 10

    def __init__(self, nc, ctx):
        self.nc = nc
        self.ctx = ctx
        self.sem = {e: ctx.enter_context(nc.semaphore(f"es_{e}")) for e in self.CENG}
        self.dpool = {}
        self.dsem_by_key = {}
        for q in self.QUEUES:
            pool = []
            for j in range(self.NDSEM):
                key = f"ds_{q}_{j}"
                h = ctx.enter_context(nc.semaphore(key))
                pool.append({"key": key, "h": h, "val": 0})
                self.dsem_by_key[key] = h
            self.dpool[q] = pool
        self.nops = {e: 0 for e in self.CENG}
        self.last_comp = {e: 0 for e in self.CENG}
        self.ops = {e: [] for e in list(self.CENG) + ["y"]}
        self.miles = {e: [] for e in self.CENG}
        self.semval = {e: 0 for e in self.CENG}
        self.wr = {}
        self.rd = {}
        self.watermark = {e: {} for e in list(self.CENG) + []}
        self.bank_last = {}
        self.relax_same_engine = False
        self.late_ops = {e: set() for e in self.CENG}
        self.barrier_tokens = []

    LATE_OPS = {"reduce_sum", "reduce_max", "tensor_reduce", "iota", "affine_select", "memset",
                "reciprocal", "bn_stats", "bn_aggr"}

    def _wait_for_token(self, eng, tok):
        kind = tok[0]
        if kind == "d":
            _, key, v = tok
            if self.watermark[eng].get(key, 0) >= v:
                return None
            self.watermark[eng][key] = v
            return (self.dsem_by_key[key], v)
        else:
            _, peng, opidx = tok
            if peng == eng and self.relax_same_engine and opidx not in self.late_ops[peng]:
                return None
            if peng == eng and opidx > self.nops[eng]:
                raise RuntimeError("self-wait on future op")
            v = self._milestone(peng, opidx)
            key = "e" + peng
            if self.watermark[eng].get(key, 0) >= v:
                return None
            self.watermark[eng][key] = v
            return (self.sem[peng], v)

    def _milestone(self, eng, opidx):
        for idx, val in self.miles[eng]:
            if idx >= opidx:
                return val
        self.semval[eng] += 1
        val = self.semval[eng]
        self.ops[eng][opidx - 1][2].append((self.sem[eng], 1))
        self.miles[eng].append((opidx, val))
        return val

    def _record(self, eng_kind, token, reads, writes):
        for r in reads:
            self.rd.setdefault(r, {})[eng_kind] = token
        for w in writes:
            self.wr[w] = token
            self.rd[w] = {}

    def _gather_waits(self, eng, reads, writes, banks):
        waits = []
        seen = set()

        def add(tok):
            if tok is None:
                return
            w = self._wait_for_token(eng, tok)
            if w is not None and (id(w[0]), w[1]) not in seen:
                seen.add((id(w[0]), w[1]))
                waits.append(w)

        for tok in self.barrier_tokens:
            add(tok)
        for r in reads:
            add(self.wr.get(r))
        for w in writes:
            add(self.wr.get(w))
            for tok in self.rd.get(w, {}).values():
                add(tok)
        for b in banks:
            last = self.bank_last.get(b)
            if last is not None:
                last_eng, tok = last
                if last_eng != eng:
                    add(tok)
        return waits

    @staticmethod
    def _thunk(fn):
        if callable(fn):
            return fn
        name, kw = fn
        return lambda e: getattr(e, name)(**kw)

    def op(self, eng, fn, reads=(), writes=(), banks=()):
        assert eng in self.CENG
        is_late = (not callable(fn)) and (fn[0] in self.LATE_OPS or fn[1].get("accum_out") is not None)
        fn = self._thunk(fn)
        waits = self._gather_waits(eng, reads, writes, banks)
        self.ops[eng].append([fn, waits, []])
        self.nops[eng] += 1
        if is_late:
            self.late_ops[eng].add(self.nops[eng])
        tok = ("e", eng, self.nops[eng])
        self._record(("e", eng), tok, reads, writes)
        self.last_comp[eng] = self.nops[eng]
        for b in banks:
            self.bank_last[b] = (eng, tok)
        return tok

    def dma(self, q, fn, reads=(), writes=()):
        iss = {"y": "y", "s": "s", "p": "p"}[q]
        fn = self._thunk(fn)
        if iss not in self.watermark:
            self.watermark[iss] = {}
        waits = []
        seen = set()

        def addw(w):
            if w is not None and (id(w[0]), w[1]) not in seen:
                seen.add((id(w[0]), w[1]))
                waits.append(w)

        def add(tok):
            if tok is None:
                return
            addw(self._wait_for_token(iss, tok))

        for tok in self.barrier_tokens:
            add(tok)
        for r in reads:
            add(self.wr.get(r))
        for w_ in writes:
            add(self.wr.get(w_))
            for tok in self.rd.get(w_, {}).values():
                add(tok)
        pool = self.dpool[q]
        pick = None
        for ent in pool:
            if self.watermark[iss].get(ent["key"], 0) >= ent["val"]:
                pick = ent
                break
        if pick is None:
            pick = pool[0]
            addw(self._wait_for_token(iss, ("d", pick["key"], pick["val"])))
        pool.remove(pick)
        pool.append(pick)
        pick["val"] += 16
        tok = ("d", pick["key"], pick["val"])
        self.ops[iss].append([fn, waits, [(pick["h"], 16)]])
        if iss != "y":
            self.nops[iss] += 1
        self._record(("d", pick["key"]), tok, reads, writes)
        return tok

    def mem_barrier(self):
        toks = []
        for e in self.CENG:
            if self.last_comp[e] > 0:
                toks.append(("e", e, self.last_comp[e]))
        for q in self.QUEUES:
            for ent in self.dpool[q]:
                if ent["val"] > 0:
                    toks.append(("d", ent["key"], ent["val"]))
        self.barrier_tokens = toks

    def barrier_token(self, eng):
        if self.nops[eng] == 0:
            return None
        return ("e", eng, self.nops[eng])

    def finalize(self):
        for q in self.QUEUES:
            for ent in self.dpool[q]:
                if ent["val"] > 0:
                    self.ops["y"].append([None, [(ent["h"], ent["val"])], []])

    def emit(self):
        self.finalize()
        nc = self.nc
        engmap = {"t": nc.tensor, "v": nc.vector, "s": nc.scalar, "p": nc.gpsimd, "y": nc.sync}
        with nc.Block() as block:
            def make_section(ename):
                def section(eng):
                    for fn, waits, incs in self.ops[ename]:
                        if fn is None:
                            for smh, v in waits:
                                eng.wait_ge(smh, v)
                            continue
                        for smh, v in waits:
                            eng.wait_ge(smh, v)
                        ins = fn(eng)
                        for smh, n in incs:
                            ins.then_inc(smh, n)
                return section

            block.tensor(make_section("t"))
            block.vector(make_section("v"))
            block.scalar(make_section("s"))
            block.gpsimd(make_section("p"))
            block.sync(make_section("y"))


class Common:

    def __init__(self, pb, ctx):
        nc = pb.nc
        self.pb = pb
        self.consts = ctx.enter_context(nc.sbuf_tensor("cc_consts", [128, 8], F32))
        self.ones_row_b = ctx.enter_context(nc.sbuf_tensor("cc_ones_row", [1, 128], BF16))
        self.ident_b = ctx.enter_context(nc.sbuf_tensor("cc_ident_b", [128, 128], BF16))
        self.ones128_b = ctx.enter_context(nc.sbuf_tensor("cc_ones128", [128, 1], BF16))
        self._cmap = {}
        self._cfree = list(range(8))
        p = pb
        p.op("v", lambda e: e.memset(self.ones_row_b[:], 1.0), writes=["cc_ones_row"])
        p.op("v", lambda e: e.memset(self.ones128_b[:], 1.0), writes=["cc_ones128"])
        p.op("v", lambda e: e.memset(self.ident_b[:], 1.0), writes=["cc_ident_tmp"])
        p.op("p", lambda e: e.affine_select(
            out=self.ident_b[:], in_=self.ident_b[:], pattern=[[1, 128]],
            compare_op=ALU.is_equal, fill=0.0, base=0, channel_multiplier=-1),
            reads=["cc_ident_tmp"], writes=["cc_ident"])

    def const(self, val):
        val = float(val)
        if val not in self._cmap:
            idx = self._cfree.pop(0)
            self._cmap[val] = idx
            self.pb.op("v", lambda e, i=idx, v=val: e.memset(self.consts[:, i:i + 1], v),
                       writes=[f"cc_const_{idx}"])
        idx = self._cmap[val]
        return self.consts[:, idx:idx + 1]

    def const_region(self, val):
        return f"cc_const_{self._cmap[float(val)]}"


def bank_regions(*idxs):
    return [f"bk{i}" for i in idxs]




def build_l0a():
    nc = bass.Bass("TRN2", target_bir_lowering=False)
    x = nc.dram_tensor("x", [L, D], F32, kind="ExternalInput").ap()
    wqkv = nc.dram_tensor("wqkv", [D, 3 * D], F32, kind="ExternalInput").ap()
    gamma = nc.dram_tensor("gamma", [1, D], F32, kind="ExternalInput").ap()
    beta = nc.dram_tensor("beta", [1, D], F32, kind="ExternalInput").ap()
    xn_h = nc.dram_tensor("xn_h", [L, D], F32, kind="ExternalOutput").ap()
    q_h = nc.dram_tensor("q_h", [L, D], mybir.dt.uint16, kind="ExternalOutput").ap().bitcast(BF16)
    kv_h = nc.dram_tensor("kv_h", [64, 1024], F32, kind="ExternalOutput").ap()
    ksum_h = nc.dram_tensor("ksum_h", [1, 1024], F32, kind="ExternalOutput").ap()
    stats_h = nc.dram_tensor("stats_h", [NT, 128, 2], F32, kind="ExternalOutput").ap()
    amax_h = nc.dram_tensor("amax_h", [1, 1], F32, kind="ExternalOutput").ap()

    ctx = ExitStack()
    with ctx:
        pb = PB(nc, ctx)
        cm = Common(pb, ctx)
        emit_l0a(pb, cm, ctx, dict(x=x, wqkv=wqkv, gamma=gamma, beta=beta, xn_h=xn_h,
                                   q_h=q_h, kv_h=kv_h, ksum_h=ksum_h, stats_h=stats_h,
                                   amax_h=amax_h))
        pb.emit()
    return nc


def emit_l0a(pb, cm, ctx, io):
    nc = pb.nc
    sb = lambda name, shape, dt: ctx.enter_context(nc.sbuf_tensor(name, shape, dt))
    ps = lambda name, shape, dt: ctx.enter_context(nc.psum_tensor("pa" + name, shape, dt))
    cm.const(LN_EPS0)

    wq_f = sb("l0a_wq_f", [128, 3 * D], F32)
    wq_b = [sb(f"l0a_wqb{k}", [128, 3 * D], BF16) for k in range(8)]
    for k in range(8):
        pb.dma("y", ("dma_start", dict(out=wq_f[:], in_=io["wqkv"][k * 128:(k + 1) * 128, :])),
               reads=["h_wqkv"], writes=["l0a_wq_f"])
        pb.op("v", ("tensor_copy", dict(out=wq_b[k][:], in_=wq_f[:])),
              reads=["l0a_wq_f"], writes=[f"l0a_wqb{k}"])

    gb_row = sb("l0a_gb_row", [1, 2 * D], F32)
    pb.dma("y", ("dma_start", dict(out=gb_row[:, 0:D], in_=io["gamma"])), reads=["h_gamma"], writes=["l0a_gb_row"])
    pb.dma("y", ("dma_start", dict(out=gb_row[:, D:2 * D], in_=io["beta"])), reads=["h_beta"], writes=["l0a_gb_row"])
    gamma_b = sb("l0a_gamma_b", [128, D], F32)
    beta_b = sb("l0a_beta_b", [128, D], F32)
    ones_row_f = sb("l0a_ones_row_f", [1, 128], F32)
    pb.op("v", ("memset", dict(ap=ones_row_f[:], constant=1.0)), writes=["l0a_ones_row_f"])
    bank = [ps(f"bk{i}", [128, 512], F32) for i in range(8)]
    bank_bf = [b[:].bitcast(BF16) for b in bank]
    for dst, rdst, off in ((gamma_b, "l0a_gamma_b", 0), (beta_b, "l0a_beta_b", D)):
        for j in range(2):
            pb.op("t", ("matmul", dict(out=bank[7][:, 0:512], lhsT=ones_row_f[:],
                                       rhs=gb_row[:, off + j * 512:off + (j + 1) * 512], start=True, stop=True)),
                  reads=["l0a_ones_row_f", "l0a_gb_row"], banks=bank_regions(7))
            pb.op("v", ("tensor_copy", dict(out=dst[:, j * 512:(j + 1) * 512], in_=bank[7][:, 0:512])),
                  writes=[rdst], banks=bank_regions(7))

    xt = [sb(f"l0a_xt{i}", [128, D], F32) for i in range(2)]
    xn = [sb(f"l0a_xn{i}", [128, D], F32) for i in range(2)]
    xn2b = [sb(f"l0a_xn2b{i}", [128, D], BF16) for i in range(2)]
    xT = [sb(f"l0a_xT{i}", [128, D], BF16) for i in range(2)]
    qb = [sb(f"l0a_qb{i}", [128, D], BF16) for i in range(2)]
    kb = [sb(f"l0a_kb{i}", [128, D], BF16) for i in range(2)]
    vb = [sb(f"l0a_vb{i}", [128, D], BF16) for i in range(2)]
    reluP = [sb(f"l0a_relu{j}", [128, 2 * D], F32) for j in range(2)]
    mintP = [sb(f"l0a_min{j}", [128, 2 * D], F32) for j in range(2)]
    exptP = [sb(f"l0a_exp{j}", [128, 2 * D], F32) for j in range(2)]
    statP = [sb(f"l0a_stat{j}", [128, 10], F32) for j in range(2)]
    sqP = [sb(f"l0a_sq{j}", [128, D], F32) for j in range(2)]
    amax_run = sb("l0a_amax_run", [128, 1], F32)
    kv_acc = sb("l0a_kv_acc", [64, 1024], F32)
    ksum_acc = sb("l0a_ksum_acc", [1, 1024], F32)
    stats_out = [sb(f"l0a_stats_out{i}", [128, 2], F32) for i in range(2)]
    pb.op("v", ("memset", dict(ap=amax_run[:], constant=0.0)), writes=["l0a_amax_run"])

    inv_d = 1.0 / D

    def ln_stats(src_ap, rsrc):
        pb.op("v", ("reduce_sum", dict(out=stat[:, 0:1], in_=src_ap, axis=AX.X)),
              reads=[rsrc], writes=[f"st_s1{sfx}"])
        pb.op("s", ("activation", dict(out=sq[:], in_=src_ap, func=AF.Square, scale=1.0)),
              reads=[rsrc], writes=[f"l0a_sq{sfx}"])
        pb.op("v", ("reduce_sum", dict(out=stat[:, 1:2], in_=sq[:], axis=AX.X)),
              reads=[f"l0a_sq{sfx}"], writes=[f"st_s2{sfx}"])
        pb.op("v", ("tensor_scalar", dict(out=stat[:, 2:3], in0=stat[:, 0:1], scalar1=inv_d, scalar2=None, op0=ALU.mult)),
              reads=[f"st_s1{sfx}"], writes=[f"st_mu{sfx}"])
        pb.op("v", ("tensor_scalar", dict(out=stat[:, 3:4], in0=stat[:, 2:3], scalar1=stat[:, 2:3], scalar2=-1.0, op0=ALU.mult, op1=ALU.mult)),
              reads=[f"st_mu{sfx}"], writes=[f"st_nmu2{sfx}"])
        pb.op("v", ("tensor_scalar", dict(out=stat[:, 4:5], in0=stat[:, 1:2], scalar1=inv_d, scalar2=None, op0=ALU.mult)),
              reads=[f"st_s2{sfx}"], writes=[f"st_ex2{sfx}"])
        pb.op("v", ("tensor_tensor", dict(out=stat[:, 4:5], in0=stat[:, 4:5], in1=stat[:, 3:4], op=ALU.add)),
              reads=[f"st_ex2{sfx}", f"st_nmu2{sfx}"], writes=[f"st_var{sfx}"])
        pb.op("s", ("activation", dict(out=stat[:, 5:6], in_=stat[:, 4:5], func=AF.Sqrt, bias=cm.const(LN_EPS0), scale=1.0)),
              reads=[f"st_var{sfx}", cm.const_region(LN_EPS0)], writes=[f"st_sd{sfx}"])
        pb.op("v", ("reciprocal", dict(out=stat[:, 6:7], in_=stat[:, 5:6])),
              reads=[f"st_sd{sfx}"], writes=[f"st_rstd{sfx}"])
        pb.op("v", ("tensor_scalar", dict(out=stat[:, 7:8], in0=stat[:, 2:3], scalar1=stat[:, 6:7], scalar2=-1.0, op0=ALU.mult, op1=ALU.mult)),
              reads=[f"st_mu{sfx}", f"st_rstd{sfx}"], writes=[f"st_nmr{sfx}"])

    for i in range(NT):
        db = i % 2
        xt_i, xn_i, xn2b_i, xT_i = xt[db], xn[db], xn2b[db], xT[db]
        qb_i, kb_i, vb_i, so_i = qb[db], kb[db], vb[db], stats_out[db]
        rxt, rxn, rxn2b, rxT = f"l0a_xt{db}", f"l0a_xn{db}", f"l0a_xn2b{db}", f"l0a_xT{db}"
        rqb, rkb, rvb, rso = f"l0a_qb{db}", f"l0a_kb{db}", f"l0a_vb{db}", f"l0a_stats_out{db}"
        relu_t, mint, expt, stat, sq = reluP[db], mintP[db], exptP[db], statP[db], sqP[db]
        sfx = str(db)

        pb.dma("y", ("dma_start", dict(out=xt_i[:], in_=io["x"][i * 128:(i + 1) * 128, :])),
               reads=["h_x"], writes=[rxt])

        ln_stats(xt_i[:], rxt)
        pb.op("s", ("activation", dict(out=xn_i[:], in_=xt_i[:], func=AF.Identity, bias=stat[:, 7:8], scale=stat[:, 6:7])),
              reads=[rxt, f"st_nmr{sfx}", f"st_rstd{sfx}"], writes=[rxn])
        pb.op("v", ("tensor_copy", dict(out=so_i[:, 0:1], in_=stat[:, 2:3])), reads=[f"st_mu{sfx}"], writes=[rso])
        pb.op("v", ("tensor_copy", dict(out=so_i[:, 1:2], in_=stat[:, 6:7])), reads=[f"st_rstd{sfx}"], writes=[rso])
        pb.dma("p", ("dma_start", dict(out=io["stats_h"][i], in_=so_i[:])), reads=[rso], writes=["h_stats"])
        pb.op("v", ("tensor_reduce", dict(out=stat[:, 8:9], in_=xn_i[:], axis=AX.X, op=ALU.max, apply_absolute_value=True)),
              reads=[rxn], writes=[f"st_am{sfx}"])
        pb.op("v", ("tensor_tensor", dict(out=amax_run[:], in0=amax_run[:], in1=stat[:, 8:9], op=ALU.max)),
              reads=[f"st_am{sfx}", "l0a_amax_run"], writes=["l0a_amax_run"])
        pb.dma("y", ("dma_start", dict(out=io["xn_h"][i * 128:(i + 1) * 128, :], in_=xn_i[:])),
               reads=[rxn], writes=["h_xn"])

        ln_stats(xn_i[:], rxn)
        pb.op("s", ("activation", dict(out=sq[:], in_=xn_i[:], func=AF.Identity, bias=stat[:, 7:8], scale=stat[:, 6:7])),
              reads=[rxn, f"st_nmr{sfx}", f"st_rstd{sfx}"], writes=[f"l0a_sq{sfx}"])
        pb.op("v", ("tensor_tensor", dict(out=sq[:], in0=sq[:], in1=gamma_b[:], op=ALU.mult)),
              reads=[f"l0a_sq{sfx}", "l0a_gamma_b"], writes=[f"l0a_sq{sfx}"])
        pb.op("v", ("tensor_tensor", dict(out=xn2b_i[:], in0=sq[:], in1=beta_b[:], op=ALU.add)),
              reads=[f"l0a_sq{sfx}", "l0a_beta_b"], writes=[rxn2b])

        for c in range(8):
            pb.op("t", ("transpose", dict(out=bank_bf[0][:, c * 128:(c + 1) * 128],
                                          in_=xn2b_i[:, c * 128:(c + 1) * 128], identity=cm.ident_b[:])),
                  reads=[rxn2b, "cc_ident"], banks=bank_regions(0))
        pb.op("s", ("copy", dict(out=xT_i[:], in_=bank_bf[0][:, 0:1024])),
              writes=[rxT], banks=bank_regions(0))

        for nb in range(6):
            for k in range(8):
                pb.op("t", ("matmul", dict(out=bank[nb][:, 0:512], lhsT=xT_i[:, k * 128:(k + 1) * 128],
                                           rhs=wq_b[k][:, nb * 512:(nb + 1) * 512],
                                           start=(k == 0), stop=(k == 7))),
                      reads=[rxT, f"l0a_wqb{k}"], banks=bank_regions(nb))

        for half, (b0_, b1_) in enumerate([(0, 1), (2, 3)]):
            dst = qb_i if half == 0 else kb_i
            rdst = rqb if half == 0 else rkb
            for j, b_ in enumerate((b0_, b1_)):
                pb.op("s", ("activation", dict(out=relu_t[:, half * D + j * 512: half * D + (j + 1) * 512],
                                               in_=bank[b_][:, 0:512], func=AF.Relu, scale=1.0)),
                      writes=[f"l0a_relu{sfx}"], banks=bank_regions(b_))
                pb.op("v", ("tensor_scalar", dict(out=mint[:, half * D + j * 512: half * D + (j + 1) * 512],
                                                  in0=bank[b_][:, 0:512], scalar1=0.0, scalar2=None, op0=ALU.min)),
                      writes=[f"l0a_min{sfx}"], banks=bank_regions(b_))
            pb.op("s", ("activation", dict(out=expt[:, half * D:(half + 1) * D], in_=mint[:, half * D:(half + 1) * D],
                                           func=AF.Exp, scale=1.0)),
                  reads=[f"l0a_min{sfx}"], writes=[f"l0a_exp{sfx}"])
            pb.op("v", ("tensor_tensor", dict(out=dst[:], in0=expt[:, half * D:(half + 1) * D],
                                              in1=relu_t[:, half * D:(half + 1) * D], op=ALU.add)),
                  reads=[f"l0a_exp{sfx}", f"l0a_relu{sfx}"], writes=[rdst])
        pb.op("s", ("copy", dict(out=vb_i[:, 0:512], in_=bank[4][:, 0:512])), writes=[rvb], banks=bank_regions(4))
        pb.op("s", ("copy", dict(out=vb_i[:, 512:1024], in_=bank[5][:, 0:512])), writes=[rvb], banks=bank_regions(5))

        pb.dma("p", ("dma_start", dict(out=io["q_h"][i * 128:(i + 1) * 128, :], in_=qb_i[:])),
               reads=[rqb], writes=["h_q"])

        for bnk in (6, 7):
            for j in range(8):
                h_ = 2 * j + (bnk - 6)
                pb.op("t", ("matmul", dict(out=bank[bnk][0:64, j * 64:(j + 1) * 64],
                                           lhsT=kb_i[:, h_ * 64:(h_ + 1) * 64], rhs=vb_i[:, h_ * 64:(h_ + 1) * 64],
                                           start=(j == 0), stop=(j == 7))),
                      reads=[rkb, rvb], banks=bank_regions(bnk))
            jj = bnk - 6
            pb.op("t", ("matmul", dict(out=bank[bnk][64:65, 0:512], lhsT=cm.ones128_b[:],
                                       rhs=kb_i[:, jj * 512:(jj + 1) * 512], start=True, stop=True)),
                  reads=[rkb, "cc_ones128"], banks=bank_regions(bnk))
            if i == 0:
                pb.op("v", ("tensor_copy", dict(out=kv_acc[0:64, (bnk - 6) * 512:(bnk - 5) * 512], in_=bank[bnk][0:64, 0:512])),
                      writes=["l0a_kv_acc"], banks=bank_regions(bnk))
                pb.op("v", ("tensor_copy", dict(out=ksum_acc[0:1, (bnk - 6) * 512:(bnk - 5) * 512], in_=bank[bnk][64:65, 0:512])),
                      writes=["l0a_ksum_acc"], banks=bank_regions(bnk))
            else:
                pb.op("v", ("tensor_tensor", dict(out=kv_acc[0:64, (bnk - 6) * 512:(bnk - 5) * 512],
                                                  in0=kv_acc[0:64, (bnk - 6) * 512:(bnk - 5) * 512],
                                                  in1=bank[bnk][0:64, 0:512], op=ALU.add)),
                      reads=["l0a_kv_acc"], writes=["l0a_kv_acc"], banks=bank_regions(bnk))
                pb.op("v", ("tensor_tensor", dict(out=ksum_acc[0:1, (bnk - 6) * 512:(bnk - 5) * 512],
                                                  in0=ksum_acc[0:1, (bnk - 6) * 512:(bnk - 5) * 512],
                                                  in1=bank[bnk][64:65, 0:512], op=ALU.add)),
                      reads=["l0a_ksum_acc"], writes=["l0a_ksum_acc"], banks=bank_regions(bnk))

    pb.dma("y", ("dma_start", dict(out=io["kv_h"], in_=kv_acc[:])), reads=["l0a_kv_acc"], writes=["h_kv"])
    pb.dma("y", ("dma_start", dict(out=io["ksum_h"], in_=ksum_acc[:])), reads=["l0a_ksum_acc"], writes=["h_ksum"])
    am_scratch = nc.dram_tensor("am_scratch", [128], F32, kind="Internal").ap()
    am_row = sb("l0a_am_row", [1, 128], F32)
    am_out = sb("l0a_am_out", [1, 1], F32)
    pb.dma("y", ("dma_start", dict(out=am_scratch.rearrange("(a b) -> a b", a=128), in_=amax_run[:])),
           reads=["l0a_amax_run"], writes=["h_am_scratch"])
    pb.dma("y", ("dma_start", dict(out=am_row[:], in_=am_scratch.rearrange("(a b) -> a b", a=1))),
           reads=["h_am_scratch"], writes=["l0a_am_row"])
    pb.op("v", ("reduce_max", dict(out=am_out[:], in_=am_row[:], axis=AX.X)), reads=["l0a_am_row"], writes=["l0a_am_out"])
    pb.dma("y", ("dma_start", dict(out=io["amax_h"], in_=am_out[:])), reads=["l0a_am_out"], writes=["h_amax"])


def build_l0b():
    nc = bass.Bass("TRN2", target_bir_lowering=False)
    io = dict(
        x=nc.dram_tensor("x", [L, D], F32, kind="ExternalInput").ap(),
        xn_h=nc.dram_tensor("xn_h", [L, D], F32, kind="ExternalInput").ap(),
        q_h=nc.dram_tensor("q_h", [L, D], BF16, kind="ExternalInput").ap(),
        kv_h=nc.dram_tensor("kv_h", [64, 1024], F32, kind="ExternalInput").ap(),
        ksum_h=nc.dram_tensor("ksum_h", [1, 1024], F32, kind="ExternalInput").ap(),
        stats_h=nc.dram_tensor("stats_h", [NT, 128, 2], F32, kind="ExternalInput").ap(),
        amax8=nc.dram_tensor("amax8", [1, 8], F32, kind="ExternalInput").ap(),
        wout=nc.dram_tensor("wout", [D, D], F32, kind="ExternalInput").ap(),
        w_in=nc.dram_tensor("w_in", [D, DFF0], F32, kind="ExternalInput").ap(),
        b_in=nc.dram_tensor("b_in", [1, DFF0], F32, kind="ExternalInput").ap(),
        part1_h=nc.dram_tensor("part1_h", [L, D], F32, kind="ExternalOutput").ap(),
        h2_h=nc.dram_tensor("h2_h", [L, DFF0], F32, kind="ExternalOutput").ap(),
        amaxh2=nc.dram_tensor("amaxh2", [1, 1], F32, kind="ExternalOutput").ap(),
    )
    ctx = ExitStack()
    with ctx:
        pb = PB(nc, ctx)
        cm = Common(pb, ctx)
        emit_l0b(pb, cm, ctx, io)
        pb.emit()
    return nc


def emit_l0b(pb, cm, ctx, io):
    nc = pb.nc
    sb = lambda name, shape, dt: ctx.enter_context(nc.sbuf_tensor(name, shape, dt))
    ps = lambda name, shape, dt: ctx.enter_context(nc.psum_tensor("pb_" + name, shape, dt))
    cm.const(MAGIC)
    cm.const(-MAGIC)

    bank = [ps(f"bk{i}", [128, 512], F32) for i in range(8)]
    bank_bf = [b[:].bitcast(BF16) for b in bank]
    ident_f = sb("l0b_ident_f", [128, 128], F32)
    pb.op("v", ("memset", dict(ap=ident_f[:], constant=1.0)), writes=["idf_tmp"])
    pb.op("p", ("affine_select", dict(out=ident_f[:], in_=ident_f[:], pattern=[[1, 128]],
                                      compare_op=ALU.is_equal, fill=0.0, base=0, channel_multiplier=-1)),
          reads=["idf_tmp"], writes=["idf"])
    ones_row_f = sb("l0b_ones_row_f", [1, 128], F32)
    pb.op("v", ("memset", dict(ap=ones_row_f[:], constant=1.0)), writes=["ones_row_f"])

    am8 = sb("l0b_am8", [1, 8], F32)
    sxr = sb("l0b_sxr", [1, 2], F32)
    sx_b = sb("l0b_sx_b", [128, 2], F32)
    kvb = sb("l0b_kvb", [64, 1024], BF16)
    ksum_b = sb("l0b_ksum_b", [128, 1024], F32)
    wout_b = [sb(f"l0b_woutb{k}", [128, D], BF16) for k in range(8)]
    bin_hi = sb("l0b_bin_hi", [1, DFF0], BF16)
    bin_lo = sb("l0b_bin_lo", [1, DFF0], BF16)
    win_b = [sb(f"l0b_winb{k}", [128, DFF0], BF16) for k in range(8)]

    amax_col_scr = nc.dram_tensor("amax_col_scr", [DFF0], F32, kind="Internal").ap()
    pre = ExitStack()
    psb = lambda name, shape, dt: pre.enter_context(nc.sbuf_tensor(name, shape, dt))
    wstg = [psb("l0b_wstg0", [128, DFF0], F32)]
    colmax = psb("l0b_colmax", [128, DFF0], F32)
    is_bcast = psb("l0b_is_bcast", [128, DFF0], F32)
    s_bcast = psb("l0b_s_bcast", [128, DFF0], F32)
    amax_row = psb("l0b_amax_row", [1, DFF0], F32)
    cred = psb("l0b_cred", [128, 2], F32)
    kvf = colmax[0:64, 0:1024]
    ksum_row = colmax[0:1, 1024:2048]
    wld = colmax[:, 2048:3072]

    pb.dma("y", ("dma_start", dict(out=am8[:], in_=io["amax8"])), reads=["h_amax8"], writes=["am8"])
    pb.op("v", ("reduce_max", dict(out=sxr[:, 0:1], in_=am8[:], axis=AX.X)), reads=["am8"], writes=["am_g"])
    pb.op("v", ("tensor_scalar", dict(out=sxr[:, 1:2], in0=sxr[:, 0:1], scalar1=1.0 / 32767.0, scalar2=None, op0=ALU.mult)),
          reads=["am_g"], writes=["sxr_inv"])
    pb.op("v", ("reciprocal", dict(out=sxr[:, 0:1], in_=sxr[:, 1:2])),
          reads=["sxr_inv"], writes=["sxr_s"])
    pb.op("t", ("matmul", dict(out=bank[0][:, 0:2], lhsT=ones_row_f[:], rhs=sxr[:, 0:2], start=True, stop=True)),
          reads=["sxr_s", "sxr_inv", "ones_row_f"], banks=bank_regions(0))
    pb.op("v", ("tensor_copy", dict(out=sx_b[:], in_=bank[0][:, 0:2])), writes=["sx_b"], banks=bank_regions(0))

    pb.dma("y", ("dma_start", dict(out=kvf, in_=io["kv_h"])), reads=["h_kv"], writes=["colmax"])
    pb.op("v", ("tensor_copy", dict(out=kvb[:], in_=kvf)), reads=["colmax"], writes=["kvb"])
    pb.dma("y", ("dma_start", dict(out=ksum_row, in_=io["ksum_h"])), reads=["h_ksum"], writes=["colmax"])
    for j in range(2):
        pb.op("t", ("matmul", dict(out=bank[1][:, 0:512], lhsT=ones_row_f[:], rhs=ksum_row[:, bass.ts(j, 512)],
                                   start=True, stop=True)),
              reads=["colmax", "ones_row_f"], banks=bank_regions(1))
        pb.op("v", ("tensor_copy", dict(out=ksum_b[:, j * 512:(j + 1) * 512], in_=bank[1][:, 0:512])),
              writes=["ksum_b"], banks=bank_regions(1))

    for k in range(8):
        pb.dma("y", ("dma_start", dict(out=wld, in_=io["wout"][k * 128:(k + 1) * 128, :])),
               reads=["h_wout"], writes=["colmax"])
        pb.op("v", ("tensor_copy", dict(out=wout_b[k][:], in_=wld)), reads=["colmax"], writes=[f"woutb{k}"])

    binrow = wstg[0][0:1, :]
    pb.dma("y", ("dma_start", dict(out=binrow, in_=io["b_in"])), reads=["h_bin"], writes=["wstg0"])
    pb.op("v", ("tensor_copy", dict(out=bin_hi[:], in_=binrow)), reads=["wstg0"], writes=["bin_hi"])
    pb.op("v", ("tensor_tensor", dict(out=binrow, in0=binrow, in1=bin_hi[:], op=ALU.subtract)),
          reads=["wstg0", "bin_hi"], writes=["wstg0"])
    pb.op("v", ("tensor_copy", dict(out=bin_lo[:], in_=binrow)), reads=["wstg0"], writes=["bin_lo"])

    for k in range(8):
        j = 0
        pb.dma("y", ("dma_start", dict(out=wstg[j][:], in_=io["w_in"][k * 128:(k + 1) * 128, :])),
               reads=["h_win"], writes=[f"wstg{j}"])
        pb.op("s", ("activation", dict(out=wstg[j][:], in_=wstg[j][:], func=AF.Abs, scale=1.0)),
              reads=[f"wstg{j}"], writes=[f"wstg{j}"])
        if k == 0:
            pb.op("v", ("tensor_copy", dict(out=colmax[:], in_=wstg[0][:])),
                  reads=["wstg0"], writes=["colmax"])
        else:
            pb.op("v", ("tensor_tensor", dict(out=colmax[:], in0=colmax[:], in1=wstg[j][:], op=ALU.max)),
                  reads=["colmax", f"wstg{j}"], writes=["colmax"])
    for c in range(32):
        pb.op("t", ("transpose", dict(out=bank[2][:, 0:128], in_=colmax[:, c * 128:(c + 1) * 128], identity=ident_f[:])),
              reads=["colmax", "idf"], banks=bank_regions(2))
        pb.op("v", ("reduce_max", dict(out=cred[:, 0:1], in_=bank[2][:, 0:128], axis=AX.X, apply_absolute_value=True)),
              writes=["cred"], banks=bank_regions(2))
        pb.dma("p", ("dma_start", dict(out=amax_col_scr[c * 128:(c + 1) * 128].rearrange("(a b) -> a b", a=128),
                                       in_=cred[:, 0:1])),
               reads=["cred"], writes=["h_amax_col"])
    pb.dma("y", ("dma_start", dict(out=amax_row[:], in_=amax_col_scr.rearrange("(a b) -> a b", a=1))),
           reads=["h_amax_col"], writes=["amax_row"])
    pb.op("v", ("tensor_scalar", dict(out=amax_row[:], in0=amax_row[:], scalar1=1e-8, scalar2=1.0 / 127.0,
                                      op0=ALU.max, op1=ALU.mult)),
          reads=["amax_row"], writes=["amax_row"])
    for j in range(8):
        pb.op("t", ("matmul", dict(out=bank[3][:, 0:512], lhsT=ones_row_f[:], rhs=amax_row[:, j * 512:(j + 1) * 512],
                                   start=True, stop=True)),
              reads=["amax_row", "ones_row_f"], banks=bank_regions(3))
        pb.op("v", ("tensor_copy", dict(out=is_bcast[:, j * 512:(j + 1) * 512], in_=bank[3][:, 0:512])),
              writes=["is_bcast"], banks=bank_regions(3))
    pb.op("v", ("reciprocal", dict(out=s_bcast[:], in_=is_bcast[:])), reads=["is_bcast"], writes=["s_bcast"])
    for k in range(8):
        j = 0
        rk = "wstg0"
        pb.dma("y", ("dma_start", dict(out=wstg[j][:], in_=io["w_in"][k * 128:(k + 1) * 128, :])),
               reads=["h_win"], writes=[rk])
        pb.op("v", ("tensor_tensor", dict(out=wstg[j][:], in0=wstg[j][:], in1=s_bcast[:], op=ALU.mult)),
              reads=[rk, "s_bcast"], writes=[rk])
        pb.op("s", ("activation", dict(out=wstg[j][:], in_=wstg[j][:], func=AF.Identity, bias=cm.const(MAGIC), scale=1.0)),
              reads=[rk, cm.const_region(MAGIC)], writes=[rk])
        pb.op("s", ("activation", dict(out=wstg[j][:], in_=wstg[j][:], func=AF.Identity, bias=cm.const(-MAGIC), scale=1.0)),
              reads=[rk, cm.const_region(-MAGIC)], writes=[rk])
        pb.op("v", ("tensor_scalar", dict(out=wstg[j][:], in0=wstg[j][:], scalar1=127.0, scalar2=-128.0, op0=ALU.min, op1=ALU.max)),
              reads=[rk], writes=[rk])
        pb.op("v", ("tensor_tensor", dict(out=win_b[k][:], in0=wstg[j][:], in1=is_bcast[:], op=ALU.mult)),
              reads=[rk, "is_bcast"], writes=[f"winb{k}"])
    pre.close()
    pb.mem_barrier()

    xt = [sb(f"l0b_xt{i}", [128, D], F32) for i in range(2)]
    xnt = [sb(f"l0b_xnt{i}", [128, D], F32) for i in range(2)]
    qt = [sb(f"l0b_qt{i}", [128, D], BF16) for i in range(2)]
    stt = [sb(f"l0b_stt{i}", [128, 2], F32) for i in range(2)]
    h2t = [sb(f"l0b_h2t{i}", [128, DFF0], F32) for i in range(2)]
    p1t = [sb(f"l0b_p1t{i}", [128, D], F32) for i in range(2)]
    qTP = [sb(f"l0b_qT{j}", [64, 2 * D], BF16) for j in range(2)]
    aobP = [sb(f"l0b_aob{j}", [128, D], BF16) for j in range(2)]
    aoTP = [sb(f"l0b_aoT{j}", [128, D], BF16) for j in range(2)]
    xsbP = [sb(f"l0b_xsb{j}", [128, D], BF16) for j in range(2)]
    xsTP = [sb(f"l0b_xsT{j}", [128, D], BF16) for j in range(2)]
    xxnP = [sb(f"l0b_xxn{j}", [128, D], F32) for j in range(2)]
    quantP = [sb(f"l0b_quant{j}", [128, D], F32) for j in range(2)]
    dumpP = [sb(f"l0b_dump{j}", [128, 64], F32) for j in range(2)]
    denP = [sb(f"l0b_den{j}", [128, 36], F32) for j in range(2)]
    amax_run = sb("l0b_amax_run", [128, 1], F32)
    amr = sb("l0b_amr", [128, 1], F32)
    pb.op("v", ("memset", dict(ap=amax_run[:], constant=0.0)), writes=["am_run"])

    for i in range(DBG.get("l0b_nt", NT)):
        db = i % 2
        xt_i, xnt_i, qt_i, stt_i, h2t_i, p1t_i = xt[db], xnt[db], qt[db], stt[db], h2t[db], p1t[db]
        rxt, rxn, rqt, rst, rh2, rp1 = (f"l0b_xt{db}", f"l0b_xnt{db}", f"l0b_qt{db}",
                                        f"l0b_stt{db}", f"l0b_h2t{db}", f"l0b_p1t{db}")
        qT, aob, aoT, xsb, xsT, xxn, quant, dump, den = (qTP[db], aobP[db], aoTP[db], xsbP[db],
                                                         xsTP[db], xxnP[db], quantP[db], dumpP[db], denP[db])
        sfx = str(db)
        sl = slice(i * 128, (i + 1) * 128)
        pb.dma("y", ("dma_start", dict(out=xnt_i[:], in_=io["xn_h"][sl, :])), reads=["h_xn"], writes=[rxn])
        pb.dma("y", ("dma_start", dict(out=qt_i[:], in_=io["q_h"][sl, :])), reads=["h_q"], writes=[rqt])
        pb.dma("p", ("dma_start", dict(out=xt_i[:], in_=io["x"][sl, :])), reads=["h_x"], writes=[rxt])
        pb.dma("p", ("dma_start", dict(out=stt_i[:], in_=io["stats_h"][i])), reads=["h_stats"], writes=[rst])

        skip_attn = DBG.get("l0b_skip_attn", False)
        skip_ffn = DBG.get("l0b_skip_ffn", False)
        if not skip_ffn:
            pb.op("v", ("tensor_scalar", dict(out=quant[:], in0=xnt_i[:], scalar1=sx_b[:, 0:1], scalar2=MAGIC,
                                              op0=ALU.mult, op1=ALU.add)),
                  reads=[rxn, "sx_b"], writes=[f"quant{sfx}"])
            pb.op("s", ("activation", dict(out=quant[:], in_=quant[:], func=AF.Identity, bias=cm.const(-MAGIC), scale=1.0)),
                  reads=[f"quant{sfx}", cm.const_region(-MAGIC)], writes=[f"quant{sfx}"])
            pb.op("v", ("tensor_scalar", dict(out=quant[:], in0=quant[:], scalar1=32767.0, scalar2=-32768.0,
                                              op0=ALU.min, op1=ALU.max)),
                  reads=[f"quant{sfx}"], writes=[f"quant{sfx}"])
            pb.op("v", ("tensor_scalar", dict(out=xsb[:], in0=quant[:], scalar1=sx_b[:, 1:2], scalar2=None, op0=ALU.mult)),
                  reads=[f"quant{sfx}", "sx_b"], writes=[f"xsb{sfx}"])
        if not skip_attn:
            upto = DBG.get("l0b_attn_upto", 99)
            if upto >= 1:
                for h in range(16):
                    tb = 0 if h < 8 else 5
                    pb.op("t", ("transpose", dict(out=bank_bf[tb][0:64, (h % 8) * 128:(h % 8 + 1) * 128],
                                                  in_=qt_i[:, h * 64:(h + 1) * 64], identity=cm.ident_b[:])),
                          reads=[rqt, "cc_ident"], banks=bank_regions(tb))
                pb.op("s", ("copy", dict(out=qT[:, 0:1024], in_=bank_bf[0][0:64, 0:1024])), writes=[f"qT{sfx}"], banks=bank_regions(0))
                pb.op("s", ("copy", dict(out=qT[:, 1024:2048], in_=bank_bf[5][0:64, 0:1024])), writes=[f"qT{sfx}"], banks=bank_regions(5))
            for h in range(16 if upto >= 2 else 0):
                bnk = 1 + h // 8
                pb.op("t", ("matmul", dict(out=bank[bnk][:, (h % 8) * 64:(h % 8 + 1) * 64],
                                           lhsT=qT[:, h * 128:(h + 1) * 128],
                                           rhs=kvb[:, (h % 2) * 512 + (h // 2) * 64:(h % 2) * 512 + (h // 2 + 1) * 64],
                                           start=True, stop=True)),
                      reads=[f"qT{sfx}", "kvb"], banks=bank_regions(bnk))
            for h in range(16 if upto >= 3 else 0):
                pb.op("v", ("scalar_tensor_tensor", dict(out=dump[:], in0=qt_i[:, h * 64:(h + 1) * 64], scalar=1.0,
                                                         in1=ksum_b[:, h * 64:(h + 1) * 64], op0=ALU.mult, op1=ALU.mult,
                                                         accum_out=den[:, h:h + 1])),
                      reads=[rqt, "ksum_b"], writes=[f"dump{sfx}", f"den{sfx}_{h}"])
            if upto >= 3:
                pb.op("v", ("tensor_scalar", dict(out=den[:, 0:16], in0=den[:, 0:16], scalar1=DENOM_EPS, scalar2=None, op0=ALU.add)),
                      reads=[f"den{sfx}_{h}" for h in range(16)], writes=[f"den_eps{sfx}"])
                pb.op("v", ("reciprocal", dict(out=den[:, 16:32], in_=den[:, 0:16])), reads=[f"den_eps{sfx}"], writes=[f"den_rec{sfx}"])
            for bnk in ((1, 2) if upto >= 4 else ()):
                off = (bnk - 1) * 8
                rec_b = den[:, 16 + off:16 + off + 8].unsqueeze(2).broadcast_to([128, 8, 64])
                pb.op("v", ("tensor_tensor", dict(out=aob[:, off * 64:(off + 8) * 64].rearrange("p (h e) -> p h e", h=8),
                                                  in0=bank[bnk][:, 0:512].rearrange("p (h e) -> p h e", h=8),
                                                  in1=rec_b, op=ALU.mult)),
                      reads=[f"den_rec{sfx}"], writes=[f"aob{sfx}"], banks=bank_regions(bnk))
            for c in range(8 if upto >= 5 else 0):
                pb.op("t", ("transpose", dict(out=bank_bf[db][:, c * 128:(c + 1) * 128],
                                              in_=aob[:, c * 128:(c + 1) * 128], identity=cm.ident_b[:])),
                      reads=[f"aob{sfx}", "cc_ident"], banks=bank_regions(0))
            pb.op("s", ("copy", dict(out=aoT[:], in_=bank_bf[db][:, 0:1024])), writes=[f"aoT{sfx}"], banks=bank_regions(0))
            for nb in range(2 if upto >= 6 else 0):
                for k in range(8):
                    pb.op("t", ("matmul", dict(out=bank[3 + nb][:, 0:512], lhsT=aoT[:, k * 128:(k + 1) * 128],
                                               rhs=wout_b[k][:, nb * 512:(nb + 1) * 512], start=(k == 0), stop=(k == 7))),
                          reads=[f"aoT{sfx}", f"woutb{k}"], banks=bank_regions(3 + nb))
            if upto >= 7:
                pb.op("v", ("tensor_scalar", dict(out=den[:, 32:33], in0=stt_i[:, 1:2], scalar1=1.0, scalar2=None, op0=ALU.add)),
                      reads=[rst], writes=[f"sc1{sfx}"])
                pb.op("v", ("tensor_scalar", dict(out=den[:, 33:34], in0=stt_i[:, 0:1], scalar1=stt_i[:, 1:2], scalar2=-1.0,
                                                  op0=ALU.mult, op1=ALU.mult)),
                      reads=[rst], writes=[f"bi1{sfx}"])
                pb.op("s", ("activation", dict(out=xxn[:], in_=xt_i[:], func=AF.Identity, bias=den[:, 33:34], scale=den[:, 32:33])),
                      reads=[rxt, f"sc1{sfx}", f"bi1{sfx}"], writes=[f"xxn{sfx}"])
                for nb in range(2):
                    pb.op("v", ("tensor_tensor", dict(out=p1t_i[:, nb * 512:(nb + 1) * 512], in0=xxn[:, nb * 512:(nb + 1) * 512],
                                                      in1=bank[3 + nb][:, 0:512], op=ALU.add)),
                          reads=[f"xxn{sfx}"], writes=[rp1], banks=bank_regions(3 + nb))
                pb.dma("y", ("dma_start", dict(out=io["part1_h"][sl, :], in_=p1t_i[:])), reads=[rp1], writes=["h_part1"])

        if not skip_ffn:
            for c in range(8):
                pb.op("t", ("transpose", dict(out=bank_bf[db][:, c * 128:(c + 1) * 128],
                                              in_=xsb[:, c * 128:(c + 1) * 128], identity=cm.ident_b[:])),
                      reads=[f"xsb{sfx}", "cc_ident"], banks=bank_regions(0))
            pb.op("s", ("copy", dict(out=xsT[:], in_=bank_bf[db][:, 0:1024])), writes=[f"xsT{sfx}"], banks=bank_regions(0))
            for nb in range(8):
                bnk = 5 + nb % 3
                for k in range(8):
                    pb.op("t", ("matmul", dict(out=bank[bnk][:, 0:512], lhsT=xsT[:, k * 128:(k + 1) * 128],
                                               rhs=win_b[k][:, nb * 512:(nb + 1) * 512], start=(k == 0), stop=False)),
                          reads=[f"xsT{sfx}", f"winb{k}"], banks=bank_regions(bnk))
                pb.op("t", ("matmul", dict(out=bank[bnk][:, 0:512], lhsT=cm.ones_row_b[:],
                                           rhs=bin_hi[:, nb * 512:(nb + 1) * 512], start=False, stop=False)),
                      reads=["bin_hi", "cc_ones_row"], banks=bank_regions(bnk))
                pb.op("t", ("matmul", dict(out=bank[bnk][:, 0:512], lhsT=cm.ones_row_b[:],
                                           rhs=bin_lo[:, nb * 512:(nb + 1) * 512], start=False, stop=True)),
                      reads=["bin_lo", "cc_ones_row"], banks=bank_regions(bnk))
                pb.op("s", ("activation", dict(out=h2t_i[:, nb * 512:(nb + 1) * 512], in_=bank[bnk][:, 0:512],
                                               func=AF.Gelu, scale=1.0)),
                      writes=[rh2], banks=bank_regions(bnk))
                pb.op("v", ("tensor_reduce", dict(out=amr[:], in_=h2t_i[:, nb * 512:(nb + 1) * 512], axis=AX.X,
                                                  op=ALU.max, apply_absolute_value=True)),
                      reads=[rh2], writes=["amr"])
                pb.op("v", ("tensor_tensor", dict(out=amax_run[:], in0=amax_run[:], in1=amr[:], op=ALU.max)),
                      reads=["amr", "am_run"], writes=["am_run"])
            pb.dma("p", ("dma_start", dict(out=io["h2_h"][sl, :], in_=h2t_i[:])), reads=[rh2], writes=["h_h2"])

    am_scr = nc.dram_tensor("am_scr2", [128], F32, kind="Internal").ap()
    am_row = sb("l0b_am_rrow", [1, 128], F32)
    am_out = sb("l0b_am_out", [1, 1], F32)
    pb.dma("y", ("dma_start", dict(out=am_scr.rearrange("(a b) -> a b", a=128), in_=amax_run[:])),
           reads=["am_run"], writes=["h_amscr"])
    pb.dma("y", ("dma_start", dict(out=am_row[:], in_=am_scr.rearrange("(a b) -> a b", a=1))),
           reads=["h_amscr"], writes=["am_rrow"])
    pb.op("v", ("reduce_max", dict(out=am_out[:], in_=am_row[:], axis=AX.X)), reads=["am_rrow"], writes=["am_o"])
    pb.dma("y", ("dma_start", dict(out=io["amaxh2"], in_=am_out[:])), reads=["am_o"], writes=["h_amaxh2"])


def build_l0c():
    nc = bass.Bass("TRN2", target_bir_lowering=False)
    io = dict(
        part1_h=nc.dram_tensor("part1_h", [L, D], F32, kind="ExternalInput").ap(),
        h2_h=nc.dram_tensor("h2_h", [L, DFF0], F32, kind="ExternalInput").ap(),
        amaxh2_8=nc.dram_tensor("amaxh2_8", [1, 8], F32, kind="ExternalInput").ap(),
        w_out=nc.dram_tensor("w_out", [DFF0, D], F32, kind="ExternalInput").ap(),
        b_out=nc.dram_tensor("b_out", [1, D], F32, kind="ExternalInput").ap(),
        x1_h=nc.dram_tensor("x1_h", [L, D], F32, kind="ExternalOutput").ap(),
        xn1_h=nc.dram_tensor("xn1_h", [L, D], F32, kind="ExternalOutput").ap(),
        amax_xn1=nc.dram_tensor("amax_xn1", [1, 1], F32, kind="ExternalOutput").ap(),
    )
    ctx = ExitStack()
    with ctx:
        pb = PB(nc, ctx)
        cm = Common(pb, ctx)
        emit_l0c(pb, cm, ctx, io)
        pb.emit()
    return nc


def emit_l0c(pb, cm, ctx, io):
    nc = pb.nc
    sb = lambda name, shape, dt: ctx.enter_context(nc.sbuf_tensor(name, shape, dt))
    ps = lambda name, shape, dt: ctx.enter_context(nc.psum_tensor("pc_" + name, shape, dt))
    cm.const(MAGIC); cm.const(-MAGIC); cm.const(LN_EPS0)

    bank = [ps(f"bk{i}", [128, 512], F32) for i in range(8)]
    bank_bf = [b[:].bitcast(BF16) for b in bank]
    ones_row_f = sb("l0c_ones_row_f", [1, 128], F32)
    pb.op("v", ("memset", dict(ap=ones_row_f[:], constant=1.0)), writes=["ones_row_f"])

    wout_b = [sb(f"l0c_woutb{k}", [128, D], BF16) for k in range(32)]
    bout_hi = sb("l0c_bout_hi", [1, D], BF16)
    bout_lo = sb("l0c_bout_lo", [1, D], BF16)
    sh_b = sb("l0c_sh_b", [128, 2], F32)

    pre = ExitStack()
    psb = lambda name, shape, dt: pre.enter_context(nc.sbuf_tensor(name, shape, dt))
    wstg = psb("l0c_wstg", [128, 1024 + 8], F32)
    wsc = psb("l0c_wsc", [128, 4], F32)

    am8 = wstg[0:1, 1024:1032]
    pb.dma("y", ("dma_start", dict(out=am8, in_=io["amaxh2_8"])), reads=["h_am8"], writes=["am8"])
    shr = psb("l0c_shr", [1, 2], F32)
    pb.op("v", ("reduce_max", dict(out=shr[0:1, 0:1], in_=am8, axis=AX.X)), reads=["am8"], writes=["shr0"])
    pb.op("v", ("tensor_scalar", dict(out=shr[0:1, 1:2], in0=shr[0:1, 0:1], scalar1=1.0 / 32767.0, scalar2=None, op0=ALU.mult)),
          reads=["shr0"], writes=["shr1"])
    pb.op("v", ("reciprocal", dict(out=shr[0:1, 0:1], in_=shr[0:1, 0:1])), reads=["shr0"], writes=["shr0"])
    pb.op("v", ("tensor_scalar", dict(out=shr[0:1, 0:1], in0=shr[0:1, 0:1], scalar1=32767.0, scalar2=None, op0=ALU.mult)),
          reads=["shr0"], writes=["shr0"])
    pb.op("t", ("matmul", dict(out=bank[0][:, 0:2], lhsT=ones_row_f[:], rhs=shr[:, 0:2], start=True, stop=True)),
          reads=["shr0", "shr1", "ones_row_f"], banks=bank_regions(0))
    pb.op("v", ("tensor_copy", dict(out=sh_b[:], in_=bank[0][:, 0:2])), writes=["sh_b"], banks=bank_regions(0))

    bo = wstg[0:1, 0:1024]
    pb.dma("y", ("dma_start", dict(out=bo, in_=io["b_out"])), reads=["h_bout"], writes=["wstg"])
    pb.op("v", ("tensor_copy", dict(out=bout_hi[:], in_=bo)), reads=["wstg"], writes=["bout_hi"])
    pb.op("v", ("tensor_tensor", dict(out=bo, in0=bo, in1=bout_hi[:], op=ALU.subtract)),
          reads=["wstg", "bout_hi"], writes=["wstg"])
    pb.op("v", ("tensor_copy", dict(out=bout_lo[:], in_=bo)), reads=["wstg"], writes=["bout_lo"])

    for k in range(32):
        pb.dma("y", ("dma_start", dict(out=wstg[:, 0:1024], in_=io["w_out"][k * 128:(k + 1) * 128, :])),
               reads=["h_wout2"], writes=["wstg"])
        pb.op("v", ("tensor_reduce", dict(out=wsc[:, 0:1], in_=wstg[:, 0:1024], axis=AX.X, op=ALU.max,
                                          apply_absolute_value=True)),
              reads=["wstg"], writes=["wsc_am"])
        pb.op("v", ("tensor_scalar", dict(out=wsc[:, 1:2], in0=wsc[:, 0:1], scalar1=1e-8, scalar2=1.0 / 127.0,
                                          op0=ALU.max, op1=ALU.mult)),
              reads=["wsc_am"], writes=["wsc_inv"])
        pb.op("v", ("reciprocal", dict(out=wsc[:, 2:3], in_=wsc[:, 1:2])), reads=["wsc_inv"], writes=["wsc_s"])
        pb.op("v", ("tensor_scalar", dict(out=wstg[:, 0:1024], in0=wstg[:, 0:1024], scalar1=wsc[:, 2:3], scalar2=MAGIC,
                                          op0=ALU.mult, op1=ALU.add)),
              reads=["wstg", "wsc_s"], writes=["wstg"])
        pb.op("s", ("activation", dict(out=wstg[:, 0:1024], in_=wstg[:, 0:1024], func=AF.Identity,
                                       bias=cm.const(-MAGIC), scale=1.0)),
              reads=["wstg", cm.const_region(-MAGIC)], writes=["wstg"])
        pb.op("v", ("tensor_scalar", dict(out=wstg[:, 0:1024], in0=wstg[:, 0:1024], scalar1=127.0, scalar2=-128.0,
                                          op0=ALU.min, op1=ALU.max)),
              reads=["wstg"], writes=["wstg"])
        pb.op("v", ("tensor_scalar", dict(out=wout_b[k][:], in0=wstg[:, 0:1024], scalar1=wsc[:, 1:2], scalar2=None,
                                          op0=ALU.mult)),
              reads=["wstg", "wsc_inv"], writes=[f"woutb{k}"])
    pre.close()
    pb.mem_barrier()

    h2t = [sb(f"l0c_h2t{i}", [128, DFF0], F32) for i in range(2)]
    p1t = [sb(f"l0c_p1t{i}", [128, D], F32) for i in range(2)]
    x1t = [sb(f"l0c_x1t{i}", [128, D], F32) for i in range(2)]
    xn1t = [sb(f"l0c_xn1t{i}", [128, D], F32) for i in range(2)]
    hsbP = [sb(f"l0c_hsb{j}", [128, DFF0], BF16) for j in range(2)]
    hsTP = [sb(f"l0c_hsT{j}", [128, DFF0], BF16) for j in range(2)]
    statP = [sb(f"l0c_stat{j}", [128, 10], F32) for j in range(2)]
    sqP = [sb(f"l0c_sq{j}", [128, D], F32) for j in range(2)]
    amax_run = sb("l0c_amax_run", [128, 1], F32)
    pb.op("v", ("memset", dict(ap=amax_run[:], constant=0.0)), writes=["am_run"])
    inv_d = 1.0 / D

    for i in range(NT):
        db = i % 2
        h2_i, p1_i, x1_i, xn1_i = h2t[db], p1t[db], x1t[db], xn1t[db]
        rh2, rp1, rx1, rxn1 = f"l0c_h2t{db}", f"l0c_p1t{db}", f"l0c_x1t{db}", f"l0c_xn1t{db}"
        hsb, hsT, stat, sq = hsbP[db], hsTP[db], statP[db], sqP[db]
        sfx = str(db)
        sl = slice(i * 128, (i + 1) * 128)
        pb.dma("y", ("dma_start", dict(out=h2_i[:], in_=io["h2_h"][sl, :])), reads=["h_h2"], writes=[rh2])
        pb.dma("p", ("dma_start", dict(out=p1_i[:], in_=io["part1_h"][sl, :])), reads=["h_part1"], writes=[rp1])

        pb.op("v", ("tensor_scalar", dict(out=h2_i[:], in0=h2_i[:], scalar1=sh_b[:, 0:1], scalar2=MAGIC,
                                          op0=ALU.mult, op1=ALU.add)),
              reads=[rh2, "sh_b"], writes=[rh2])
        pb.op("s", ("activation", dict(out=h2_i[:], in_=h2_i[:], func=AF.Identity, bias=cm.const(-MAGIC), scale=1.0)),
              reads=[rh2, cm.const_region(-MAGIC)], writes=[rh2])
        pb.op("v", ("tensor_scalar", dict(out=h2_i[:], in0=h2_i[:], scalar1=32767.0, scalar2=-32768.0,
                                          op0=ALU.min, op1=ALU.max)),
              reads=[rh2], writes=[rh2])
        pb.op("v", ("tensor_scalar", dict(out=hsb[:], in0=h2_i[:], scalar1=sh_b[:, 1:2], scalar2=None, op0=ALU.mult)),
              reads=[rh2, "sh_b"], writes=[f"hsb{sfx}"])
        for rnd in range(2):
            for c in range(16):
                cc = rnd * 16 + c
                tb = c // 8
                pb.op("t", ("transpose", dict(out=bank_bf[tb][:, (c % 8) * 128:(c % 8 + 1) * 128],
                                              in_=hsb[:, cc * 128:(cc + 1) * 128], identity=cm.ident_b[:])),
                      reads=[f"hsb{sfx}", "cc_ident"], banks=bank_regions(tb))
            for tb in range(2):
                pb.op("s", ("copy", dict(out=hsT[:, rnd * 2048 + tb * 1024: rnd * 2048 + (tb + 1) * 1024],
                                         in_=bank_bf[tb][:, 0:1024])),
                      writes=[f"hsT{sfx}"], banks=bank_regions(tb))
        for nb in range(2):
            for k in range(32):
                pb.op("t", ("matmul", dict(out=bank[2 + nb][:, 0:512], lhsT=hsT[:, k * 128:(k + 1) * 128],
                                           rhs=wout_b[k][:, nb * 512:(nb + 1) * 512], start=(k == 0), stop=False)),
                      reads=[f"hsT{sfx}", f"woutb{k}"], banks=bank_regions(2 + nb))
            pb.op("t", ("matmul", dict(out=bank[2 + nb][:, 0:512], lhsT=cm.ones_row_b[:],
                                       rhs=bout_hi[:, nb * 512:(nb + 1) * 512], start=False, stop=False)),
                  reads=["bout_hi", "cc_ones_row"], banks=bank_regions(2 + nb))
            pb.op("t", ("matmul", dict(out=bank[2 + nb][:, 0:512], lhsT=cm.ones_row_b[:],
                                       rhs=bout_lo[:, nb * 512:(nb + 1) * 512], start=False, stop=True)),
                  reads=["bout_lo", "cc_ones_row"], banks=bank_regions(2 + nb))
            pb.op("v", ("tensor_tensor", dict(out=x1_i[:, nb * 512:(nb + 1) * 512], in0=p1_i[:, nb * 512:(nb + 1) * 512],
                                              in1=bank[2 + nb][:, 0:512], op=ALU.add)),
                  reads=[rp1], writes=[rx1], banks=bank_regions(2 + nb))
        pb.dma("y", ("dma_start", dict(out=io["x1_h"][sl, :], in_=x1_i[:])), reads=[rx1], writes=["h_x1"])

        pb.op("v", ("reduce_sum", dict(out=stat[:, 0:1], in_=x1_i[:], axis=AX.X)), reads=[rx1], writes=[f"st_s1{sfx}"])
        pb.op("s", ("activation", dict(out=sq[:], in_=x1_i[:], func=AF.Square, scale=1.0)), reads=[rx1], writes=[f"sq{sfx}"])
        pb.op("v", ("reduce_sum", dict(out=stat[:, 1:2], in_=sq[:], axis=AX.X)), reads=[f"sq{sfx}"], writes=[f"st_s2{sfx}"])
        pb.op("v", ("tensor_scalar", dict(out=stat[:, 2:3], in0=stat[:, 0:1], scalar1=inv_d, scalar2=None, op0=ALU.mult)),
              reads=[f"st_s1{sfx}"], writes=[f"st_mu{sfx}"])
        pb.op("v", ("tensor_scalar", dict(out=stat[:, 3:4], in0=stat[:, 2:3], scalar1=stat[:, 2:3], scalar2=-1.0,
                                          op0=ALU.mult, op1=ALU.mult)),
              reads=[f"st_mu{sfx}"], writes=[f"st_nmu2{sfx}"])
        pb.op("v", ("tensor_scalar", dict(out=stat[:, 4:5], in0=stat[:, 1:2], scalar1=inv_d, scalar2=None, op0=ALU.mult)),
              reads=[f"st_s2{sfx}"], writes=[f"st_ex2{sfx}"])
        pb.op("v", ("tensor_tensor", dict(out=stat[:, 4:5], in0=stat[:, 4:5], in1=stat[:, 3:4], op=ALU.add)),
              reads=[f"st_ex2{sfx}", f"st_nmu2{sfx}"], writes=[f"st_var{sfx}"])
        pb.op("s", ("activation", dict(out=stat[:, 5:6], in_=stat[:, 4:5], func=AF.Sqrt, bias=cm.const(LN_EPS0), scale=1.0)),
              reads=[f"st_var{sfx}", cm.const_region(LN_EPS0)], writes=[f"st_sd{sfx}"])
        pb.op("v", ("reciprocal", dict(out=stat[:, 6:7], in_=stat[:, 5:6])), reads=[f"st_sd{sfx}"], writes=[f"st_rstd{sfx}"])
        pb.op("v", ("tensor_scalar", dict(out=stat[:, 7:8], in0=stat[:, 2:3], scalar1=stat[:, 6:7], scalar2=-1.0,
                                          op0=ALU.mult, op1=ALU.mult)),
              reads=[f"st_mu{sfx}", f"st_rstd{sfx}"], writes=[f"st_nmr{sfx}"])
        pb.op("s", ("activation", dict(out=xn1_i[:], in_=x1_i[:], func=AF.Identity, bias=stat[:, 7:8], scale=stat[:, 6:7])),
              reads=[rx1, f"st_nmr{sfx}", f"st_rstd{sfx}"], writes=[rxn1])
        pb.op("v", ("tensor_reduce", dict(out=stat[:, 8:9], in_=xn1_i[:], axis=AX.X, op=ALU.max, apply_absolute_value=True)),
              reads=[rxn1], writes=[f"st_am{sfx}"])
        pb.op("v", ("tensor_tensor", dict(out=amax_run[:], in0=amax_run[:], in1=stat[:, 8:9], op=ALU.max)),
              reads=[f"st_am{sfx}", "am_run"], writes=["am_run"])
        pb.dma("p", ("dma_start", dict(out=io["xn1_h"][sl, :], in_=xn1_i[:])), reads=[rxn1], writes=["h_xn1"])

    am_scr = nc.dram_tensor("am_scr3", [128], F32, kind="Internal").ap()
    am_row = sb("l0c_am_row", [1, 128], F32)
    am_out = sb("l0c_am_out", [1, 1], F32)
    pb.dma("y", ("dma_start", dict(out=am_scr.rearrange("(a b) -> a b", a=128), in_=amax_run[:])),
           reads=["am_run"], writes=["h_amscr"])
    pb.dma("y", ("dma_start", dict(out=am_row[:], in_=am_scr.rearrange("(a b) -> a b", a=1))),
           reads=["h_amscr"], writes=["am_rrow"])
    pb.op("v", ("reduce_max", dict(out=am_out[:], in_=am_row[:], axis=AX.X)), reads=["am_rrow"], writes=["am_o"])
    pb.dma("y", ("dma_start", dict(out=io["amax_xn1"], in_=am_out[:])), reads=["am_o"], writes=["h_amax1"])


def build_l1a():
    nc = bass.Bass("TRN2", target_bir_lowering=False)
    U16 = mybir.dt.uint16
    io = dict(
        xn1_h=nc.dram_tensor("xn1_h", [L, D], F32, kind="ExternalInput").ap(),
        wq=nc.dram_tensor("wq", [D, D], F32, kind="ExternalInput").ap(),
        wk=nc.dram_tensor("wk", [D, D], F32, kind="ExternalInput").ap(),
        wv=nc.dram_tensor("wv", [D, D], F32, kind="ExternalInput").ap(),
        lr_w=nc.dram_tensor("lr_w", [H, D], F32, kind="ExternalInput").ap(),
        lr_b=nc.dram_tensor("lr_b", [1, H], F32, kind="ExternalInput").ap(),
        tnw=nc.dram_tensor("tnw", [1, D], F32, kind="ExternalInput").ap(),
        tnb=nc.dram_tensor("tnb", [1, D], F32, kind="ExternalInput").ap(),
        qtok_h=nc.dram_tensor("qtok_h", [L, D], F32, kind="ExternalOutput").ap(),
        ktok_h=nc.dram_tensor("ktok_h", [L, D], U16, kind="ExternalOutput").ap().bitcast(BF16),
        tgt_h=nc.dram_tensor("tgt_h", [L, D], F32, kind="ExternalOutput").ap(),
        eta_h=nc.dram_tensor("eta_h", [L, H], F32, kind="ExternalOutput").ap(),
        qT_h=nc.dram_tensor("qT_h", [NT, 64, 16 * 128], U16, kind="ExternalOutput").ap().bitcast(BF16),
        kT_h=nc.dram_tensor("kT_h", [NT, 64, 16 * 128], U16, kind="ExternalOutput").ap().bitcast(BF16),
    )
    ctx = ExitStack()
    with ctx:
        pb = PB(nc, ctx)
        cm = Common(pb, ctx)
        emit_l1a(pb, cm, ctx, io)
        pb.emit()
    return nc


def emit_l1a(pb, cm, ctx, io):
    nc = pb.nc
    sb = lambda name, shape, dt: ctx.enter_context(nc.sbuf_tensor(name, shape, dt))
    ps = lambda name, shape, dt: ctx.enter_context(nc.psum_tensor("pd" + name, shape, dt))
    cm.const(MAGIC); cm.const(-MAGIC); cm.const(1e-12); cm.const(1e-8)

    bank = [ps(f"bk{i}", [128, 512], F32) for i in range(8)]
    bank_bf = [b[:].bitcast(BF16) for b in bank]
    ones_row_f = sb("l1a_ones_row_f", [1, 128], F32)
    pb.op("v", ("memset", dict(ap=ones_row_f[:], constant=1.0)), writes=["ones_row_f"])

    wqT = [sb(f"l1a_wqT{k}", [128, D], BF16) for k in range(8)]
    wkT = [sb(f"l1a_wkT{k}", [128, D], BF16) for k in range(8)]
    wvT = [sb(f"l1a_wvT{k}", [128, D], BF16) for k in range(8)]
    lrT = [sb(f"l1a_lrT{k}", [128, H], BF16) for k in range(8)]
    lrb_hi = sb("l1a_lrb_hi", [1, H], BF16)
    lrb_lo = sb("l1a_lrb_lo", [1, H], BF16)
    g4b = sb("l1a_g4b", [128, D], F32)
    b4b = sb("l1a_b4b", [128, D], F32)
    cosb = sb("l1a_cosb", [128, D], F32)
    sinb = sb("l1a_sinb", [128, D], F32)

    pre = ExitStack()
    psb = lambda name, shape, dt: pre.enter_context(nc.sbuf_tensor(name, shape, dt))
    stg = psb("l1a_stg", [128, D], F32)
    stgb = psb("l1a_stgb", [128, D], BF16)
    trig = psb("l1a_trig", [128, 8], F32)
    trow = psb("l1a_trow", [1, 160], F32)

    for wname, wdst in (("wq", wqT), ("wk", wkT), ("wv", wvT)):
        for k in range(8):
            pb.dma("y", ("dma_start", dict(out=stg[:], in_=io[wname][k * 128:(k + 1) * 128, :])),
                   reads=[f"h_{wname}"], writes=["stg"])
            pb.op("v", ("tensor_copy", dict(out=stgb[:], in_=stg[:])), reads=["stg"], writes=["stgb"])
            for c in range(8):
                pb.op("t", ("transpose", dict(out=bank_bf[c % 2][:, (c // 2) * 128:(c // 2 + 1) * 128],
                                              in_=stgb[:, c * 128:(c + 1) * 128], identity=cm.ident_b[:])),
                      reads=["stgb", "cc_ident"], banks=bank_regions(c % 2))
            for c in range(8):
                pb.op("s", ("copy", dict(out=wdst[c][:, k * 128:(k + 1) * 128],
                                         in_=bank_bf[c % 2][:, (c // 2) * 128:(c // 2 + 1) * 128])),
                      writes=[f"{wname}T{c}"], banks=bank_regions(c % 2))
    pb.dma("y", ("dma_start", dict(out=stg[0:16, :], in_=io["lr_w"])), reads=["h_lrw"], writes=["stg"])
    pb.op("v", ("tensor_copy", dict(out=stgb[0:16, :], in_=stg[0:16, :])), reads=["stg"], writes=["stgb"])
    for k in range(8):
        pb.op("t", ("transpose", dict(out=bank_bf[2][0:128, k * 16:(k + 1) * 16],
                                      in_=stgb[0:16, k * 128:(k + 1) * 128], identity=cm.ident_b[0:16, 0:16])),
              reads=["stgb", "cc_ident"], banks=bank_regions(2))
    for k in range(8):
        pb.op("s", ("copy", dict(out=lrT[k][:], in_=bank_bf[2][:, k * 16:(k + 1) * 16])),
              writes=[f"lrT{k}"], banks=bank_regions(2))
    pb.dma("y", ("dma_start", dict(out=stg[0:1, 0:16], in_=io["lr_b"])), reads=["h_lrb"], writes=["stg"])
    pb.op("v", ("tensor_copy", dict(out=lrb_hi[:], in_=stg[0:1, 0:16])), reads=["stg"], writes=["lrb_hi"])
    pb.op("v", ("tensor_tensor", dict(out=stg[0:1, 0:16], in0=stg[0:1, 0:16], in1=lrb_hi[:], op=ALU.subtract)),
          reads=["stg", "lrb_hi"], writes=["stg"])
    pb.op("v", ("tensor_copy", dict(out=lrb_lo[:], in_=stg[0:1, 0:16])), reads=["stg"], writes=["lrb_lo"])
    for name, dst, rdst in (("tnw", g4b, "g4b"), ("tnb", b4b, "b4b")):
        pb.dma("y", ("dma_start", dict(out=trow[0:1, 0:128].tensor.ap()[0:1, 0:0] if False else stg[0:1, :], in_=io[name])),
               reads=[f"h_{name}"], writes=["stg"])
        for j in range(2):
            pb.op("t", ("matmul", dict(out=bank[3][:, 0:512], lhsT=ones_row_f[:], rhs=stg[0:1, j * 512:(j + 1) * 512],
                                       start=True, stop=True)),
                  reads=["stg", "ones_row_f"], banks=bank_regions(3))
            pb.op("v", ("tensor_copy", dict(out=dst[:, j * 512:(j + 1) * 512], in_=bank[3][:, 0:512])),
                  writes=[rdst], banks=bank_regions(3))

    pb.op("p", ("iota", dict(out=trow[0:1, 0:128], pattern=[[1, 128]], base=0, channel_multiplier=0, allow_small_or_imprecise_dtypes=True)),
          writes=["trow_pos"])
    posrow = trow[0:1, 0:128]
    tmaskr = psb("l1a_tmask", [1, 128], F32)
    pb.op("v", ("tensor_single_scalar", dict(out=tmaskr[:], in_=posrow, scalar=64.0, op=ALU.is_ge)),
          reads=["trow_pos"], writes=["tmask"])
    pb.op("v", ("scalar_tensor_tensor", dict(out=posrow, in0=tmaskr[:], scalar=-64.0, in1=posrow,
                                             op0=ALU.mult, op1=ALU.add)),
          reads=["tmask", "trow_pos"], writes=["trow_pos"])
    invf = trow[0:1, 128:160]
    pb.op("p", ("iota", dict(out=invf, pattern=[[1, 32]], base=0, channel_multiplier=0, allow_small_or_imprecise_dtypes=True)),
          writes=["invf"])
    pb.op("s", ("activation", dict(out=invf, in_=invf, func=AF.Exp, scale=-float(np.log(ROPE_THETA)) * 2.0 / 64.0)),
          reads=["invf"], writes=["invf"])
    pb.op("t", ("matmul", dict(out=bank[4][:, 0:32], lhsT=posrow, rhs=invf, start=True, stop=True)),
          reads=["trow_pos", "invf"], banks=bank_regions(4))
    fr = psb("l1a_fr", [128, 96], F32)
    pb.op("v", ("tensor_copy", dict(out=fr[:, 0:32], in_=bank[4][:, 0:32])), writes=["fr"], banks=bank_regions(4))
    cos_t = psb("l1a_cos", [128, 64], F32)
    sin_t = psb("l1a_sin", [128, 64], F32)
    TWO_PI = float(2.0 * np.pi)
    pb.op("v", ("tensor_scalar", dict(out=fr[:, 32:64], in0=fr[:, 0:32], scalar1=1.0 / TWO_PI, scalar2=MAGIC,
                                      op0=ALU.mult, op1=ALU.add)),
          reads=["fr"], writes=["fr32"])
    pb.op("s", ("activation", dict(out=fr[:, 32:64], in_=fr[:, 32:64], func=AF.Identity, bias=cm.const(-MAGIC), scale=1.0)),
          reads=["fr32", cm.const_region(-MAGIC)], writes=["fr32"])
    pb.op("v", ("scalar_tensor_tensor", dict(out=fr[:, 32:64], in0=fr[:, 32:64], scalar=-TWO_PI, in1=fr[:, 0:32],
                                             op0=ALU.mult, op1=ALU.add)),
          reads=["fr32", "fr"], writes=["fr32"])
    pb.op("s", ("activation", dict(out=sin_t[:, 0:32], in_=fr[:, 32:64], func=AF.Sin, scale=1.0)),
          reads=["fr32"], writes=["sin_t"])
    pb.op("v", ("tensor_scalar", dict(out=fr[:, 64:96], in0=fr[:, 0:32], scalar1=-1.0, scalar2=float(np.pi / 2),
                                      op0=ALU.mult, op1=ALU.add)),
          reads=["fr"], writes=["fr64"])
    pb.op("v", ("tensor_scalar", dict(out=fr[:, 32:64], in0=fr[:, 64:96], scalar1=1.0 / TWO_PI, scalar2=MAGIC,
                                      op0=ALU.mult, op1=ALU.add)),
          reads=["fr64"], writes=["fr32"])
    pb.op("s", ("activation", dict(out=fr[:, 32:64], in_=fr[:, 32:64], func=AF.Identity, bias=cm.const(-MAGIC), scale=1.0)),
          reads=["fr32", cm.const_region(-MAGIC)], writes=["fr32"])
    pb.op("v", ("scalar_tensor_tensor", dict(out=fr[:, 32:64], in0=fr[:, 32:64], scalar=-TWO_PI, in1=fr[:, 64:96],
                                             op0=ALU.mult, op1=ALU.add)),
          reads=["fr32", "fr64"], writes=["fr32"])
    pb.op("s", ("activation", dict(out=cos_t[:, 0:32], in_=fr[:, 32:64], func=AF.Sin, scale=1.0)),
          reads=["fr32"], writes=["cos_t"])
    pb.op("v", ("tensor_copy", dict(out=sin_t[:, 32:64], in_=sin_t[:, 0:32])), reads=["sin_t"], writes=["sin_t"])
    pb.op("v", ("tensor_copy", dict(out=cos_t[:, 32:64], in_=cos_t[:, 0:32])), reads=["cos_t"], writes=["cos_t"])
    pb.op("v", ("tensor_copy", dict(out=cosb[:].rearrange("p (h e) -> p h e", h=H),
                                    in_=cos_t[:].unsqueeze(1).broadcast_to([128, H, 64]))),
          reads=["cos_t"], writes=["cosb"])
    pb.op("v", ("tensor_copy", dict(out=sinb[:].rearrange("p (h e) -> p h e", h=H),
                                    in_=sin_t[:].unsqueeze(1).broadcast_to([128, H, 64]))),
          reads=["sin_t"], writes=["sinb"])
    pre.close()
    pb.mem_barrier()

    xnt = [sb(f"l1a_xnt{i}", [128, D], F32) for i in range(2)]
    xnbP = [sb(f"l1a_xnb{j}", [128, D], BF16) for j in range(2)]
    xnTP = [sb(f"l1a_xnT{j}", [128, D], BF16) for j in range(2)]
    qnP = [sb(f"l1a_qn{j}", [128, D], F32) for j in range(2)]
    knP = [sb(f"l1a_kn{j}", [128, D], F32) for j in range(2)]
    rotP = [sb(f"l1a_rot{j}", [128, D], F32) for j in range(2)]
    qr = [sb(f"l1a_qr{i}", [128, D], F32) for i in range(2)]
    kr = [sb(f"l1a_kr{i}", [128, D], F32) for i in range(2)]
    tgt = [sb(f"l1a_tgt{i}", [128, D], F32) for i in range(2)]
    eta_t = [sb(f"l1a_eta{i}", [128, H], F32) for i in range(2)]
    qrb = [sb(f"l1a_qrb{i}", [128, D], BF16) for i in range(2)]
    krb = [sb(f"l1a_krb{i}", [128, D], BF16) for i in range(2)]
    qTt = [sb(f"l1a_qTt{i}", [64, 2048], BF16) for i in range(2)]
    kTt = [sb(f"l1a_kTt{i}", [64, 2048], BF16) for i in range(2)]
    ssP = [sb(f"l1a_ss{j}", [128, 2 * H + 4], F32) for j in range(2)]
    dsP = [sb(f"l1a_ds{j}", [128, 2 * H], F32) for j in range(2)]
    sqsP = [sb(f"l1a_sqs{j}", [128, D], F32) for j in range(2)]
    dumpP = [sb(f"l1a_dump{j}", [128, 64], F32) for j in range(2)]

    for i in range(NT):
        db = i % 2
        xnt_i, qr_i, kr_i, tgt_i, eta_i = xnt[db], qr[db], kr[db], tgt[db], eta_t[db]
        qrb_i, krb_i, qTt_i, kTt_i = qrb[db], krb[db], qTt[db], kTt[db]
        rxn, rqr, rkr, rtgt, reta = f"l1a_xnt{db}", f"l1a_qr{db}", f"l1a_kr{db}", f"l1a_tgt{db}", f"l1a_eta{db}"
        rqrb, rkrb, rqT, rkT = f"l1a_qrb{db}", f"l1a_krb{db}", f"l1a_qTt{db}", f"l1a_kTt{db}"
        xnb, xnT, qn, kn, rot = xnbP[db], xnTP[db], qnP[db], knP[db], rotP[db]
        ss, ds, sqs, dump = ssP[db], dsP[db], sqsP[db], dumpP[db]
        sfx = str(db)
        sl = slice(i * 128, (i + 1) * 128)

        pb.dma("y", ("dma_start", dict(out=xnt_i[:], in_=io["xn1_h"][sl, :])), reads=["h_xn1"], writes=[rxn])
        pb.op("v", ("tensor_copy", dict(out=xnb[:], in_=xnt_i[:])), reads=[rxn], writes=[f"xnb{sfx}"])
        for c in range(8):
            pb.op("t", ("transpose", dict(out=bank_bf[0][:, c * 128:(c + 1) * 128],
                                          in_=xnb[:, c * 128:(c + 1) * 128], identity=cm.ident_b[:])),
                  reads=[f"xnb{sfx}", "cc_ident"], banks=bank_regions(0))
        pb.op("s", ("copy", dict(out=xnT[:], in_=bank_bf[0][:, 0:1024])), writes=[f"xnT{sfx}"], banks=bank_regions(0))

        for wn, wT, bks in (("wq", wqT, (1, 2)), ("wk", wkT, (3, 4)), ("wv", wvT, (5, 6))):
            for nb in range(2):
                for k in range(8):
                    pb.op("t", ("matmul", dict(out=bank[bks[nb]][:, 0:512], lhsT=xnT[:, k * 128:(k + 1) * 128],
                                               rhs=wT[k][:, nb * 512:(nb + 1) * 512], start=(k == 0), stop=(k == 7))),
                          reads=[f"xnT{sfx}", f"{wn}T{k}"], banks=bank_regions(bks[nb]))
        for k in range(8):
            pb.op("t", ("matmul", dict(out=bank[7][:, 0:16], lhsT=xnT[:, k * 128:(k + 1) * 128],
                                       rhs=lrT[k][:], start=(k == 0), stop=False)),
                  reads=[f"xnT{sfx}", f"lrT{k}"], banks=bank_regions(7))
        pb.op("t", ("matmul", dict(out=bank[7][:, 0:16], lhsT=cm.ones_row_b[:], rhs=lrb_hi[:], start=False, stop=False)),
              reads=["lrb_hi", "cc_ones_row"], banks=bank_regions(7))
        pb.op("t", ("matmul", dict(out=bank[7][:, 0:16], lhsT=cm.ones_row_b[:], rhs=lrb_lo[:], start=False, stop=True)),
              reads=["lrb_lo", "cc_ones_row"], banks=bank_regions(7))
        pb.op("s", ("activation", dict(out=eta_i[:], in_=bank[7][:, 0:16], func=AF.Sigmoid, scale=1.0)),
              writes=[reta], banks=bank_regions(7))
        pb.op("v", ("tensor_scalar", dict(out=eta_i[:], in0=eta_i[:], scalar1=1.0 / 64.0, scalar2=None, op0=ALU.mult)),
              reads=[reta], writes=[reta])
        pb.dma("p", ("dma_start", dict(out=io["eta_h"][sl, :], in_=eta_i[:])), reads=[reta], writes=["h_eta"])

        for (bks, dst, rdst, nrm_off) in (((1, 2), qn, f"qn{sfx}", 0), ((3, 4), kn, f"kn{sfx}", H)):
            for nb in range(2):
                pb.op("s", ("activation", dict(out=sqs[:, nb * 512:(nb + 1) * 512], in_=bank[bks[nb]][:, 0:512],
                                               func=AF.Square, scale=1.0)),
                      writes=[f"sqs{sfx}"], banks=bank_regions(bks[nb]))
            for h in range(H):
                pb.op("v", ("reduce_sum", dict(out=ss[:, nrm_off + h:nrm_off + h + 1], in_=sqs[:, h * 64:(h + 1) * 64], axis=AX.X)),
                      reads=[f"sqs{sfx}"], writes=[f"ss{sfx}_{nrm_off + h}"])
            pb.op("s", ("activation", dict(out=ss[:, nrm_off:nrm_off + H], in_=ss[:, nrm_off:nrm_off + H], func=AF.Sqrt, scale=1.0)),
                  reads=[f"ss{sfx}_{nrm_off + h}" for h in range(H)], writes=[f"ssn{sfx}"])
            pb.op("v", ("tensor_scalar", dict(out=ss[:, nrm_off:nrm_off + H], in0=ss[:, nrm_off:nrm_off + H],
                                              scalar1=1e-12, scalar2=None, op0=ALU.max)),
                  reads=[f"ssn{sfx}"], writes=[f"ssn{sfx}"])
            pb.op("v", ("reciprocal", dict(out=ss[:, nrm_off:nrm_off + H], in_=ss[:, nrm_off:nrm_off + H])),
                  reads=[f"ssn{sfx}"], writes=[f"ssn2{sfx}"])
            rn_b = ss[:, nrm_off:nrm_off + H].unsqueeze(2).broadcast_to([128, H, 64])
            for nb in range(2):
                pb.op("v", ("tensor_tensor", dict(out=dst[:, nb * 512:(nb + 1) * 512].rearrange("p (h e) -> p h e", h=8),
                                                  in0=bank[bks[nb]][:, 0:512].rearrange("p (h e) -> p h e", h=8),
                                                  in1=rn_b[:, nb * 8:(nb + 1) * 8, :], op=ALU.mult)),
                      reads=[f"ssn2{sfx}"], writes=[rdst], banks=bank_regions(bks[nb]))
        for (srcn, rsrc, dstq, rdq) in ((f"qn{sfx}", f"qn{sfx}", qr_i, rqr), (f"kn{sfx}", f"kn{sfx}", kr_i, rkr)):
            srct = qn if srcn.startswith("qn") else kn
            rv = srct[:].rearrange("p (h t e) -> p h t e", h=H, t=2)
            pb.op("v", ("tensor_scalar", dict(out=rot[:].rearrange("p (h t e) -> p h t e", h=H, t=2)[:, :, 0, :],
                                              in0=rv[:, :, 1, :], scalar1=-1.0, scalar2=None, op0=ALU.mult)),
                  reads=[rsrc], writes=[f"rot{sfx}"])
            pb.op("v", ("tensor_copy", dict(out=rot[:].rearrange("p (h t e) -> p h t e", h=H, t=2)[:, :, 1, :],
                                            in_=rv[:, :, 0, :])),
                  reads=[rsrc], writes=[f"rot{sfx}"])
            pb.op("p", ("tensor_tensor", dict(out=rot[:], in0=rot[:], in1=sinb[:], op=ALU.mult)),
                  reads=[f"rot{sfx}", "sinb"], writes=[f"rot{sfx}"])
            pb.op("v", ("tensor_tensor", dict(out=dstq[:], in0=srct[:], in1=cosb[:], op=ALU.mult)),
                  reads=[rsrc, "cosb"], writes=[rdq])
            pb.op("v", ("tensor_tensor", dict(out=dstq[:], in0=dstq[:], in1=rot[:], op=ALU.add)),
                  reads=[rdq, f"rot{sfx}"], writes=[rdq])
        for nb in range(2):
            pb.op("v", ("tensor_tensor", dict(out=tgt_i[:, nb * 512:(nb + 1) * 512],
                                              in0=bank[5 + nb][:, 0:512], in1=kr_i[:, nb * 512:(nb + 1) * 512],
                                              op=ALU.subtract)),
                  reads=[rkr], writes=[rtgt], banks=bank_regions(5 + nb))
        pb.dma("y", ("dma_start", dict(out=io["qtok_h"][sl, :], in_=qr_i[:])), reads=[rqr], writes=["h_qtok"])
        pb.op("s", ("copy", dict(out=qrb_i[:], in_=qr_i[:])), reads=[rqr], writes=[rqrb])
        pb.op("s", ("copy", dict(out=krb_i[:], in_=kr_i[:])), reads=[rkr], writes=[rkrb])
        pb.dma("p", ("dma_start", dict(out=io["ktok_h"][sl, :], in_=krb_i[:])), reads=[rkrb], writes=["h_ktok"])
        for (srcb, rsb, dstT, rdT, hb) in ((qrb_i, rqrb, qTt_i, rqT, "q"), (krb_i, rkrb, kTt_i, rkT, "k")):
            for h in range(16):
                tb = 0 if h < 8 else 5
                pb.op("t", ("transpose", dict(out=bank_bf[tb][0:64, (h % 8) * 128:(h % 8 + 1) * 128],
                                              in_=srcb[:, h * 64:(h + 1) * 64], identity=cm.ident_b[:])),
                      reads=[rsb, "cc_ident"], banks=bank_regions(tb))
            pb.op("s", ("copy", dict(out=dstT[:, 0:1024], in_=bank_bf[0][0:64, 0:1024])), writes=[rdT], banks=bank_regions(0))
            pb.op("s", ("copy", dict(out=dstT[:, 1024:2048], in_=bank_bf[5][0:64, 0:1024])), writes=[rdT], banks=bank_regions(5))
        pb.dma("y", ("dma_start", dict(out=io["qT_h"][i], in_=qTt_i[:])), reads=[rqT], writes=["h_qT"])
        pb.dma("y", ("dma_start", dict(out=io["kT_h"][i], in_=kTt_i[:])), reads=[rkT], writes=["h_kT"])

        for h in range(H):
            pb.op("v", ("reduce_sum", dict(out=ds[:, h:h + 1], in_=tgt_i[:, h * 64:(h + 1) * 64], axis=AX.X)),
                  reads=[rtgt], writes=[f"dsm{sfx}_{h}"])
        pb.op("v", ("tensor_scalar", dict(out=ds[:, 0:H], in0=ds[:, 0:H], scalar1=1.0 / 64.0, scalar2=None, op0=ALU.mult)),
              reads=[f"dsm{sfx}_{h}" for h in range(H)], writes=[f"ds_mu{sfx}"])
        mu_b = ds[:, 0:H].unsqueeze(2).broadcast_to([128, H, 64])
        pb.op("v", ("tensor_tensor", dict(out=tgt_i[:].rearrange("p (h e) -> p h e", h=H),
                                          in0=tgt_i[:].rearrange("p (h e) -> p h e", h=H), in1=mu_b, op=ALU.subtract)),
              reads=[rtgt, f"ds_mu{sfx}"], writes=[rtgt])
        for h in range(H):
            pb.op("v", ("scalar_tensor_tensor", dict(out=dump[:], in0=tgt_i[:, h * 64:(h + 1) * 64], scalar=1.0,
                                                     in1=tgt_i[:, h * 64:(h + 1) * 64], op0=ALU.mult, op1=ALU.mult,
                                                     accum_out=ds[:, H + h:H + h + 1])),
                  reads=[rtgt], writes=[f"dump{sfx}", f"dsv{sfx}_{h}"])
        pb.op("s", ("activation", dict(out=ds[:, H:2 * H], in_=ds[:, H:2 * H], func=AF.Sqrt, scale=1.0 / 63.0)),
              reads=[f"dsv{sfx}_{h}" for h in range(H)], writes=[f"ds_sd{sfx}"])
        pb.op("v", ("tensor_scalar", dict(out=ds[:, H:2 * H], in0=ds[:, H:2 * H], scalar1=1e-8, scalar2=None, op0=ALU.add)),
              reads=[f"ds_sd{sfx}"], writes=[f"ds_sd{sfx}"])
        pb.op("v", ("reciprocal", dict(out=ds[:, H:2 * H], in_=ds[:, H:2 * H])), reads=[f"ds_sd{sfx}"], writes=[f"ds_rd{sfx}"])
        rd_b = ds[:, H:2 * H].unsqueeze(2).broadcast_to([128, H, 64])
        pb.op("v", ("tensor_tensor", dict(out=tgt_i[:].rearrange("p (h e) -> p h e", h=H),
                                          in0=tgt_i[:].rearrange("p (h e) -> p h e", h=H), in1=rd_b, op=ALU.mult)),
              reads=[rtgt, f"ds_rd{sfx}"], writes=[rtgt])
        pb.op("p", ("tensor_tensor", dict(out=tgt_i[:], in0=tgt_i[:], in1=g4b[:], op=ALU.mult)),
              reads=[rtgt, "g4b"], writes=[rtgt])
        pb.op("v", ("tensor_tensor", dict(out=tgt_i[:], in0=tgt_i[:], in1=b4b[:], op=ALU.add)),
              reads=[rtgt, "b4b"], writes=[rtgt])
        pb.dma("p", ("dma_start", dict(out=io["tgt_h"][sl, :], in_=tgt_i[:])), reads=[rtgt], writes=["h_tgt"])


def build_l1b():
    nc = bass.Bass("TRN2", target_bir_lowering=False)
    io = dict(
        qT_h=nc.dram_tensor("qT_h", [NT, 64, 16 * 128], BF16, kind="ExternalInput").ap(),
        kT_h=nc.dram_tensor("kT_h", [NT, 64, 16 * 128], BF16, kind="ExternalInput").ap(),
        ktok_h=nc.dram_tensor("ktok_h", [L, D], BF16, kind="ExternalInput").ap(),
        qtok_h=nc.dram_tensor("qtok_h", [L, D], F32, kind="ExternalInput").ap(),
        tgt_h=nc.dram_tensor("tgt_h", [L, D], F32, kind="ExternalInput").ap(),
        eta_h=nc.dram_tensor("eta_h", [L, H], F32, kind="ExternalInput").ap(),
        w1=nc.dram_tensor("w1", [H, 64, 256], F32, kind="ExternalInput").ap(),
        b1=nc.dram_tensor("b1", [H, 1, 256], F32, kind="ExternalInput").ap(),
        w2=nc.dram_tensor("w2", [H, 256, 64], F32, kind="ExternalInput").ap(),
        b2=nc.dram_tensor("b2", [H, 1, 64], F32, kind="ExternalInput").ap(),
        tnw=nc.dram_tensor("tnw", [1, D], F32, kind="ExternalInput").ap(),
        tnb=nc.dram_tensor("tnb", [1, D], F32, kind="ExternalInput").ap(),
        y_h=nc.dram_tensor("y_h", [L, D], F32, kind="ExternalOutput").ap(),
    )
    ctx = ExitStack()
    with ctx:
        pb = PB(nc, ctx)
        cm = Common(pb, ctx)
        emit_l1b(pb, cm, ctx, io)
        pb.emit()
    return nc


def emit_l1b(pb, cm, ctx, io):
    nc = pb.nc
    sb = lambda name, shape, dt: ctx.enter_context(nc.sbuf_tensor(name, shape, dt))
    SQA = float(np.sqrt(GELU_A))
    SQC = float(np.sqrt(GELU_C))
    cm.const(TTT_LN_EPS)

    psA = ctx.enter_context(nc.psum_tensor("psA", [128, 2048], F32))
    psB = ctx.enter_context(nc.psum_tensor("psB", [128, 2048], F32))
    psA_bf = psA[:].bitcast(BF16)
    psB_bf = psB[:].bitcast(BF16)
    A = lambda *i: [f"bk{j}" for j in i]
    Bk = lambda *i: [f"bk{4 + j}" for j in i]
    ident_f = sb("ttt_ident_f", [128, 128], F32)
    pb.op("v", ("memset", dict(ap=ident_f[:], constant=1.0)), writes=["idf_t"])
    pb.op("p", ("affine_select", dict(out=ident_f[:], in_=ident_f[:], pattern=[[1, 128]],
                                      compare_op=ALU.is_equal, fill=0.0, base=0, channel_multiplier=-1)),
          reads=["idf_t"], writes=["idf"])

    slot = lambda h: 64 * (h // 8)
    pc = lambda h: h % 8
    HORD = [0, 8, 1, 9, 2, 10, 3, 11, 4, 12, 5, 13, 6, 14, 7, 15]
    tp_of = lambda h: None if h < 8 else (0, 64)

    W1f = sb("ttt_W1f", [128, 2048], F32)
    W1bP = [sb(f"ttt_W1b_{p}", [128, 2048], BF16) for p in range(2)]
    b1f = [sb(f"ttt_b1f{s}", [1, 2048], F32) for s in range(2)]
    b1bP = [[sb(f"ttt_b1b{s}_{p}", [1, 2048], BF16) for s in range(2)] for p in range(2)]
    W2bP = [sb(f"ttt_W2b_{p}", [128, 2048], BF16) for p in range(2)]
    W2Tf = sb("ttt_W2Tf", [128, 2048], F32)
    W2TbP = [sb(f"ttt_W2Tb_{p}", [128, 2048], BF16) for p in range(2)]
    b2f = [sb(f"ttt_b2f{s}", [1, 512], F32) for s in range(2)]
    b2bP = [[sb(f"ttt_b2b{s}_{p}", [1, 512], BF16) for s in range(2)] for p in range(2)]
    g4p = sb("ttt_g4p", [128, 512], F32)
    b4p = sb("ttt_b4p", [128, 512], F32)
    uptri_rep = sb("ttt_uptri", [128, 512], BF16)
    ind_s = [sb(f"ttt_ind{s}", [1, 128], BF16) for s in range(2)]
    ind_f = [sb(f"ttt_indf{s}", [1, 128], F32) for s in range(2)]

    Z1f = sb("ttt_Z1f", [128, 2048], F32)
    X2b = sb("ttt_X2b", [128, 2048], BF16)
    X2T = sb("ttt_X2T", [128, 2048], BF16)
    tA = sb("ttt_tA", [128, 2048], F32)
    tB = sb("ttt_tB", [128, 2048], F32)
    gZ1n = sb("ttt_gZ1n", [128, 2048], BF16)
    Z2f = sb("ttt_Z2f", [128, 512], F32)
    gZ2n = sb("ttt_gZ2n", [128, 512], BF16)
    gZ2nT = sb("ttt_gZ2nT", [128, 512], BF16)
    Xbarb = sb("ttt_Xbarb", [128, 2048], BF16)
    XbarT = sb("ttt_XbarT", [128, 2048], BF16)
    Ct = sb("ttt_Ct", [128, 512], BF16)
    Dt = sb("ttt_Dt", [128, 512], BF16)
    Met = sb("ttt_Met", [128, 512], BF16)
    etkp = sb("ttt_etkp", [128, 512], BF16)
    etg2 = sb("ttt_etg2", [128, 512], BF16)
    etabp = sb("ttt_etabp", [128, 8], BF16)
    Zb2f = sb("ttt_Zb2f", [128, 512], F32)
    Z2s = sb("ttt_Z2s", [128, 512], F32)
    outp = sb("ttt_outp", [128, 512], F32)
    scr = sb("ttt_scr", [128, 512], F32)
    ds = sb("ttt_ds", [128, 64], F32)
    QTc = [sb(f"ttt_QTc{d}", [128, 512], BF16) for d in range(2)]
    KTc = [sb(f"ttt_KTc{d}", [128, 512], BF16) for d in range(2)]
    Ktokp = [sb(f"ttt_Ktokp{d}", [128, 512], BF16) for d in range(2)]
    Qtokp = [sb(f"ttt_Qtokp{d}", [128, 512], F32) for d in range(2)]
    TGTp = [sb(f"ttt_TGTp{d}", [128, 512], F32) for d in range(2)]
    etap = [sb(f"ttt_etap{d}", [128, 8], F32) for d in range(2)]
    ut1 = sb("ttt_ut1", [128, 64], BF16)

    for s in range(2):
        pb.op("v", ("memset", dict(ap=ind_s[s][:], constant=0.0)), writes=[f"ind{s}"])
        pb.op("v", ("memset", dict(ap=ind_s[s][0:1, s * 64:(s + 1) * 64], constant=1.0)), writes=[f"ind{s}"])
        pb.op("v", ("memset", dict(ap=ind_f[s][:], constant=0.0)), writes=[f"indf{s}"])
        pb.op("v", ("memset", dict(ap=ind_f[s][0:1, s * 64:(s + 1) * 64], constant=1.0)), writes=[f"indf{s}"])
    pb.op("v", ("memset", dict(ap=ut1[:], constant=1.0)), writes=["ut1"])
    for s in range(2):
        pb.op("p", ("affine_select", dict(out=ut1[s * 64:(s + 1) * 64, :], in_=ut1[s * 64:(s + 1) * 64, :],
                                          pattern=[[1, 64]], compare_op=ALU.is_ge, fill=0.0, base=0,
                                          channel_multiplier=-1)),
              reads=["ut1"], writes=["ut1"])
    pb.op("v", ("tensor_copy", dict(out=uptri_rep[:].rearrange("p (g e) -> p g e", g=8),
                                    in_=ut1[:].unsqueeze(1).broadcast_to([128, 8, 64]))),
          reads=["ut1"], writes=["uptri"])
    for h in range(H):
        pb.dma("y", ("dma_start", dict(out=W1f[slot(h):slot(h) + 64, pc(h) * 256:(pc(h) + 1) * 256], in_=io["w1"][h])),
               reads=["h_w1"], writes=["W1f"])
        for cc in range(2):
            pb.dma("s", ("dma_start", dict(out=Z1f[:, 0:64] if False else tA[:, cc * 64:(cc + 1) * 64],
                                           in_=io["w2"][h][cc * 128:(cc + 1) * 128, :])),
                   reads=["h_w2"], writes=["tA"])
        for cc in range(2):
            pb.op("t", ("transpose", dict(out=psA[0:64, 0:128], in_=tA[:, cc * 64:(cc + 1) * 64],
                                          identity=ident_f[:])),
                  reads=["tA", "idf"], banks=A(0))
            pb.op("v", ("tensor_copy", dict(out=W2Tf[slot(h):slot(h) + 64, pc(h) * 256 + cc * 128:pc(h) * 256 + (cc + 1) * 128],
                                            in_=psA[0:64, 0:128])),
                  writes=["W2Tf"], banks=A(0))
    b1r = io["b1"].rearrange("h a f -> a (h f)")
    b2r = io["b2"].rearrange("h a f -> a (h f)")
    pb.dma("p", ("dma_start", dict(out=b1f[0][:], in_=b1r[0:1, 0:2048])), reads=["h_b1"], writes=["b1f0"])
    pb.dma("p", ("dma_start", dict(out=b1f[1][:], in_=b1r[0:1, 2048:4096])), reads=["h_b1"], writes=["b1f1"])
    pb.dma("p", ("dma_start", dict(out=b2f[0][:], in_=b2r[0:1, 0:512])), reads=["h_b2"], writes=["b2f0"])
    pb.dma("p", ("dma_start", dict(out=b2f[1][:], in_=b2r[0:1, 512:1024])), reads=["h_b2"], writes=["b2f1"])
    pb.op("s", ("copy", dict(out=W1bP[0][:], in_=W1f[:])), reads=["W1f"], writes=["W1b_0"])
    pb.op("s", ("copy", dict(out=W2TbP[0][:], in_=W2Tf[:])), reads=["W2Tf"], writes=["W2Tb_0"])
    for s in range(2):
        pb.op("v", ("tensor_copy", dict(out=b1bP[0][s][:], in_=b1f[s][:])), reads=[f"b1f{s}"], writes=[f"b1b{s}_0"])
        pb.op("v", ("tensor_copy", dict(out=b2bP[0][s][:], in_=b2f[s][:])), reads=[f"b2f{s}"], writes=[f"b2b{s}_0"])
    grow = sb("ttt_grow", [1, 2048], F32)
    pb.dma("y", ("dma_start", dict(out=grow[0:1, 0:1024], in_=io["tnw"])), reads=["h_tnw"], writes=["grow"])
    pb.dma("y", ("dma_start", dict(out=grow[0:1, 1024:2048], in_=io["tnb"])), reads=["h_tnb"], writes=["grow"])
    for s in range(2):
        for (coff, dst, rdst) in ((0, g4p, "g4p"), (1024, b4p, "b4p")):
            pb.op("t", ("matmul", dict(out=psA[:, 0:512], lhsT=ind_f[s][:],
                                       rhs=grow[0:1, coff + s * 512:coff + (s + 1) * 512], start=True, stop=True)),
                  reads=["grow", f"indf{s}"], banks=A(0))
            pb.op("v", ("tensor_copy", dict(out=dst[s * 64:(s + 1) * 64, :], in_=psA[s * 64:(s + 1) * 64, 0:512])),
                  writes=[rdst], banks=A(0))

    seg = lambda t_: t_[:].rearrange("p (g e) -> p g e", e=64)

    def w2b_from_w2t(p):
        for t_ in range(32):
            h, cc = t_ // 2, t_ % 2
            pb.op("t", ("transpose", dict(out=psB_bf[:, t_ * 64:(t_ + 1) * 64],
                                          in_=W2TbP[p][slot(h):slot(h) + 64, pc(h) * 256 + cc * 128:pc(h) * 256 + (cc + 1) * 128],
                                          identity=cm.ident_b[slot(h):slot(h) + 64, slot(h):slot(h) + 64])),
                  reads=[f"W2Tb_{p}", "cc_ident"], banks=Bk(t_ // 8))
        pb.op("s", ("copy", dict(out=W2bP[p][:], in_=psB_bf[:, 0:2048])), writes=[f"W2b_{p}"], banks=Bk(0, 1, 2, 3))

    w2b_from_w2t(0)

    NCH = DBG.get("ttt_nc", NC)
    X2TP = [sb(f"ttt_X2T_{p}", [128, 2048], BF16) for p in range(2)]
    gZ1nP = [sb(f"ttt_gZ1n_{p}", [128, 2048], BF16) for p in range(2)]
    gZ2nP = [sb(f"ttt_gZ2n_{p}", [128, 512], BF16) for p in range(2)]
    psA_Z2 = psA[:, 1024:1536]
    psA_g2T = psA_bf[:, 3072:3584]

    def stage_a1a(c):
        d = c % 2
        cur, nxt = c % 2, (c + 1) % 2
        pp = c % 2
        i, half = c // 2, c % 2
        W1b = W1bP[cur]; rW1b = f"W1b_{cur}"
        W2Tb = W2TbP[cur]; rW2Tb = f"W2Tb_{cur}"
        W2b = W2bP[cur]; rW2b = f"W2b_{cur}"
        b1b = b1bP[cur]; b2b = b2bP[cur]
        rb1b = lambda s: f"b1b{s}_{cur}"
        rb2b = lambda s: f"b2b{s}_{cur}"
        X2T = X2TP[pp]; rX2T = f"X2T_{pp}"
        gZ1n = gZ1nP[pp]; rgZ1n = f"gZ1n_{pp}"
        gZ2n = gZ2nP[pp]; rgZ2n = f"gZ2n_{pp}"
        QT, KT, Kt, Qt, TG, ET = QTc[d], KTc[d], Ktokp[d], Qtokp[d], TGTp[d], etap[d]
        rQT, rKT, rKt, rQt, rTG, rET = (f"QTc{d}", f"KTc{d}", f"Ktokp{d}", f"Qtokp{d}", f"TGTp{d}", f"etap{d}")
        rows = slice(c * 64, (c + 1) * 64)
        eta_b64 = ET[:].unsqueeze(2).broadcast_to([128, 8, 64])
        for s in range(2):
            pb.dma("y", ("dma_start", dict(
                out=QT[s * 64:(s + 1) * 64, :],
                in_=io["qT_h"][i].rearrange("d (h t) -> d h t", h=H)[:, s * 8:(s + 1) * 8, half * 64:(half + 1) * 64])),
                reads=["h_qT"], writes=[rQT])
            pb.dma("y", ("dma_start", dict(
                out=KT[s * 64:(s + 1) * 64, :],
                in_=io["kT_h"][i].rearrange("d (h t) -> d h t", h=H)[:, s * 8:(s + 1) * 8, half * 64:(half + 1) * 64])),
                reads=["h_kT"], writes=[rKT])
            pb.dma("s", ("dma_start", dict(out=Kt[s * 64:(s + 1) * 64, :], in_=io["ktok_h"][rows, s * 512:(s + 1) * 512])),
                   reads=["h_ktok"], writes=[rKt])
            pb.dma("s", ("dma_start", dict(out=Qt[s * 64:(s + 1) * 64, :], in_=io["qtok_h"][rows, s * 512:(s + 1) * 512])),
                   reads=["h_qtok"], writes=[rQt])
            pb.dma("p", ("dma_start", dict(out=TG[s * 64:(s + 1) * 64, :], in_=io["tgt_h"][rows, s * 512:(s + 1) * 512])),
                   reads=["h_tgt"], writes=[rTG])
            pb.dma("p", ("dma_start", dict(out=ET[s * 64:(s + 1) * 64, :], in_=io["eta_h"][rows, s * 8:(s + 1) * 8])),
                   reads=["h_eta"], writes=[rET])

        for h in HORD:
            pb.op("t", ("matmul", dict(skip_group_check=True,
                                       out=psA[slot(h):slot(h) + 64, pc(h) * 256:(pc(h) + 1) * 256],
                                       lhsT=KT[slot(h):slot(h) + 64, pc(h) * 64:(pc(h) + 1) * 64],
                                       rhs=W1b[slot(h):slot(h) + 64, pc(h) * 256:(pc(h) + 1) * 256],
                                       start=(pc(h) % 2 == 0), stop=False)),
                  reads=[rKT, rW1b], banks=A(pc(h) // 2))
        for bk in range(4):
            for s in range(2):
                pb.op("t", ("matmul", dict(skip_group_check=True, out=psA[:, bk * 512:(bk + 1) * 512], lhsT=ind_s[s][:],
                                           rhs=b1b[s][0:1, bk * 512:(bk + 1) * 512], start=False, stop=(s == 1))),
                      reads=[rb1b(s), f"ind{s}"], banks=A(bk))
        pb.op("s", ("copy", dict(out=Z1f[:], in_=psA[:, 0:2048])), writes=["Z1f"], banks=A(0, 1, 2, 3))

        pb.op("s", ("activation", dict(out=X2b[:], in_=Z1f[:], func=AF.Gelu_apprx_tanh, scale=1.0)),
              reads=["Z1f"], writes=["X2b"])
        pb.op("s", ("activation", dict(out=tA[:], in_=Z1f[:], func=AF.Square, scale=SQA)),
              reads=["Z1f"], writes=["tA"])
        pb.op("v", ("scalar_tensor_tensor", dict(out=tB[:], in0=tA[:], scalar=1.0, in1=Z1f[:],
                                                 op0=ALU.add, op1=ALU.mult)),
              reads=["tA", "Z1f"], writes=["tB"])
        pb.op("s", ("activation", dict(out=tB[:], in_=tB[:], func=AF.Tanh, scale=GELU_C)),
              reads=["tB"], writes=["tB"])
        pb.op("p", ("tensor_scalar", dict(out=tA[:], in0=tA[:], scalar1=3.0, scalar2=1.0, op0=ALU.mult, op1=ALU.add)),
              reads=["tA"], writes=["tA"])
        pb.op("v", ("tensor_tensor", dict(out=tA[:], in0=tA[:], in1=Z1f[:], op=ALU.mult)),
              reads=["tA", "Z1f"], writes=["tA"])
        pb.op("s", ("activation", dict(out=Z1f[:], in_=tB[:], func=AF.Square, scale=SQC)),
              reads=["tB"], writes=["Z1f"])
        pb.op("p", ("tensor_scalar", dict(out=Z1f[:], in0=Z1f[:], scalar1=-1.0, scalar2=GELU_C, op0=ALU.mult, op1=ALU.add)),
              reads=["Z1f"], writes=["Z1f"])
        pb.op("v", ("tensor_tensor", dict(out=tA[:], in0=tA[:], in1=Z1f[:], op=ALU.mult)),
              reads=["tA", "Z1f"], writes=["tA"])
        pb.op("v", ("tensor_tensor", dict(out=tA[:], in0=tA[:], in1=tB[:], op=ALU.add)),
              reads=["tA", "tB"], writes=["tA"])

        for t_ in range(32):
            h, cc = t_ // 2, t_ % 2
            pb.op("t", ("transpose", dict(out=psA_bf[:, t_ * 64:(t_ + 1) * 64],
                                          in_=X2b[slot(h):slot(h) + 64, pc(h) * 256 + cc * 128:pc(h) * 256 + (cc + 1) * 128],
                                          identity=cm.ident_b[slot(h):slot(h) + 64, slot(h):slot(h) + 64])),
                  reads=["X2b", "cc_ident"], banks=A(t_ // 16))
        pb.op("s", ("copy", dict(out=X2T[:], in_=psA_bf[:, 0:2048])), writes=[rX2T], banks=A(0, 1))

        for h in HORD:
            for cc in range(2):
                pb.op("t", ("matmul", dict(skip_group_check=True,
                                           out=psA_Z2[slot(h):slot(h) + 64, (h % 8) * 64:(h % 8 + 1) * 64],
                                           lhsT=X2T[:, (2 * h + cc) * 64:(2 * h + cc + 1) * 64],
                                           rhs=W2b[:, (2 * h + cc) * 64:(2 * h + cc + 1) * 64],
                                           start=(h in (0, 8) and cc == 0), stop=False, tile_position=tp_of(h))),
                      reads=[rX2T, rW2b], banks=A(2))
        for s in range(2):
            pb.op("t", ("matmul", dict(skip_group_check=True, out=psA_Z2[s * 64:(s + 1) * 64, 0:512],
                                       lhsT=cm.ones_row_b[0:1, 0:64],
                                       rhs=b2b[s][:], start=False, stop=True,
                                       tile_position=None if s == 0 else (0, 64))),
                  reads=[rb2b(s), "cc_ones_row"], banks=A(2))
        pb.op("s", ("copy", dict(out=Z2f[:], in_=psA_Z2[:, 0:512])), writes=["Z2f"], banks=A(2))

        pb.op("v", ("reduce_sum", dict(out=ds[:, 0:8].unsqueeze(2), in_=seg(Z2f), axis=AX.X)),
              reads=["Z2f"], writes=["ds_mu"])
        pb.op("v", ("tensor_scalar", dict(out=ds[:, 0:8], in0=ds[:, 0:8], scalar1=1.0 / 64.0, scalar2=None, op0=ALU.mult)),
              reads=["ds_mu"], writes=["ds_mu"])
        pb.op("v", ("tensor_tensor", dict(out=seg(Z2f), in0=seg(Z2f),
                                          in1=ds[:, 0:8].unsqueeze(2).broadcast_to([128, 8, 64]), op=ALU.subtract)),
              reads=["Z2f", "ds_mu"], writes=["Z2f"])
        pb.op("s", ("activation", dict(out=scr[:], in_=Z2f[:], func=AF.Square, scale=1.0)),
              reads=["Z2f"], writes=["scr"])
        pb.op("v", ("reduce_sum", dict(out=ds[:, 8:16].unsqueeze(2), in_=seg(scr), axis=AX.X)),
              reads=["scr"], writes=["ds_var"])
        pb.op("s", ("activation", dict(out=ds[:, 8:16], in_=ds[:, 8:16], func=AF.Sqrt, bias=cm.const(TTT_LN_EPS),
                                       scale=1.0 / 64.0)),
              reads=["ds_var", cm.const_region(TTT_LN_EPS)], writes=["ds_sd"])
        pb.op("v", ("reciprocal", dict(out=ds[:, 8:16], in_=ds[:, 8:16])), reads=["ds_sd"], writes=["ds_rstd"])
        pb.op("v", ("tensor_tensor", dict(out=seg(Z2f), in0=seg(Z2f),
                                          in1=ds[:, 8:16].unsqueeze(2).broadcast_to([128, 8, 64]), op=ALU.mult)),
              reads=["Z2f", "ds_rstd"], writes=["Z2f"])
        pb.op("v", ("tensor_tensor", dict(out=scr[:], in0=Z2f[:], in1=g4p[:], op=ALU.mult)),
              reads=["Z2f", "g4p"], writes=["scr"])
        pb.op("p", ("tensor_tensor", dict(out=scr[:], in0=scr[:], in1=b4p[:], op=ALU.add)),
              reads=["scr", "b4p"], writes=["scr"])
        pb.op("v", ("tensor_tensor", dict(out=scr[:], in0=scr[:], in1=TG[:], op=ALU.subtract)),
              reads=["scr", rTG], writes=["scr"])
        pb.op("v", ("tensor_tensor", dict(out=scr[:], in0=scr[:], in1=g4p[:], op=ALU.mult)),
              reads=["scr", "g4p"], writes=["scr"])
        pb.op("v", ("reduce_sum", dict(out=ds[:, 16:24].unsqueeze(2), in_=seg(scr), axis=AX.X)),
              reads=["scr"], writes=["ds_s1"])
        pb.op("v", ("tensor_tensor", dict(out=Z2s[:], in0=scr[:], in1=Z2f[:], op=ALU.mult)),
              reads=["scr", "Z2f"], writes=["Z2s"])
        pb.op("v", ("reduce_sum", dict(out=ds[:, 24:32].unsqueeze(2), in_=seg(Z2s), axis=AX.X)),
              reads=["Z2s"], writes=["ds_s2"])
        pb.op("v", ("scalar_tensor_tensor", dict(out=seg(scr), in0=seg(scr), scalar=-64.0,
                                                 in1=ds[:, 16:24].unsqueeze(2).broadcast_to([128, 8, 64]),
                                                 op0=ALU.mult, op1=ALU.add)),
              reads=["scr", "ds_s1"], writes=["scr"])
        pb.op("v", ("tensor_tensor", dict(out=seg(Z2s), in0=seg(Z2f),
                                          in1=ds[:, 24:32].unsqueeze(2).broadcast_to([128, 8, 64]), op=ALU.mult)),
              reads=["Z2f", "ds_s2"], writes=["Z2s"])
        pb.op("v", ("tensor_tensor", dict(out=scr[:], in0=scr[:], in1=Z2s[:], op=ALU.add)),
              reads=["scr", "Z2s"], writes=["scr"])
        pb.op("v", ("tensor_scalar", dict(out=ds[:, 32:40], in0=ds[:, 8:16], scalar1=1.0 / 64.0, scalar2=None, op0=ALU.mult)),
              reads=["ds_rstd"], writes=["ds_r64"])
        pb.op("v", ("tensor_tensor", dict(out=gZ2n[:].rearrange("p (g e) -> p g e", e=64), in0=seg(scr),
                                          in1=ds[:, 32:40].unsqueeze(2).broadcast_to([128, 8, 64]), op=ALU.mult)),
              reads=["scr", "ds_r64"], writes=[rgZ2n])

    def stage_a1b(c):
        d = c % 2
        cur, nxt = c % 2, (c + 1) % 2
        pp = c % 2
        i, half = c // 2, c % 2
        W1b = W1bP[cur]; rW1b = f"W1b_{cur}"
        W2Tb = W2TbP[cur]; rW2Tb = f"W2Tb_{cur}"
        W2b = W2bP[cur]; rW2b = f"W2b_{cur}"
        b1b = b1bP[cur]; b2b = b2bP[cur]
        rb1b = lambda s: f"b1b{s}_{cur}"
        rb2b = lambda s: f"b2b{s}_{cur}"
        X2T = X2TP[pp]; rX2T = f"X2T_{pp}"
        gZ1n = gZ1nP[pp]; rgZ1n = f"gZ1n_{pp}"
        gZ2n = gZ2nP[pp]; rgZ2n = f"gZ2n_{pp}"
        QT, KT, Kt, Qt, TG, ET = QTc[d], KTc[d], Ktokp[d], Qtokp[d], TGTp[d], etap[d]
        rQT, rKT, rKt, rQt, rTG, rET = (f"QTc{d}", f"KTc{d}", f"Ktokp{d}", f"Qtokp{d}", f"TGTp{d}", f"etap{d}")
        rows = slice(c * 64, (c + 1) * 64)
        eta_b64 = ET[:].unsqueeze(2).broadcast_to([128, 8, 64])
        for h in HORD:
            pb.op("t", ("transpose", dict(out=psA_g2T[slot(h):slot(h) + 64, pc(h) * 64:(pc(h) + 1) * 64],
                                          in_=gZ2n[slot(h):slot(h) + 64, pc(h) * 64:(pc(h) + 1) * 64],
                                          identity=cm.ident_b[slot(h):slot(h) + 64, slot(h):slot(h) + 64])),
                  reads=[rgZ2n, "cc_ident"], banks=A(3))
        pb.op("s", ("copy", dict(out=gZ2nT[:], in_=psA_g2T)), writes=["gZ2nT"], banks=A(3))

        for h in HORD:
            pb.op("t", ("matmul", dict(skip_group_check=True,
                                       out=psA[slot(h):slot(h) + 64, pc(h) * 256:(pc(h) + 1) * 256],
                                       lhsT=gZ2nT[slot(h):slot(h) + 64, pc(h) * 64:(pc(h) + 1) * 64],
                                       rhs=W2Tb[slot(h):slot(h) + 64, pc(h) * 256:(pc(h) + 1) * 256],
                                       start=(pc(h) % 2 == 0), stop=(pc(h) % 2 == 1))),
                  reads=["gZ2nT", rW2Tb], banks=A(pc(h) // 2))
        pb.op("s", ("activation", dict(out=tB[:], in_=psA[:, 0:2048], func=AF.Copy, scale=0.5)),
              writes=["tB"], banks=A(0, 1, 2, 3))
        pb.op("v", ("scalar_tensor_tensor", dict(out=gZ1n[:], in0=tA[:], scalar=1.0, in1=tB[:],
                                                 op0=ALU.add, op1=ALU.mult)),
              reads=["tA", "tB"], writes=[rgZ1n])

        pb.op("v", ("tensor_tensor", dict(out=etkp[:].rearrange("p (g e) -> p g e", e=64),
                                          in0=Kt[:].rearrange("p (g e) -> p g e", e=64),
                                          in1=eta_b64, op=ALU.mult)),
              reads=[rKt, rET], writes=["etkp"])
        pb.op("v", ("tensor_tensor", dict(out=etg2[:].rearrange("p (g e) -> p g e", e=64),
                                          in0=gZ2n[:].rearrange("p (g e) -> p g e", e=64),
                                          in1=eta_b64, op=ALU.mult)),
              reads=[rgZ2n, rET], writes=["etg2"])
        pb.op("v", ("tensor_copy", dict(out=etabp[:], in_=ET[:])), reads=[rET], writes=["etabp"])
        for h in HORD:
            pb.op("t", ("matmul", dict(skip_group_check=True,
                                       out=psA[slot(h):slot(h) + 64, pc(h) * 256:(pc(h) + 1) * 256],
                                       lhsT=etkp[slot(h):slot(h) + 64, pc(h) * 64:(pc(h) + 1) * 64],
                                       rhs=gZ1n[slot(h):slot(h) + 64, pc(h) * 256:(pc(h) + 1) * 256],
                                       start=(pc(h) % 2 == 0), stop=(pc(h) % 2 == 1))),
                  reads=["etkp", rgZ1n], banks=A(pc(h) // 2))
        pb.op("v", ("tensor_tensor", dict(out=W1f[:], in0=W1f[:], in1=psA[:, 0:2048], op=ALU.add)),
              reads=["W1f"], writes=["W1f"], banks=A(0, 1, 2, 3))
        for h in HORD:
            pb.op("t", ("matmul", dict(skip_group_check=True,
                                       out=psA[slot(h):slot(h) + 1, pc(h) * 256:(pc(h) + 1) * 256],
                                       lhsT=etabp[slot(h):slot(h) + 64, pc(h):pc(h) + 1],
                                       rhs=gZ1n[slot(h):slot(h) + 64, pc(h) * 256:(pc(h) + 1) * 256],
                                       start=True, stop=True)),
                  reads=["etabp", rgZ1n], banks=A(pc(h) // 2))
        for s in range(2):
            pb.op("v", ("tensor_tensor", dict(out=b1f[s][:], in0=b1f[s][:], in1=psA[s * 64:s * 64 + 1, 0:2048], op=ALU.add)),
                  reads=[f"b1f{s}"], writes=[f"b1f{s}"], banks=A(0, 1, 2, 3))
        for h in HORD:
            pb.op("t", ("matmul", dict(skip_group_check=True,
                                       out=psA[slot(h):slot(h) + 64, pc(h) * 256:(pc(h) + 1) * 256],
                                       lhsT=etg2[slot(h):slot(h) + 64, pc(h) * 64:(pc(h) + 1) * 64],
                                       rhs=X2b[slot(h):slot(h) + 64, pc(h) * 256:(pc(h) + 1) * 256],
                                       start=(pc(h) % 2 == 0), stop=(pc(h) % 2 == 1))),
                  reads=["etg2", "X2b"], banks=A(pc(h) // 2))
        pb.op("v", ("tensor_tensor", dict(out=W2Tf[:], in0=W2Tf[:], in1=psA[:, 0:2048], op=ALU.add)),
              reads=["W2Tf"], writes=["W2Tf"], banks=A(0, 1, 2, 3))
        for h in HORD:
            pb.op("t", ("matmul", dict(skip_group_check=True,
                                       out=psA[slot(h):slot(h) + 1, pc(h) * 64:(pc(h) + 1) * 64],
                                       lhsT=etabp[slot(h):slot(h) + 64, pc(h):pc(h) + 1],
                                       rhs=gZ2n[slot(h):slot(h) + 64, pc(h) * 64:(pc(h) + 1) * 64],
                                       start=True, stop=True)),
                  reads=["etabp", rgZ2n], banks=A(0))
        for s in range(2):
            pb.op("v", ("tensor_tensor", dict(out=b2f[s][:], in0=b2f[s][:], in1=psA[s * 64:s * 64 + 1, 0:512], op=ALU.add)),
                  reads=[f"b2f{s}"], writes=[f"b2f{s}"], banks=A(0))

    def stage_b(c):
        d = c % 2
        cur = c % 2
        pp = c % 2
        W1b = W1bP[cur]; rW1b = f"W1b_{cur}"
        W2b = W2bP[cur]; rW2b = f"W2b_{cur}"
        b1b = b1bP[cur]; b2b = b2bP[cur]
        rb1b = lambda s: f"b1b{s}_{cur}"
        rb2b = lambda s: f"b2b{s}_{cur}"
        X2T = X2TP[pp]; rX2T = f"X2T_{pp}"
        gZ1n = gZ1nP[pp]; rgZ1n = f"gZ1n_{pp}"
        gZ2n = gZ2nP[pp]; rgZ2n = f"gZ2n_{pp}"
        QT, KT, Qt, ET = QTc[d], KTc[d], Qtokp[d], etap[d]
        rQT, rKT, rQt, rET = f"QTc{d}", f"KTc{d}", f"Qtokp{d}", f"etap{d}"
        rows = slice(c * 64, (c + 1) * 64)
        eta_b64 = ET[:].unsqueeze(2).broadcast_to([128, 8, 64])

        pb.op("v", ("tensor_tensor", dict(out=Met[:].rearrange("p (g e) -> p g e", e=64),
                                          in0=uptri_rep[:].rearrange("p (g e) -> p g e", e=64),
                                          in1=eta_b64, op=ALU.mult)),
              reads=["uptri", rET], writes=["Met"])
        for h in HORD:
            pb.op("t", ("transpose" if False else "matmul", dict(skip_group_check=True,
                                       out=psB[slot(h):slot(h) + 64, pc(h) * 64:(pc(h) + 1) * 64],
                                       lhsT=KT[slot(h):slot(h) + 64, pc(h) * 64:(pc(h) + 1) * 64],
                                       rhs=QT[slot(h):slot(h) + 64, pc(h) * 64:(pc(h) + 1) * 64],
                                       start=True, stop=True)),
                  reads=[rKT, rQT], banks=Bk(0))
        pb.op("v", ("scalar_tensor_tensor", dict(out=Ct[:], in0=psB[:, 0:512], scalar=1.0, in1=Met[:],
                                                 op0=ALU.add, op1=ALU.mult)),
              reads=["Met"], writes=["Ct"], banks=Bk(0))

        for h in HORD:
            pb.op("t", ("matmul", dict(skip_group_check=True,
                                       out=psB[slot(h):slot(h) + 64, pc(h) * 256:(pc(h) + 1) * 256],
                                       lhsT=QT[slot(h):slot(h) + 64, pc(h) * 64:(pc(h) + 1) * 64],
                                       rhs=W1b[slot(h):slot(h) + 64, pc(h) * 256:(pc(h) + 1) * 256],
                                       start=(pc(h) % 2 == 0), stop=False)),
                  reads=[rQT, rW1b], banks=Bk(pc(h) // 2))
            pb.op("t", ("matmul", dict(skip_group_check=True,
                                       out=psB[slot(h):slot(h) + 64, pc(h) * 256:(pc(h) + 1) * 256],
                                       lhsT=Ct[slot(h):slot(h) + 64, pc(h) * 64:(pc(h) + 1) * 64],
                                       rhs=gZ1n[slot(h):slot(h) + 64, pc(h) * 256:(pc(h) + 1) * 256],
                                       start=False, stop=False)),
                  reads=["Ct", rgZ1n], banks=Bk(pc(h) // 2))
        for bk in range(4):
            for s in range(2):
                pb.op("t", ("matmul", dict(skip_group_check=True, out=psB[:, bk * 512:(bk + 1) * 512], lhsT=ind_s[s][:],
                                           rhs=b1b[s][0:1, bk * 512:(bk + 1) * 512], start=False, stop=(s == 1))),
                      reads=[rb1b(s), f"ind{s}"], banks=Bk(bk))
        pb.op("s", ("activation", dict(out=Xbarb[:], in_=psB[:, 0:2048], func=AF.Gelu_apprx_tanh, scale=1.0)),
              writes=["Xbarb"], banks=Bk(0, 1, 2, 3))
        for t_ in range(32):
            h, cc = t_ // 2, t_ % 2
            pb.op("t", ("transpose", dict(out=psB_bf[:, t_ * 64:(t_ + 1) * 64],
                                          in_=Xbarb[slot(h):slot(h) + 64, pc(h) * 256 + cc * 128:pc(h) * 256 + (cc + 1) * 128],
                                          identity=cm.ident_b[slot(h):slot(h) + 64, slot(h):slot(h) + 64])),
                  reads=["Xbarb", "cc_ident"], banks=Bk(t_ // 16))
        pb.op("s", ("copy", dict(out=XbarT[:], in_=psB_bf[:, 0:2048])), writes=["XbarT"], banks=Bk(0, 1))

        for h in HORD:
            for cc in range(2):
                pb.op("t", ("matmul", dict(skip_group_check=True,
                                           out=psB[slot(h):slot(h) + 64, 1024 + (h % 8) * 64:1024 + (h % 8 + 1) * 64],
                                           lhsT=X2T[:, (2 * h + cc) * 64:(2 * h + cc + 1) * 64],
                                           rhs=XbarT[:, (2 * h + cc) * 64:(2 * h + cc + 1) * 64],
                                           start=(h in (0, 8) and cc == 0), stop=(h in (7, 15) and cc == 1),
                                           tile_position=tp_of(h))),
                      reads=[rX2T, "XbarT"], banks=Bk(2))
        pb.op("v", ("scalar_tensor_tensor", dict(out=Dt[:], in0=psB[:, 1024:1536], scalar=1.0, in1=Met[:],
                                                 op0=ALU.add, op1=ALU.mult)),
              reads=["Met"], writes=["Dt"], banks=Bk(2))

        for h in HORD:
            for cc in range(2):
                pb.op("t", ("matmul", dict(skip_group_check=True,
                                           out=psB[slot(h):slot(h) + 64, 1536 + (h % 8) * 64:1536 + (h % 8 + 1) * 64],
                                           lhsT=XbarT[:, (2 * h + cc) * 64:(2 * h + cc + 1) * 64],
                                           rhs=W2b[:, (2 * h + cc) * 64:(2 * h + cc + 1) * 64],
                                           start=(h in (0, 8) and cc == 0), stop=False, tile_position=tp_of(h))),
                      reads=["XbarT", rW2b], banks=Bk(3))
            pb.op("t", ("matmul", dict(skip_group_check=True,
                                       out=psB[slot(h):slot(h) + 64, 1536 + (h % 8) * 64:1536 + (h % 8 + 1) * 64],
                                       lhsT=Dt[slot(h):slot(h) + 64, pc(h) * 64:(pc(h) + 1) * 64],
                                       rhs=gZ2n[slot(h):slot(h) + 64, pc(h) * 64:(pc(h) + 1) * 64],
                                       start=False, stop=False)),
                  reads=["Dt", rgZ2n], banks=Bk(3))
        for s in range(2):
            pb.op("t", ("matmul", dict(skip_group_check=True, out=psB[s * 64:(s + 1) * 64, 1536:2048],
                                       lhsT=cm.ones_row_b[0:1, 0:64],
                                       rhs=b2b[s][:], start=False, stop=True,
                                       tile_position=None if s == 0 else (0, 64))),
                  reads=[rb2b(s), "cc_ones_row"], banks=Bk(3))
        pb.op("s", ("copy", dict(out=Zb2f[:], in_=psB[:, 1536:2048])), writes=["Zb2f"], banks=Bk(3))

        pb.op("v", ("reduce_sum", dict(out=ds[:, 40:48].unsqueeze(2), in_=seg(Zb2f), axis=AX.X)),
              reads=["Zb2f"], writes=["ds_mu2"])
        pb.op("v", ("tensor_scalar", dict(out=ds[:, 40:48], in0=ds[:, 40:48], scalar1=1.0 / 64.0, scalar2=None, op0=ALU.mult)),
              reads=["ds_mu2"], writes=["ds_mu2"])
        pb.op("v", ("tensor_tensor", dict(out=seg(Zb2f), in0=seg(Zb2f),
                                          in1=ds[:, 40:48].unsqueeze(2).broadcast_to([128, 8, 64]), op=ALU.subtract)),
              reads=["Zb2f", "ds_mu2"], writes=["Zb2f"])
        pb.op("s", ("activation", dict(out=outp[:], in_=Zb2f[:], func=AF.Square, scale=1.0)),
              reads=["Zb2f"], writes=["outp"])
        pb.op("v", ("reduce_sum", dict(out=ds[:, 48:56].unsqueeze(2), in_=seg(outp), axis=AX.X)),
              reads=["outp"], writes=["ds_var2"])
        pb.op("s", ("activation", dict(out=ds[:, 48:56], in_=ds[:, 48:56], func=AF.Sqrt, bias=cm.const(TTT_LN_EPS),
                                       scale=1.0 / 64.0)),
              reads=["ds_var2", cm.const_region(TTT_LN_EPS)], writes=["ds_sd2"])
        pb.op("v", ("reciprocal", dict(out=ds[:, 48:56], in_=ds[:, 48:56])), reads=["ds_sd2"], writes=["ds_rstd2"])
        pb.op("v", ("tensor_tensor", dict(out=seg(Zb2f), in0=seg(Zb2f),
                                          in1=ds[:, 48:56].unsqueeze(2).broadcast_to([128, 8, 64]), op=ALU.mult)),
              reads=["Zb2f", "ds_rstd2"], writes=["Zb2f"])
        pb.op("v", ("tensor_tensor", dict(out=outp[:], in0=Zb2f[:], in1=g4p[:], op=ALU.mult)),
              reads=["Zb2f", "g4p"], writes=["outp"])
        pb.op("p", ("tensor_tensor", dict(out=outp[:], in0=outp[:], in1=b4p[:], op=ALU.add)),
              reads=["outp", "b4p"], writes=["outp"])
        pb.op("v", ("tensor_tensor", dict(out=outp[:], in0=outp[:], in1=Qt[:], op=ALU.add)),
              reads=["outp", rQt], writes=["outp"])
        for s in range(2):
            pb.dma("y", ("dma_start", dict(out=io["y_h"][rows, s * 512:(s + 1) * 512], in_=outp[s * 64:(s + 1) * 64, :])),
                   reads=["outp"], writes=["h_y"])

    def stage_a2(c):
        nxt = (c + 1) % 2
        pb.op("s", ("copy", dict(out=W1bP[nxt][:], in_=W1f[:])), reads=["W1f"], writes=[f"W1b_{nxt}"])
        for s in range(2):
            pb.op("v", ("tensor_copy", dict(out=b1bP[nxt][s][:], in_=b1f[s][:])), reads=[f"b1f{s}"], writes=[f"b1b{s}_{nxt}"])
        pb.op("s", ("copy", dict(out=W2TbP[nxt][:], in_=W2Tf[:])), reads=["W2Tf"], writes=[f"W2Tb_{nxt}"])
        for s in range(2):
            pb.op("v", ("tensor_copy", dict(out=b2bP[nxt][s][:], in_=b2f[s][:])), reads=[f"b2f{s}"], writes=[f"b2b{s}_{nxt}"])
        for t_ in range(32):
            h, cc = t_ // 2, t_ % 2
            pb.op("t", ("transpose", dict(out=psA_bf[:, t_ * 64:(t_ + 1) * 64],
                                          in_=W2TbP[nxt][slot(h):slot(h) + 64, pc(h) * 256 + cc * 128:pc(h) * 256 + (cc + 1) * 128],
                                          identity=cm.ident_b[slot(h):slot(h) + 64, slot(h):slot(h) + 64])),
                  reads=[f"W2Tb_{nxt}", "cc_ident"], banks=A(t_ // 16))
        pb.op("s", ("copy", dict(out=W2bP[nxt][:], in_=psA_bf[:, 0:2048])), writes=[f"W2b_{nxt}"], banks=A(0, 1))

    stage_a1a(0)
    stage_a1b(0)
    if NCH > 1:
        stage_a2(0)
        stage_a1a(1)
    for c in range(NCH):
        stage_b(c)
        if c + 1 < NCH:
            stage_a1b(c + 1)
        if c + 2 < NCH:
            stage_a2(c + 1)
            stage_a1a(c + 2)


def build_l1c():
    nc = bass.Bass("TRN2", target_bir_lowering=False)
    io = dict(
        y_h=nc.dram_tensor("y_h", [L, D], F32, kind="ExternalInput").ap(),
        xn1_h=nc.dram_tensor("xn1_h", [L, D], F32, kind="ExternalInput").ap(),
        amax8=nc.dram_tensor("amax8", [1, 8], F32, kind="ExternalInput").ap(),
        wo=nc.dram_tensor("wo", [D, D], F32, kind="ExternalInput").ap(),
        pnw=nc.dram_tensor("pnw", [1, D], F32, kind="ExternalInput").ap(),
        pnb=nc.dram_tensor("pnb", [1, D], F32, kind="ExternalInput").ap(),
        clip_x=nc.dram_tensor("clip_x", [1, 1], F32, kind="ExternalInput").ap(),
        w_gate=nc.dram_tensor("w_gate", [DFF1, D], F32, kind="ExternalInput").ap(),
        w_up=nc.dram_tensor("w_up", [DFF1, D], F32, kind="ExternalInput").ap(),
        attn1_h=nc.dram_tensor("attn1_h", [L, D], F32, kind="ExternalOutput").ap(),
        hgl_h=nc.dram_tensor("hgl_h", [L, DFF1], F32, kind="ExternalOutput").ap(),
        amax_h1=nc.dram_tensor("amax_h1", [1, 1], F32, kind="ExternalOutput").ap(),
    )
    ctx = ExitStack()
    with ctx:
        pb = PB(nc, ctx)
        cm = Common(pb, ctx)
        emit_l1c(pb, cm, ctx, io)
        pb.emit()
    return nc


def emit_l1c(pb, cm, ctx, io):
    nc = pb.nc
    sb = lambda name, shape, dt: ctx.enter_context(nc.sbuf_tensor(name, shape, dt))
    ps = lambda name, shape, dt: ctx.enter_context(nc.psum_tensor("pe" + name, shape, dt))
    cm.const(MAGIC); cm.const(-MAGIC); cm.const(LN_EPS0)
    NCI = 22

    bank = [ps(f"bk{i}", [128, 512], F32) for i in range(8)]
    bank_bf = [b[:].bitcast(BF16) for b in bank]
    ones_row_f = sb("c_ones_row_f", [1, 128], F32)
    pb.op("v", ("memset", dict(ap=ones_row_f[:], constant=1.0)), writes=["ones_row_f"])

    woT = [sb(f"c_woT{k}", [128, D], BF16) for k in range(8)]
    wgT = [sb(f"c_wgT{k}", [128, DFF1], BF16) for k in range(8)]
    wuT = [sb(f"c_wuT{k}", [128, DFF1], BF16) for k in range(8)]
    pnw_b = sb("c_pnw_b", [128, D], F32)
    pnb_b = sb("c_pnb_b", [128, D], F32)
    sq_b = sb("c_sq_b", [128, 2], F32)

    pre = ExitStack()
    psb = lambda name, shape, dt: pre.enter_context(nc.sbuf_tensor(name, shape, dt))
    stg = psb("c_stg", [128, D + 16], F32)
    stgb = psb("c_stgb", [128, D], BF16)
    wsc = psb("c_wsc", [128, 4], F32)

    pb.dma("y", ("dma_start", dict(out=stg[0:1, D:D + 8], in_=io["amax8"])), reads=["h_amax8"], writes=["stgr"])
    pb.dma("y", ("dma_start", dict(out=stg[0:1, D + 8:D + 9], in_=io["clip_x"])), reads=["h_clipx"], writes=["stgr"])
    pb.op("v", ("reduce_max", dict(out=wsc[0:1, 0:1], in_=stg[0:1, D:D + 8], axis=AX.X)), reads=["stgr"], writes=["wsc0"])
    pb.op("s", ("activation", dict(out=wsc[0:1, 1:2], in_=stg[0:1, D + 8:D + 9], func=AF.Sigmoid, scale=1.0)),
          reads=["stgr"], writes=["wsc1"])
    pb.op("v", ("tensor_tensor", dict(out=wsc[0:1, 2:3], in0=wsc[0:1, 0:1], in1=wsc[0:1, 1:2], op=ALU.mult)),
          reads=["wsc0", "wsc1"], writes=["wsc2"])
    pb.op("v", ("tensor_scalar", dict(out=wsc[0:1, 3:4], in0=wsc[0:1, 2:3], scalar1=1.0 / 127.0, scalar2=None, op0=ALU.mult)),
          reads=["wsc2"], writes=["wsc3"])
    pb.op("v", ("reciprocal", dict(out=wsc[0:1, 2:3], in_=wsc[0:1, 3:4])), reads=["wsc3"], writes=["wsc2"])
    pb.op("t", ("matmul", dict(out=bank[0][:, 0:2], lhsT=ones_row_f[:], rhs=wsc[0:1, 2:4], start=True, stop=True)),
          reads=["wsc2", "wsc3", "ones_row_f"], banks=bank_regions(0))
    pb.op("v", ("tensor_copy", dict(out=sq_b[:], in_=bank[0][:, 0:2])), writes=["sq_b"], banks=bank_regions(0))

    for name, dst, rdst in (("pnw", pnw_b, "pnw_b"), ("pnb", pnb_b, "pnb_b")):
        pb.dma("y", ("dma_start", dict(out=stg[0:1, 0:D], in_=io[name])), reads=[f"h_{name}"], writes=["stg"])
        for j in range(2):
            pb.op("t", ("matmul", dict(out=bank[1][:, 0:512], lhsT=ones_row_f[:], rhs=stg[0:1, j * 512:(j + 1) * 512],
                                       start=True, stop=True)),
                  reads=["stg", "ones_row_f"], banks=bank_regions(1))
            pb.op("v", ("tensor_copy", dict(out=dst[:, j * 512:(j + 1) * 512], in_=bank[1][:, 0:512])),
                  writes=[rdst], banks=bank_regions(1))

    for k in range(8):
        pb.dma("y", ("dma_start", dict(out=stg[:, 0:D], in_=io["wo"][k * 128:(k + 1) * 128, :])),
               reads=["h_wo"], writes=["stg"])
        pb.op("v", ("tensor_copy", dict(out=stgb[:], in_=stg[:, 0:D])), reads=["stg"], writes=["stgb"])
        for c in range(8):
            pb.op("t", ("transpose", dict(out=bank_bf[2][:, c * 128:(c + 1) * 128],
                                          in_=stgb[:, c * 128:(c + 1) * 128], identity=cm.ident_b[:])),
                  reads=["stgb", "cc_ident"], banks=bank_regions(2))
        for c in range(8):
            pb.op("s", ("copy", dict(out=woT[c][:, k * 128:(k + 1) * 128], in_=bank_bf[2][:, c * 128:(c + 1) * 128])),
                  writes=[f"woT{c}"], banks=bank_regions(2))

    for wname, wdst in (("w_gate", wgT), ("w_up", wuT)):
        for k in range(NCI):
            pb.dma("y", ("dma_start", dict(out=stg[:, 0:D], in_=io[wname][k * 128:(k + 1) * 128, :])),
                   reads=[f"h_{wname}"], writes=["stg"])
            pb.op("v", ("tensor_reduce", dict(out=wsc[:, 0:1], in_=stg[:, 0:D], axis=AX.X, op=ALU.max,
                                              apply_absolute_value=True)),
                  reads=["stg"], writes=["wsc_am"])
            pb.op("v", ("tensor_scalar", dict(out=wsc[:, 1:2], in0=wsc[:, 0:1], scalar1=1.0 / 127.0, scalar2=None,
                                              op0=ALU.mult)),
                  reads=["wsc_am"], writes=["wsc_inv"])
            pb.op("v", ("reciprocal", dict(out=wsc[:, 2:3], in_=wsc[:, 1:2])), reads=["wsc_inv"], writes=["wsc_s"])
            pb.op("v", ("tensor_scalar", dict(out=stg[:, 0:D], in0=stg[:, 0:D], scalar1=wsc[:, 2:3], scalar2=MAGIC,
                                              op0=ALU.mult, op1=ALU.add)),
                  reads=["stg", "wsc_s"], writes=["stg"])
            pb.op("s", ("activation", dict(out=stg[:, 0:D], in_=stg[:, 0:D], func=AF.Identity, bias=cm.const(-MAGIC),
                                           scale=1.0)),
                  reads=["stg", cm.const_region(-MAGIC)], writes=["stg"])
            pb.op("v", ("tensor_scalar", dict(out=stg[:, 0:D], in0=stg[:, 0:D], scalar1=127.0, scalar2=-128.0,
                                              op0=ALU.min, op1=ALU.max)),
                  reads=["stg"], writes=["stg"])
            pb.op("v", ("tensor_scalar", dict(out=stgb[:], in0=stg[:, 0:D], scalar1=wsc[:, 1:2], scalar2=None, op0=ALU.mult)),
                  reads=["stg", "wsc_inv"], writes=["stgb"])
            for c in range(8):
                pb.op("t", ("transpose", dict(out=bank_bf[3][:, c * 128:(c + 1) * 128],
                                              in_=stgb[:, c * 128:(c + 1) * 128], identity=cm.ident_b[:])),
                      reads=["stgb", "cc_ident"], banks=bank_regions(3))
            for c in range(8):
                pb.op("s", ("copy", dict(out=wdst[c][:, k * 128:(k + 1) * 128], in_=bank_bf[3][:, c * 128:(c + 1) * 128])),
                      writes=[f"{wname}T{c}"], banks=bank_regions(3))
    pre.close()
    pb.mem_barrier()

    yt = [sb(f"c_yt{i}", [128, D], F32) for i in range(2)]
    xnt = [sb(f"c_xnt{i}", [128, D], F32) for i in range(2)]
    a1t = [sb(f"c_a1t{i}", [128, D], F32) for i in range(2)]
    ht = [sb(f"c_ht{i}", [128, DFF1], F32) for i in range(2)]
    ynbP = [sb(f"c_ynb{j}", [128, D], BF16) for j in range(2)]
    ynTP = [sb(f"c_ynT{j}", [128, D], BF16) for j in range(2)]
    xqbP = [sb(f"c_xqb{j}", [128, D], BF16) for j in range(2)]
    xqTP = [sb(f"c_xqT{j}", [128, D], BF16) for j in range(2)]
    quantP = [sb(f"c_quant{j}", [128, D], F32) for j in range(2)]
    siluP = [sb(f"c_silu{j}", [128, 512], F32) for j in range(2)]
    statP = [sb(f"c_stat{j}", [128, 10], F32) for j in range(2)]
    sq2P = [sb(f"c_sq2{j}", [128, D], F32) for j in range(2)]
    amax_run = sb("c_amax_run", [128, 1], F32)
    amr = sb("c_amr", [128, 1], F32)
    pb.op("v", ("memset", dict(ap=amax_run[:], constant=0.0)), writes=["am_run"])
    inv_d = 1.0 / D
    NBI = [512, 512, 512, 512, 512, 256]

    for i in range(NT):
        db = i % 2
        y_i, xn_i, a1_i, h_i = yt[db], xnt[db], a1t[db], ht[db]
        ry, rxn, ra1, rh = f"c_yt{db}", f"c_xnt{db}", f"c_a1t{db}", f"c_ht{db}"
        ynb, ynT, xqb, xqT, quant, silu_t, stat, sq2 = (ynbP[db], ynTP[db], xqbP[db], xqTP[db],
                                                        quantP[db], siluP[db], statP[db], sq2P[db])
        sfx = str(db)
        sl = slice(i * 128, (i + 1) * 128)
        pb.dma("y", ("dma_start", dict(out=y_i[:], in_=io["y_h"][sl, :])), reads=["h_y"], writes=[ry])
        pb.dma("p", ("dma_start", dict(out=xn_i[:], in_=io["xn1_h"][sl, :])), reads=["h_xn1"], writes=[rxn])

        pb.op("v", ("reduce_sum", dict(out=stat[:, 0:1], in_=y_i[:], axis=AX.X)), reads=[ry], writes=[f"st_s1{sfx}"])
        pb.op("s", ("activation", dict(out=sq2[:], in_=y_i[:], func=AF.Square, scale=1.0)), reads=[ry], writes=[f"sq2{sfx}"])
        pb.op("v", ("reduce_sum", dict(out=stat[:, 1:2], in_=sq2[:], axis=AX.X)), reads=[f"sq2{sfx}"], writes=[f"st_s2{sfx}"])
        pb.op("v", ("tensor_scalar", dict(out=stat[:, 2:3], in0=stat[:, 0:1], scalar1=inv_d, scalar2=None, op0=ALU.mult)),
              reads=[f"st_s1{sfx}"], writes=[f"st_mu{sfx}"])
        pb.op("v", ("tensor_scalar", dict(out=stat[:, 3:4], in0=stat[:, 2:3], scalar1=stat[:, 2:3], scalar2=-1.0,
                                          op0=ALU.mult, op1=ALU.mult)),
              reads=[f"st_mu{sfx}"], writes=[f"st_nmu2{sfx}"])
        pb.op("v", ("tensor_scalar", dict(out=stat[:, 4:5], in0=stat[:, 1:2], scalar1=inv_d, scalar2=None, op0=ALU.mult)),
              reads=[f"st_s2{sfx}"], writes=[f"st_ex2{sfx}"])
        pb.op("v", ("tensor_tensor", dict(out=stat[:, 4:5], in0=stat[:, 4:5], in1=stat[:, 3:4], op=ALU.add)),
              reads=[f"st_ex2{sfx}", f"st_nmu2{sfx}"], writes=[f"st_var{sfx}"])
        pb.op("s", ("activation", dict(out=stat[:, 5:6], in_=stat[:, 4:5], func=AF.Sqrt, bias=cm.const(LN_EPS0), scale=1.0)),
              reads=[f"st_var{sfx}", cm.const_region(LN_EPS0)], writes=[f"st_sd{sfx}"])
        pb.op("v", ("reciprocal", dict(out=stat[:, 6:7], in_=stat[:, 5:6])), reads=[f"st_sd{sfx}"], writes=[f"st_rstd{sfx}"])
        pb.op("v", ("tensor_scalar", dict(out=stat[:, 7:8], in0=stat[:, 2:3], scalar1=stat[:, 6:7], scalar2=-1.0,
                                          op0=ALU.mult, op1=ALU.mult)),
              reads=[f"st_mu{sfx}", f"st_rstd{sfx}"], writes=[f"st_nmr{sfx}"])
        pb.op("s", ("activation", dict(out=sq2[:], in_=y_i[:], func=AF.Identity, bias=stat[:, 7:8], scale=stat[:, 6:7])),
              reads=[ry, f"st_nmr{sfx}", f"st_rstd{sfx}"], writes=[f"sq2{sfx}"])
        pb.op("v", ("tensor_tensor", dict(out=sq2[:], in0=sq2[:], in1=pnw_b[:], op=ALU.mult)),
              reads=[f"sq2{sfx}", "pnw_b"], writes=[f"sq2{sfx}"])
        pb.op("v", ("tensor_tensor", dict(out=ynb[:], in0=sq2[:], in1=pnb_b[:], op=ALU.add)),
              reads=[f"sq2{sfx}", "pnb_b"], writes=[f"ynb{sfx}"])
        for c in range(8):
            pb.op("t", ("transpose", dict(out=bank_bf[0][:, c * 128:(c + 1) * 128],
                                          in_=ynb[:, c * 128:(c + 1) * 128], identity=cm.ident_b[:])),
                  reads=[f"ynb{sfx}", "cc_ident"], banks=bank_regions(0))
        pb.op("s", ("copy", dict(out=ynT[:], in_=bank_bf[0][:, 0:1024])), writes=[f"ynT{sfx}"], banks=bank_regions(0))
        for nb in range(2):
            for k in range(8):
                pb.op("t", ("matmul", dict(out=bank[1 + nb][:, 0:512], lhsT=ynT[:, k * 128:(k + 1) * 128],
                                           rhs=woT[k][:, nb * 512:(nb + 1) * 512], start=(k == 0), stop=(k == 7))),
                      reads=[f"ynT{sfx}", f"woT{k}"], banks=bank_regions(1 + nb))
            pb.op("v", ("tensor_copy", dict(out=a1_i[:, nb * 512:(nb + 1) * 512], in_=bank[1 + nb][:, 0:512])),
                  writes=[ra1], banks=bank_regions(1 + nb))
        pb.dma("y", ("dma_start", dict(out=io["attn1_h"][sl, :], in_=a1_i[:])), reads=[ra1], writes=["h_attn1"])

        pb.op("v", ("tensor_scalar", dict(out=quant[:], in0=xn_i[:], scalar1=sq_b[:, 0:1], scalar2=MAGIC,
                                          op0=ALU.mult, op1=ALU.add)),
              reads=[rxn, "sq_b"], writes=[f"quant{sfx}"])
        pb.op("s", ("activation", dict(out=quant[:], in_=quant[:], func=AF.Identity, bias=cm.const(-MAGIC), scale=1.0)),
              reads=[f"quant{sfx}", cm.const_region(-MAGIC)], writes=[f"quant{sfx}"])
        pb.op("v", ("tensor_scalar", dict(out=quant[:], in0=quant[:], scalar1=127.0, scalar2=-128.0,
                                          op0=ALU.min, op1=ALU.max)),
              reads=[f"quant{sfx}"], writes=[f"quant{sfx}"])
        pb.op("v", ("tensor_scalar", dict(out=xqb[:], in0=quant[:], scalar1=sq_b[:, 1:2], scalar2=None, op0=ALU.mult)),
              reads=[f"quant{sfx}", "sq_b"], writes=[f"xqb{sfx}"])
        for c in range(8):
            pb.op("t", ("transpose", dict(out=bank_bf[0][:, c * 128:(c + 1) * 128],
                                          in_=xqb[:, c * 128:(c + 1) * 128], identity=cm.ident_b[:])),
                  reads=[f"xqb{sfx}", "cc_ident"], banks=bank_regions(0))
        pb.op("s", ("copy", dict(out=xqT[:], in_=bank_bf[0][:, 0:1024])), writes=[f"xqT{sfx}"], banks=bank_regions(0))
        off = 0
        for j, nsz in enumerate(NBI):
            gb = 3 + (j % 2)
            ub = 5 + (j % 2)
            for k in range(8):
                pb.op("t", ("matmul", dict(out=bank[gb][:, 0:nsz], lhsT=xqT[:, k * 128:(k + 1) * 128],
                                           rhs=wgT[k][:, off:off + nsz], start=(k == 0), stop=(k == 7))),
                      reads=[f"xqT{sfx}", f"w_gateT{k}"], banks=bank_regions(gb))
            for k in range(8):
                pb.op("t", ("matmul", dict(out=bank[ub][:, 0:nsz], lhsT=xqT[:, k * 128:(k + 1) * 128],
                                           rhs=wuT[k][:, off:off + nsz], start=(k == 0), stop=(k == 7))),
                      reads=[f"xqT{sfx}", f"w_upT{k}"], banks=bank_regions(ub))
            pb.op("s", ("activation", dict(out=silu_t[:, 0:nsz], in_=bank[gb][:, 0:nsz], func=AF.Silu, scale=1.0)),
                  writes=[f"silu_t{sfx}"], banks=bank_regions(gb))
            pb.op("v", ("tensor_tensor", dict(out=h_i[:, off:off + nsz], in0=silu_t[:, 0:nsz], in1=bank[ub][:, 0:nsz],
                                              op=ALU.mult)),
                  reads=[f"silu_t{sfx}"], writes=[rh], banks=bank_regions(ub))
            pb.op("v", ("tensor_reduce", dict(out=amr[:], in_=h_i[:, off:off + nsz], axis=AX.X, op=ALU.max,
                                              apply_absolute_value=True)),
                  reads=[rh], writes=[f"amr{sfx}"])
            pb.op("v", ("tensor_tensor", dict(out=amax_run[:], in0=amax_run[:], in1=amr[:], op=ALU.max)),
                  reads=[f"amr{sfx}", "am_run"], writes=["am_run"])
            off += nsz
        pb.dma("p", ("dma_start", dict(out=io["hgl_h"][sl, :], in_=h_i[:])), reads=[rh], writes=["h_hgl"])

    am_scr = nc.dram_tensor("am_scr4", [128], F32, kind="Internal").ap()
    am_row = sb("c_am_row", [1, 128], F32)
    am_out = sb("c_am_out", [1, 1], F32)
    pb.dma("y", ("dma_start", dict(out=am_scr.rearrange("(a b) -> a b", a=128), in_=amax_run[:])),
           reads=["am_run"], writes=["h_amscr"])
    pb.dma("y", ("dma_start", dict(out=am_row[:], in_=am_scr.rearrange("(a b) -> a b", a=1))),
           reads=["h_amscr"], writes=["am_rrow"])
    pb.op("v", ("reduce_max", dict(out=am_out[:], in_=am_row[:], axis=AX.X)), reads=["am_rrow"], writes=["am_o"])
    pb.dma("y", ("dma_start", dict(out=io["amax_h1"], in_=am_out[:])), reads=["am_o"], writes=["h_amaxh1"])


def build_l1d():
    nc = bass.Bass("TRN2", target_bir_lowering=False)
    io = dict(
        hgl_h=nc.dram_tensor("hgl_h", [L, DFF1], F32, kind="ExternalInput").ap(),
        amax8=nc.dram_tensor("amax8", [1, 8], F32, kind="ExternalInput").ap(),
        clip_down=nc.dram_tensor("clip_down", [1, 1], F32, kind="ExternalInput").ap(),
        w_down=nc.dram_tensor("w_down", [D, DFF1], F32, kind="ExternalInput").ap(),
        x1_h=nc.dram_tensor("x1_h", [L, D], F32, kind="ExternalInput").ap(),
        attn1_h=nc.dram_tensor("attn1_h", [L, D], F32, kind="ExternalInput").ap(),
        out_h=nc.dram_tensor("out_h", [L, D], F32, kind="ExternalOutput").ap(),
    )
    ctx = ExitStack()
    with ctx:
        pb = PB(nc, ctx)
        cm = Common(pb, ctx)
        emit_l1d(pb, cm, ctx, io)
        pb.emit()
    return nc


def emit_l1d(pb, cm, ctx, io):
    nc = pb.nc
    sb = lambda name, shape, dt: ctx.enter_context(nc.sbuf_tensor(name, shape, dt))
    ps = lambda name, shape, dt: ctx.enter_context(nc.psum_tensor("pf" + name, shape, dt))
    cm.const(MAGIC); cm.const(-MAGIC)
    NCI = 22

    bank = [ps(f"bk{i}", [128, 512], F32) for i in range(8)]
    bank_bf = [b[:].bitcast(BF16) for b in bank]
    ones_row_f = sb("d_ones_row_f", [1, 128], F32)
    pb.op("v", ("memset", dict(ap=ones_row_f[:], constant=1.0)), writes=["ones_row_f"])

    wdT = [sb(f"d_wdT{k}", [128, D], BF16) for k in range(NCI)]
    sq_b = sb("d_sq_b", [128, 2], F32)

    pre = ExitStack()
    psb = lambda name, shape, dt: pre.enter_context(nc.sbuf_tensor(name, shape, dt))
    stg = psb("d_stg", [128, DFF1 + 16], F32)
    stgb = psb("d_stgb", [128, DFF1], BF16)
    wsc = psb("d_wsc", [128, 4], F32)

    pb.dma("y", ("dma_start", dict(out=stg[0:1, DFF1:DFF1 + 8], in_=io["amax8"])), reads=["h_amax8"], writes=["stgr"])
    pb.dma("y", ("dma_start", dict(out=stg[0:1, DFF1 + 8:DFF1 + 9], in_=io["clip_down"])), reads=["h_clipd"], writes=["stgr"])
    pb.op("v", ("reduce_max", dict(out=wsc[0:1, 0:1], in_=stg[0:1, DFF1:DFF1 + 8], axis=AX.X)), reads=["stgr"], writes=["wsc0"])
    pb.op("s", ("activation", dict(out=wsc[0:1, 1:2], in_=stg[0:1, DFF1 + 8:DFF1 + 9], func=AF.Sigmoid, scale=1.0)),
          reads=["stgr"], writes=["wsc1"])
    pb.op("v", ("tensor_tensor", dict(out=wsc[0:1, 2:3], in0=wsc[0:1, 0:1], in1=wsc[0:1, 1:2], op=ALU.mult)),
          reads=["wsc0", "wsc1"], writes=["wsc2"])
    pb.op("v", ("tensor_scalar", dict(out=wsc[0:1, 3:4], in0=wsc[0:1, 2:3], scalar1=1.0 / 127.0, scalar2=None, op0=ALU.mult)),
          reads=["wsc2"], writes=["wsc3"])
    pb.op("v", ("reciprocal", dict(out=wsc[0:1, 2:3], in_=wsc[0:1, 3:4])), reads=["wsc3"], writes=["wsc2"])
    pb.op("t", ("matmul", dict(out=bank[0][:, 0:2], lhsT=ones_row_f[:], rhs=wsc[0:1, 2:4], start=True, stop=True)),
          reads=["wsc2", "wsc3", "ones_row_f"], banks=bank_regions(0))
    pb.op("v", ("tensor_copy", dict(out=sq_b[:], in_=bank[0][:, 0:2])), writes=["sq_b"], banks=bank_regions(0))

    for k in range(8):
        pb.dma("y", ("dma_start", dict(out=stg[:, 0:DFF1], in_=io["w_down"][k * 128:(k + 1) * 128, :])),
               reads=["h_wd"], writes=["stg"])
        pb.op("v", ("tensor_reduce", dict(out=wsc[:, 0:1], in_=stg[:, 0:DFF1], axis=AX.X, op=ALU.max,
                                          apply_absolute_value=True)),
              reads=["stg"], writes=["wsc_am"])
        pb.op("v", ("tensor_scalar", dict(out=wsc[:, 1:2], in0=wsc[:, 0:1], scalar1=1.0 / 127.0, scalar2=None, op0=ALU.mult)),
              reads=["wsc_am"], writes=["wsc_inv"])
        pb.op("v", ("reciprocal", dict(out=wsc[:, 2:3], in_=wsc[:, 1:2])), reads=["wsc_inv"], writes=["wsc_s"])
        pb.op("v", ("tensor_scalar", dict(out=stg[:, 0:DFF1], in0=stg[:, 0:DFF1], scalar1=wsc[:, 2:3], scalar2=MAGIC,
                                          op0=ALU.mult, op1=ALU.add)),
              reads=["stg", "wsc_s"], writes=["stg"])
        pb.op("s", ("activation", dict(out=stg[:, 0:DFF1], in_=stg[:, 0:DFF1], func=AF.Identity, bias=cm.const(-MAGIC),
                                       scale=1.0)),
              reads=["stg", cm.const_region(-MAGIC)], writes=["stg"])
        pb.op("v", ("tensor_scalar", dict(out=stg[:, 0:DFF1], in0=stg[:, 0:DFF1], scalar1=127.0, scalar2=-128.0,
                                          op0=ALU.min, op1=ALU.max)),
              reads=["stg"], writes=["stg"])
        pb.op("v", ("tensor_scalar", dict(out=stgb[:], in0=stg[:, 0:DFF1], scalar1=wsc[:, 1:2], scalar2=None, op0=ALU.mult)),
              reads=["stg", "wsc_inv"], writes=["stgb"])
        for c in range(NCI):
            pb.op("t", ("transpose", dict(out=bank_bf[1 + c % 2][:, (c // 2 % 4) * 128:(c // 2 % 4 + 1) * 128],
                                          in_=stgb[:, c * 128:(c + 1) * 128], identity=cm.ident_b[:])),
                  reads=["stgb", "cc_ident"], banks=bank_regions(1 + c % 2))
            pb.op("s", ("copy", dict(out=wdT[c][:, k * 128:(k + 1) * 128],
                                     in_=bank_bf[1 + c % 2][:, (c // 2 % 4) * 128:(c // 2 % 4 + 1) * 128])),
                  writes=[f"wdT{c}"], banks=bank_regions(1 + c % 2))
    pre.close()
    pb.mem_barrier()

    ht = [sb(f"d_ht{i}", [128, DFF1], F32) for i in range(2)]
    x1t = [sb(f"d_x1t{i}", [128, D], F32) for i in range(2)]
    a1t = [sb(f"d_a1t{i}", [128, D], F32) for i in range(2)]
    ot = [sb(f"d_ot{i}", [128, D], F32) for i in range(2)]
    hqbP = [sb(f"d_hqb{j}", [128, DFF1], BF16) for j in range(2)]
    hqTP = [sb(f"d_hqT{j}", [128, DFF1], BF16) for j in range(2)]

    for i in range(NT):
        db = i % 2
        h_i, x1_i, a1_i, o_i = ht[db], x1t[db], a1t[db], ot[db]
        rh, rx1, ra1, ro = f"d_ht{db}", f"d_x1t{db}", f"d_a1t{db}", f"d_ot{db}"
        hqb, hqT = hqbP[db], hqTP[db]
        sfx = str(db)
        sl = slice(i * 128, (i + 1) * 128)
        pb.dma("y", ("dma_start", dict(out=h_i[:], in_=io["hgl_h"][sl, :])), reads=["h_hgl"], writes=[rh])
        pb.dma("p", ("dma_start", dict(out=x1_i[:], in_=io["x1_h"][sl, :])), reads=["h_x1"], writes=[rx1])
        pb.dma("p", ("dma_start", dict(out=a1_i[:], in_=io["attn1_h"][sl, :])), reads=["h_attn1"], writes=[ra1])
        pb.op("v", ("tensor_scalar", dict(out=h_i[:], in0=h_i[:], scalar1=sq_b[:, 0:1], scalar2=MAGIC,
                                          op0=ALU.mult, op1=ALU.add)),
              reads=[rh, "sq_b"], writes=[rh])
        pb.op("s", ("activation", dict(out=h_i[:], in_=h_i[:], func=AF.Identity, bias=cm.const(-MAGIC), scale=1.0)),
              reads=[rh, cm.const_region(-MAGIC)], writes=[rh])
        pb.op("v", ("tensor_scalar", dict(out=h_i[:], in0=h_i[:], scalar1=127.0, scalar2=-128.0,
                                          op0=ALU.min, op1=ALU.max)),
              reads=[rh], writes=[rh])
        pb.op("v", ("tensor_scalar", dict(out=hqb[:], in0=h_i[:], scalar1=sq_b[:, 1:2], scalar2=None, op0=ALU.mult)),
              reads=[rh, "sq_b"], writes=[f"hqb{sfx}"])
        for c in range(NCI):
            tb = c % 2
            cslot = (c // 2) % 4
            pb.op("t", ("transpose", dict(out=bank_bf[tb][:, cslot * 128 + 512 * 0:cslot * 128 + 128 + 512 * 0],
                                          in_=hqb[:, c * 128:(c + 1) * 128], identity=cm.ident_b[:])),
                  reads=[f"hqb{sfx}", "cc_ident"], banks=bank_regions(tb))
            pb.op("s", ("copy", dict(out=hqT[:, c * 128:(c + 1) * 128], in_=bank_bf[tb][:, cslot * 128:cslot * 128 + 128])),
                  writes=[f"hqT{sfx}"], banks=bank_regions(tb))
        for nb in range(2):
            for k in range(NCI):
                pb.op("t", ("matmul", dict(out=bank[2 + nb][:, 0:512], lhsT=hqT[:, k * 128:(k + 1) * 128],
                                           rhs=wdT[k][:, nb * 512:(nb + 1) * 512], start=(k == 0), stop=(k == NCI - 1))),
                      reads=[f"hqT{sfx}", f"wdT{k}"], banks=bank_regions(2 + nb))
            pb.op("v", ("tensor_tensor", dict(out=o_i[:, nb * 512:(nb + 1) * 512], in0=x1_i[:, nb * 512:(nb + 1) * 512],
                                              in1=bank[2 + nb][:, 0:512], op=ALU.add)),
                  reads=[rx1], writes=[ro], banks=bank_regions(2 + nb))
        pb.op("v", ("tensor_tensor", dict(out=o_i[:], in0=o_i[:], in1=a1_i[:], op=ALU.add)),
              reads=[ro, ra1], writes=[ro])
        pb.dma("y", ("dma_start", dict(out=io["out_h"][sl, :], in_=o_i[:])), reads=[ro], writes=["h_out"])


def build_full():
    nc = bass.Bass("TRN2", target_bir_lowering=False)
    mkI = lambda name, shape, dt: nc.dram_tensor(name, shape, dt, kind="Internal").ap()
    ein = lambda name, shape: nc.dram_tensor(name, shape, F32, kind="ExternalInput").ap()
    x = ein("x", [L, D])
    wqkv = ein("wqkv", [D, 3 * D]); gamma = ein("gamma", [1, D]); beta = ein("beta", [1, D])
    wout = ein("wout", [D, D]); w_in = ein("w_in", [D, DFF0]); b_in = ein("b_in", [1, DFF0])
    w_out = ein("w_out", [DFF0, D]); b_out = ein("b_out", [1, D])
    wq = ein("wq", [D, D]); wk = ein("wk", [D, D]); wv = ein("wv", [D, D])
    lr_w = ein("lr_w", [H, D]); lr_b = ein("lr_b", [1, H])
    tnw = ein("tnw", [1, D]); tnb = ein("tnb", [1, D])
    w1 = ein("w1", [H, 64, 256]); b1 = ein("b1", [H, 1, 256])
    w2 = ein("w2", [H, 256, 64]); b2 = ein("b2", [H, 1, 64])
    wo = ein("wo", [D, D]); pnw = ein("pnw", [1, D]); pnb = ein("pnb", [1, D])
    clip_x = ein("clip_x", [1, 1]); clip_down = ein("clip_down", [1, 1])
    w_gate = ein("w_gate", [DFF1, D]); w_up = ein("w_up", [DFF1, D]); w_down = ein("w_down", [D, DFF1])
    out_h = nc.dram_tensor("out_h", [L, D], F32, kind="ExternalOutput").ap()
    xn_h = mkI("i_xn", [L, D], F32)
    q_h = mkI("i_q", [L, D], BF16)
    kv_h = mkI("i_kv", [64, 1024], F32)
    ksum_h = mkI("i_ksum", [1, 1024], F32)
    stats_h = mkI("i_stats", [NT, 128, 2], F32)
    part1_h = mkI("i_part1", [L, D], F32)
    h2_h = mkI("i_h2", [L, DFF0], F32)
    x1_h = mkI("i_x1", [L, D], F32)
    xn1_h = mkI("i_xn1", [L, D], F32)
    qtok_h = mkI("i_qtok", [L, D], F32)
    ktok_h = mkI("i_ktok", [L, D], BF16)
    tgt_h = mkI("i_tgt", [L, D], F32)
    eta_h = mkI("i_eta", [L, H], F32)
    qT_h = mkI("i_qT", [NT, 64, 16 * 128], BF16)
    kT_h = mkI("i_kT", [NT, 64, 16 * 128], BF16)
    y_h = mkI("i_y", [L, D], F32)
    attn1_h = mkI("i_attn1", [L, D], F32)
    hgl_h = mkI("i_hgl", [L, DFF1], F32)
    amp = [mkI(f"i_amp{j}", [1, 1], F32) for j in range(4)]
    ama = [mkI(f"i_ama{j}", [8, 1], F32) for j in range(4)]
    groups = [list(range(8))]

    ctx = ExitStack()
    with ctx:
        pb = PB(nc, ctx)
        cm = Common(pb, ctx)

        def gather(j, rpart, rall):
            pb.dma("p", lambda e, j=j: e.collective_compute(
                "AllGather", ALU.bypass, replica_groups=groups,
                ins=[amp[j]], outs=[ama[j]]),
                reads=[rpart], writes=[rall])

        def phase(emitfn, io):
            pctx = ExitStack()
            emitfn(pb, cm, pctx, io)
            pctx.close()
            pb.mem_barrier()

        phase(emit_l0a, dict(x=x, wqkv=wqkv, gamma=gamma, beta=beta, xn_h=xn_h, q_h=q_h,
                             kv_h=kv_h, ksum_h=ksum_h, stats_h=stats_h, amax_h=amp[0]))
        gather(0, "h_amax", "h_amax8")
        phase(emit_l0b, dict(x=x, xn_h=xn_h, q_h=q_h, kv_h=kv_h, ksum_h=ksum_h, stats_h=stats_h,
                             amax8=ama[0].rearrange("a b -> b a"), wout=wout, w_in=w_in, b_in=b_in,
                             part1_h=part1_h, h2_h=h2_h, amaxh2=amp[1]))
        gather(1, "h_amaxh2", "h_am8")
        phase(emit_l0c, dict(part1_h=part1_h, h2_h=h2_h, amaxh2_8=ama[1].rearrange("a b -> b a"),
                             w_out=w_out, b_out=b_out, x1_h=x1_h, xn1_h=xn1_h, amax_xn1=amp[2]))
        gather(2, "h_amax1", "h_amax8")
        phase(emit_l1a, dict(xn1_h=xn1_h, wq=wq, wk=wk, wv=wv, lr_w=lr_w, lr_b=lr_b, tnw=tnw, tnb=tnb,
                             qtok_h=qtok_h, ktok_h=ktok_h, tgt_h=tgt_h, eta_h=eta_h, qT_h=qT_h, kT_h=kT_h))
        phase(emit_l1b, dict(qT_h=qT_h, kT_h=kT_h, ktok_h=ktok_h, qtok_h=qtok_h, tgt_h=tgt_h,
                             eta_h=eta_h, w1=w1, b1=b1, w2=w2, b2=b2, tnw=tnw, tnb=tnb, y_h=y_h))
        phase(emit_l1c, dict(y_h=y_h, xn1_h=xn1_h, amax8=ama[2].rearrange("a b -> b a"), wo=wo,
                             pnw=pnw, pnb=pnb, clip_x=clip_x, w_gate=w_gate, w_up=w_up,
                             attn1_h=attn1_h, hgl_h=hgl_h, amax_h1=amp[3]))
        gather(3, "h_amaxh1", "h_amax8")
        phase(emit_l1d, dict(hgl_h=hgl_h, amax8=ama[3].rearrange("a b -> b a"), clip_down=clip_down,
                             w_down=w_down, x1_h=x1_h, attn1_h=attn1_h, out_h=out_h))
        pb.emit()
    return nc


def build_g3():
    nc = bass.Bass("TRN2", target_bir_lowering=False)
    U16 = mybir.dt.uint16
    ein = lambda name, shape, dt=F32: nc.dram_tensor(name, shape, dt, kind="ExternalInput").ap()
    eout = lambda name, shape: nc.dram_tensor(name, shape, F32, kind="ExternalOutput").ap()
    eoutb = lambda name, shape: nc.dram_tensor(name, shape, U16, kind="ExternalOutput").ap().bitcast(BF16)
    io_c = dict(
        part1_h=ein("part1_h", [L, D]), h2_h=ein("h2_h", [L, DFF0]), amaxh2_8=ein("amaxh2_8", [1, 8]),
        w_out=ein("w_out", [DFF0, D]), b_out=ein("b_out", [1, D]),
        x1_h=eout("x1_h", [L, D]), xn1_h=eout("xn1_h", [L, D]), amax_xn1=eout("amax_xn1", [1, 1]),
    )
    io_a = dict(
        xn1_h=io_c["xn1_h"],
        wq=ein("wq", [D, D]), wk=ein("wk", [D, D]), wv=ein("wv", [D, D]),
        lr_w=ein("lr_w", [H, D]), lr_b=ein("lr_b", [1, H]), tnw=ein("tnw", [1, D]), tnb=ein("tnb", [1, D]),
        qtok_h=eout("qtok_h", [L, D]), ktok_h=eoutb("ktok_h", [L, D]), tgt_h=eout("tgt_h", [L, D]),
        eta_h=eout("eta_h", [L, H]), qT_h=eoutb("qT_h", [NT, 64, 16 * 128]), kT_h=eoutb("kT_h", [NT, 64, 16 * 128]),
    )
    ctx = ExitStack()
    with ctx:
        pb = PB(nc, ctx)
        cm = Common(pb, ctx)
        pctx = ExitStack()
        emit_l0c(pb, cm, pctx, io_c)
        pctx.close()
        pb.mem_barrier()
        pctx = ExitStack()
        emit_l1a(pb, cm, pctx, io_a)
        pctx.close()
        pb.emit()
    return nc


def build_g4():
    nc = bass.Bass("TRN2", target_bir_lowering=False)
    ein = lambda name, shape, dt=F32: nc.dram_tensor(name, shape, dt, kind="ExternalInput").ap()
    eout = lambda name, shape: nc.dram_tensor(name, shape, F32, kind="ExternalOutput").ap()
    y_h = nc.dram_tensor("y_h", [L, D], F32, kind="Internal").ap()
    io_b = dict(
        qT_h=ein("qT_h", [NT, 64, 16 * 128], BF16), kT_h=ein("kT_h", [NT, 64, 16 * 128], BF16),
        ktok_h=ein("ktok_h", [L, D], BF16), qtok_h=ein("qtok_h", [L, D]), tgt_h=ein("tgt_h", [L, D]),
        eta_h=ein("eta_h", [L, H]), w1=ein("w1", [H, 64, 256]), b1=ein("b1", [H, 1, 256]),
        w2=ein("w2", [H, 256, 64]), b2=ein("b2", [H, 1, 64]), tnw=ein("tnw", [1, D]), tnb=ein("tnb", [1, D]),
        y_h=y_h,
    )
    io_c = dict(
        y_h=y_h, xn1_h=ein("xn1_h", [L, D]), amax8=ein("amax8", [1, 8]), wo=ein("wo", [D, D]),
        pnw=ein("pnw", [1, D]), pnb=ein("pnb", [1, D]), clip_x=ein("clip_x", [1, 1]),
        w_gate=ein("w_gate", [DFF1, D]), w_up=ein("w_up", [DFF1, D]),
        attn1_h=eout("attn1_h", [L, D]), hgl_h=eout("hgl_h", [L, DFF1]), amax_h1=eout("amax_h1", [1, 1]),
    )
    ctx = ExitStack()
    with ctx:
        pb = PB(nc, ctx)
        cm = Common(pb, ctx)
        pctx = ExitStack()
        emit_l1b(pb, cm, pctx, io_b)
        pctx.close()
        pb.mem_barrier()
        pctx = ExitStack()
        emit_l1c(pb, cm, pctx, io_c)
        pctx.close()
        pb.emit()
    return nc


_NC_CACHE = {}


def _get_nc(name):
    if name not in _NC_CACHE:
        _NC_CACHE[name] = globals()[f"build_{name}"]()
    return _NC_CACHE[name]


def _amax8(results, key):
    return np.concatenate([r[key].reshape(1, 1) for r in results], axis=1).astype(np.float32)


def kernel_unfused(**inputs):
    import ml_dtypes
    f = lambda k: np.ascontiguousarray(np.asarray(inputs[k], dtype=np.float32))
    bfv = lambda a: a.view(ml_dtypes.bfloat16) if a.dtype == np.uint16 else a
    x = f("L0_attn_x")
    gamma = f("L0_attn_gamma").reshape(1, D)
    beta = f("L0_attn_beta").reshape(1, D)
    cores = list(range(8))

    r1 = run_bass_kernel_spmd(_get_nc("l0a"), [dict(x=x[c], wqkv=f("L0_attn_Wqkv"), gamma=gamma, beta=beta)
                                               for c in cores], core_ids=cores).results
    am1 = _amax8(r1, "amax_h")
    r2 = run_bass_kernel_spmd(_get_nc("l0b"), [dict(x=x[c], xn_h=r1[c]["xn_h"], q_h=bfv(r1[c]["q_h"]),
                                                    kv_h=r1[c]["kv_h"], ksum_h=r1[c]["ksum_h"],
                                                    stats_h=r1[c]["stats_h"], amax8=am1,
                                                    wout=f("L0_attn_Wout"), w_in=f("L0_ffn_W_in"),
                                                    b_in=f("L0_ffn_b_in").reshape(1, DFF0))
                                               for c in cores], core_ids=cores).results
    am2 = _amax8(r2, "amaxh2")
    r3 = run_bass_kernel_spmd(_get_nc("l0c"), [dict(part1_h=r2[c]["part1_h"], h2_h=r2[c]["h2_h"], amaxh2_8=am2,
                                                    w_out=f("L0_ffn_W_out"), b_out=f("L0_ffn_b_out").reshape(1, D))
                                               for c in cores], core_ids=cores).results
    r4 = run_bass_kernel_spmd(_get_nc("l1a"), [dict(xn1_h=r3[c]["xn1_h"], wq=f("L1_attn_Wq"), wk=f("L1_attn_Wk"),
                                                    wv=f("L1_attn_Wv"), lr_w=f("L1_attn_lr_weight"),
                                                    lr_b=f("L1_attn_lr_bias").reshape(1, H),
                                                    tnw=f("L1_attn_ttt_norm_weight").reshape(1, D),
                                                    tnb=f("L1_attn_ttt_norm_bias").reshape(1, D))
                                               for c in cores], core_ids=cores).results
    r5 = run_bass_kernel_spmd(_get_nc("l1b"), [dict(qT_h=bfv(r4[c]["qT_h"]), kT_h=bfv(r4[c]["kT_h"]),
                                                    ktok_h=bfv(r4[c]["ktok_h"]), qtok_h=r4[c]["qtok_h"],
                                                    tgt_h=r4[c]["tgt_h"], eta_h=r4[c]["eta_h"],
                                                    w1=f("L1_attn_W1"), b1=f("L1_attn_b1"),
                                                    w2=f("L1_attn_W2"), b2=f("L1_attn_b2"),
                                                    tnw=f("L1_attn_ttt_norm_weight").reshape(1, D),
                                                    tnb=f("L1_attn_ttt_norm_bias").reshape(1, D))
                                               for c in cores], core_ids=cores).results
    r6 = run_bass_kernel_spmd(_get_nc("l1c"), [dict(y_h=r5[c]["y_h"], xn1_h=r3[c]["xn1_h"],
                                                    amax8=_amax8(r3, "amax_xn1"),
                                                    wo=f("L1_attn_Wo"), pnw=f("L1_attn_post_norm_weight").reshape(1, D),
                                                    pnb=f("L1_attn_post_norm_bias").reshape(1, D),
                                                    clip_x=f("L1_ffn_clip_x").reshape(1, 1),
                                                    w_gate=f("L1_ffn_w_gate"), w_up=f("L1_ffn_w_up"))
                                               for c in cores], core_ids=cores).results
    am4 = _amax8(r6, "amax_h1")
    r7 = run_bass_kernel_spmd(_get_nc("l1d"), [dict(hgl_h=r6[c]["hgl_h"], amax8=am4,
                                                    clip_down=f("L1_ffn_clip_down").reshape(1, 1),
                                                    w_down=f("L1_ffn_w_down"), x1_h=r3[c]["x1_h"],
                                                    attn1_h=r6[c]["attn1_h"])
                                               for c in cores], core_ids=cores).results
    return np.stack([r7[c]["out_h"] for c in cores], axis=0)


def kernel_fused(**inputs):
    f = lambda k: np.ascontiguousarray(np.asarray(inputs[k], dtype=np.float32))
    x = f("L0_attn_x")
    nc = _get_nc("full")
    common = dict(
        wqkv=f("L0_attn_Wqkv"), gamma=f("L0_attn_gamma").reshape(1, D), beta=f("L0_attn_beta").reshape(1, D),
        wout=f("L0_attn_Wout"), w_in=f("L0_ffn_W_in"), b_in=f("L0_ffn_b_in").reshape(1, DFF0),
        w_out=f("L0_ffn_W_out"), b_out=f("L0_ffn_b_out").reshape(1, D),
        wq=f("L1_attn_Wq"), wk=f("L1_attn_Wk"), wv=f("L1_attn_Wv"),
        lr_w=f("L1_attn_lr_weight"), lr_b=f("L1_attn_lr_bias").reshape(1, H),
        tnw=f("L1_attn_ttt_norm_weight").reshape(1, D), tnb=f("L1_attn_ttt_norm_bias").reshape(1, D),
        w1=f("L1_attn_W1"), b1=f("L1_attn_b1"), w2=f("L1_attn_W2"), b2=f("L1_attn_b2"),
        wo=f("L1_attn_Wo"), pnw=f("L1_attn_post_norm_weight").reshape(1, D),
        pnb=f("L1_attn_post_norm_bias").reshape(1, D),
        clip_x=f("L1_ffn_clip_x").reshape(1, 1), clip_down=f("L1_ffn_clip_down").reshape(1, 1),
        w_gate=f("L1_ffn_w_gate"), w_up=f("L1_ffn_w_up"), w_down=f("L1_ffn_w_down"),
    )
    res = run_bass_kernel_spmd(nc, [dict(common, x=x[c]) for c in range(8)], core_ids=list(range(8)))
    return np.stack([res.results[c]["out_h"] for c in range(8)], axis=0)


def kernel5(**inputs):
    import ml_dtypes
    f = lambda k: np.ascontiguousarray(np.asarray(inputs[k], dtype=np.float32))
    bfv = lambda a: a.view(ml_dtypes.bfloat16) if a.dtype == np.uint16 else a
    x = f("L0_attn_x")
    gamma = f("L0_attn_gamma").reshape(1, D)
    beta = f("L0_attn_beta").reshape(1, D)
    cores = list(range(8))
    r1 = run_bass_kernel_spmd(_get_nc("l0a"), [dict(x=x[c], wqkv=f("L0_attn_Wqkv"), gamma=gamma, beta=beta)
                                               for c in cores], core_ids=cores).results
    am1 = _amax8(r1, "amax_h")
    r2 = run_bass_kernel_spmd(_get_nc("l0b"), [dict(x=x[c], xn_h=r1[c]["xn_h"], q_h=bfv(r1[c]["q_h"]),
                                                    kv_h=r1[c]["kv_h"], ksum_h=r1[c]["ksum_h"],
                                                    stats_h=r1[c]["stats_h"], amax8=am1,
                                                    wout=f("L0_attn_Wout"), w_in=f("L0_ffn_W_in"),
                                                    b_in=f("L0_ffn_b_in").reshape(1, DFF0))
                                               for c in cores], core_ids=cores).results
    am2 = _amax8(r2, "amaxh2")
    r3 = run_bass_kernel_spmd(_get_nc("g3"), [dict(part1_h=r2[c]["part1_h"], h2_h=r2[c]["h2_h"], amaxh2_8=am2,
                                                   w_out=f("L0_ffn_W_out"), b_out=f("L0_ffn_b_out").reshape(1, D),
                                                   wq=f("L1_attn_Wq"), wk=f("L1_attn_Wk"), wv=f("L1_attn_Wv"),
                                                   lr_w=f("L1_attn_lr_weight"),
                                                   lr_b=f("L1_attn_lr_bias").reshape(1, H),
                                                   tnw=f("L1_attn_ttt_norm_weight").reshape(1, D),
                                                   tnb=f("L1_attn_ttt_norm_bias").reshape(1, D))
                                              for c in cores], core_ids=cores).results
    am3 = _amax8(r3, "amax_xn1")
    r4 = run_bass_kernel_spmd(_get_nc("g4"), [dict(qT_h=bfv(r3[c]["qT_h"]), kT_h=bfv(r3[c]["kT_h"]),
                                                   ktok_h=bfv(r3[c]["ktok_h"]), qtok_h=r3[c]["qtok_h"],
                                                   tgt_h=r3[c]["tgt_h"], eta_h=r3[c]["eta_h"],
                                                   w1=f("L1_attn_W1"), b1=f("L1_attn_b1"),
                                                   w2=f("L1_attn_W2"), b2=f("L1_attn_b2"),
                                                   tnw=f("L1_attn_ttt_norm_weight").reshape(1, D),
                                                   tnb=f("L1_attn_ttt_norm_bias").reshape(1, D),
                                                   xn1_h=r3[c]["xn1_h"], amax8=am3, wo=f("L1_attn_Wo"),
                                                   pnw=f("L1_attn_post_norm_weight").reshape(1, D),
                                                   pnb=f("L1_attn_post_norm_bias").reshape(1, D),
                                                   clip_x=f("L1_ffn_clip_x").reshape(1, 1),
                                                   w_gate=f("L1_ffn_w_gate"), w_up=f("L1_ffn_w_up"))
                                              for c in cores], core_ids=cores).results
    am4 = _amax8(r4, "amax_h1")
    r5 = run_bass_kernel_spmd(_get_nc("l1d"), [dict(hgl_h=r4[c]["hgl_h"], amax8=am4,
                                                    clip_down=f("L1_ffn_clip_down").reshape(1, 1),
                                                    w_down=f("L1_ffn_w_down"), x1_h=r3[c]["x1_h"],
                                                    attn1_h=r4[c]["attn1_h"])
                                               for c in cores], core_ids=cores).results
    return np.stack([r5[c]["out_h"] for c in cores], axis=0)


def kernel(**inputs):
    return kernel5(**inputs)
